# version 27
# baseline (speedup 1.0000x reference)
import contextlib
import numpy as np

import concourse.bass as bass
import concourse.mybir as mybir
from concourse.bass_utils import run_bass_kernel_spmd

F32 = mybir.dt.float32
BF16 = mybir.dt.bfloat16
AF = mybir.ActivationFunctionType
ALU = mybir.AluOpType
AX = mybir.AxisListType

D = 1024
H = 512
NH = 16
HD = 64
S = 2048
NT = 16
NSEQ = 2
NC = 8
E_R = 8
EPS_LN = 1e-5
EPS_QK = 1e-6
NEG_BIG = -3.0e38

PAIRS = [(("r", 0), ("r", 1)), (("r", 2), ("r", 3)), (("r", 4), ("r", 5)),
         (("r", 6), ("r", 7)), (("s", 0), ("s", 1))]

SLOT = 12288
KNT_O = 0
V_O = 16384
SUB_O = 32768
ASB_O = SUB_O
ATT_O = SUB_O + 8192
CTXT_O = SUB_O + 9216
WWA_O = SUB_O
WWB_O = SUB_O + 8192

XN2_O = 0
H1G_O = [3072, 3584]
H1GT_O = [4096, 4608]
WOUT_O = 3072
SC_BF_COLS = 11264

XQN_O = 0
MASK_O = 1024
GATES_O = 1152
STAT_O = 1216
LG_O = 1280
NM_O = 1344
NT2_O = 1352
XT_O = 1360
GT_O = 1424
SMX_O = 1456
QSTAT_O = 1472
MXS_O = 1488
LSS_O = 1504
SC_F_COLS = 1536


class KB:

    def __init__(self, semh, owner):
        self.semh = semh
        self.owner = owner
        self.cnt = {n: 0 for n in semh}
        self.lastw = {}
        self.cur = None
        self.block = None
        self.regions = {}
        self.conf = {}
        self.rw_ = {}
        self.rr_ = {}

    def region(self, name, tensor_key, lo, hi):
        assert name not in self.regions
        self.regions[name] = (tensor_key, lo, hi)
        self.conf[name] = []
        for o, (tk, l2, h2) in self.regions.items():
            if tk == tensor_key and not (hi <= l2 or h2 <= lo):
                self.conf[name].append(o)
                if o != name:
                    self.conf[o].append(name)
        self.rw_[name] = None
        self.rr_[name] = []

    def body(self, engname, fn):
        self.cur = engname
        getattr(self.block, engname)(fn)
        self.cur = None

    def tick(self, sem, instr, by=1):
        self.cnt[sem] += by
        instr.then_inc(self.semh[sem], by)
        return (sem, self.cnt[sem])

    def wait(self, eng, ev):
        if ev is None:
            return
        sem, val = ev
        lk = (self.cur, sem)
        if self.lastw.get(lk, 0) < val:
            self.lastw[lk] = val
            eng.wait_ge(self.semh[sem], val)

    def cut(self, eng, instr, sem):
        ev = self.tick(sem, instr)
        self.lastw[(self.cur, sem)] = ev[1]
        eng.wait_ge(self.semh[sem], ev[1])
        return ev

    def rw(self, eng, regs):
        for r in regs:
            for c in self.conf[r]:
                self.wait(eng, self.rw_[c])
                for ev in self.rr_[c]:
                    self.wait(eng, ev)

    def cw(self, regs, ev):
        for r in regs:
            self.rw_[r] = ev
            self.rr_[r] = []

    def rr(self, eng, regs):
        for r in regs:
            for c in self.conf[r]:
                self.wait(eng, self.rw_[c])

    def cr(self, regs, ev):
        for r in regs:
            if ev is not None and ev not in self.rr_[r]:
                self.rr_[r].append(ev)


def _psbanks(lo, hi):
    return ["pss%d" % b for b in range(lo // 512, (hi + 511) // 512)]


def build_program(phases=3, debug=False):
    nc = bass.Bass("TRN2", target_bir_lowering=False)

    x_d = nc.dram_tensor("x", [NSEQ, S, D], F32, kind="ExternalInput")
    wqkv_d = nc.dram_tensor("wqkvl", [3, 128, 8192], F32, kind="ExternalInput")
    qs_d = nc.dram_tensor("q_scale", [128, NC], F32, kind="ExternalInput")
    ks_d = nc.dram_tensor("k_scale", [128, NC], F32, kind="ExternalInput")
    wout_d = nc.dram_tensor("woutl", [128, 8192], F32, kind="ExternalInput")
    rw_d = nc.dram_tensor("routerwl", [128, NC * E_R], F32, kind="ExternalInput")
    rb_d = nc.dram_tensor("router_b", [1, E_R], F32, kind="ExternalInput")
    ffnw_d = nc.dram_tensor("ffnw", [10, 128, SLOT], F32, kind="ExternalInput")
    y_d = nc.dram_tensor("y", [NSEQ, S, D], F32, kind="ExternalOutput")
    dbg = {}
    if debug:
        dbg["smx"] = nc.dram_tensor("dbg_smx", [128, 16], F32, kind="ExternalOutput")
        dbg["asb"] = nc.dram_tensor("dbg_asb", [128, 2048], BF16, kind="ExternalOutput")
        dbg["ctxT"] = nc.dram_tensor("dbg_ctxT", [128, 4096], BF16, kind="ExternalOutput")
        dbg["qnt"] = nc.dram_tensor("dbg_qnt", [128, 8192], BF16, kind="ExternalOutput")
        dbg["knt"] = nc.dram_tensor("dbg_knt", [128, 16384], BF16, kind="ExternalOutput")
        dbg["vsb"] = nc.dram_tensor("dbg_vsb", [128, 16384], BF16, kind="ExternalOutput")

    semnames = ["sq_x", "w0", "w1", "w2", "w3", "wa", "wb", "wm",
                "gp", "pe", "ve", "se"]
    owner = {"gp": "gpsimd", "pe": "tensor", "ve": "vector", "se": "scalar"}

    with contextlib.ExitStack() as st:
        resid = st.enter_context(nc.sbuf_tensor("resid", [128, NT * D], F32))
        r1 = st.enter_context(nc.sbuf_tensor("r1", [128, 8192], BF16))
        big = st.enter_context(nc.sbuf_tensor("big", [128, 4 * SLOT], BF16))
        sc_bf = st.enter_context(nc.sbuf_tensor("sc_bf", [128, SC_BF_COLS], BF16))
        sc_f = st.enter_context(nc.sbuf_tensor("sc_f", [128, SC_F_COLS], F32))
        id_bf = st.enter_context(nc.sbuf_tensor("id_bf", [128, 128], BF16))
        mask_bf = st.enter_context(nc.sbuf_tensor("mask_bf", [128, 128], BF16))
        id_f = st.enter_context(nc.sbuf_tensor("id_f", [128, 128], F32))
        iop = st.enter_context(nc.sbuf_tensor("iop", [128, 1], F32))
        ioc = st.enter_context(nc.sbuf_tensor("ioc", [128, 128], F32))
        ones1 = st.enter_context(nc.sbuf_tensor("ones1", [1, 128], F32))
        rb1 = st.enter_context(nc.sbuf_tensor("rb1", [1, E_R], F32))
        rbb = st.enter_context(nc.sbuf_tensor("rbb", [128, E_R], F32))
        qs8 = st.enter_context(nc.sbuf_tensor("qs8", [128, NC], F32))
        ks8 = st.enter_context(nc.sbuf_tensor("ks8", [128, NC], F32))
        rwsb = st.enter_context(nc.sbuf_tensor("rwsb", [128, NC * E_R], BF16))
        cst = st.enter_context(nc.sbuf_tensor("cst", [128, 6], F32))

        ps_all = st.enter_context(nc.psum_tensor("ps_all", [128, 3072], F32))
        ps_s = ps_all[:, 0:2048]
        ps_ab = ps_all[:, 2048:3072]
        ps_tr = st.enter_context(nc.psum_tensor("ps_tr", [128, 1024], BF16))
        ps_ctx = st.enter_context(nc.psum_tensor("ps_ctx", [128, 512], F32))

        semh = {n: st.enter_context(nc.semaphore(n)) for n in semnames}
        K = KB(semh, owner)

        for t in range(NT):
            K.region("res%d" % t, "resid", t * D, (t + 1) * D)
        for i in range(8):
            K.region("r1t%d" % i, "r1", i * 1024, (i + 1) * 1024)
        for s in range(4):
            K.region("slot%d" % s, "big", s * SLOT, (s + 1) * SLOT)
        K.region("knt", "big", KNT_O, KNT_O + 16384)
        for t in range(NT):
            K.region("vsb%d" % t, "big", V_O + t * 1024, V_O + (t + 1) * 1024)
        for q in range(4):
            K.region("asb%d" % q, "big", ASB_O + q * 2048, ASB_O + (q + 1) * 2048)
        K.region("atT0", "big", ATT_O, ATT_O + 512)
        K.region("atT1", "big", ATT_O + 512, ATT_O + 1024)
        K.region("atT2", "sc_bf", 1024, 1536)
        K.region("atT3", "sc_bf", 1536, 2048)
        K.region("ctxT", "big", CTXT_O, CTXT_O + 4096)
        K.region("wwa", "big", WWA_O, WWA_O + 8192)
        K.region("wwb", "big", WWB_O, WWB_O + 8192)
        K.region("xn2", "sc_bf", XN2_O, XN2_O + 1024)
        K.region("h1g0", "sc_bf", H1G_O[0], H1G_O[0] + 512)
        K.region("h1g1", "sc_bf", H1G_O[1], H1G_O[1] + 512)
        K.region("h1gT0", "sc_bf", H1GT_O[0], H1GT_O[0] + 512)
        K.region("h1gT1", "sc_bf", H1GT_O[1], H1GT_O[1] + 512)
        K.region("woutsb", "sc_bf", WOUT_O, WOUT_O + 8192)
        K.region("xqn", "sc_f", XQN_O, XQN_O + 1024)
        K.region("gates", "sc_f", GATES_O, GATES_O + 64)
        K.region("lnstat", "sc_f", STAT_O, STAT_O + 64)
        for t in range(8):
            K.region("lg%d" % t, "sc_f", LG_O + t * 8, LG_O + (t + 1) * 8)
            K.region("nm%d" % t, "sc_f", NM_O + t, NM_O + t + 1)
            K.region("nt%d" % t, "sc_f", NT2_O + t, NT2_O + t + 1)
            K.region("xt%d" % t, "sc_f", XT_O + t * 8, XT_O + (t + 1) * 8)
        K.region("gt", "sc_f", GT_O, GT_O + 32)
        for q in range(4):
            K.region("negm%d" % q, "sc_f", SMX_O + q, SMX_O + q + 1)
            K.region("lsum%d" % q, "sc_f", SMX_O + 4 + q, SMX_O + 5 + q)
            K.region("rl%d" % q, "sc_f", SMX_O + 8 + q, SMX_O + 9 + q)
        K.region("qstat", "sc_f", QSTAT_O, QSTAT_O + 8)
        K.region("qstatB", "sc_f", QSTAT_O + 8, QSTAT_O + 16)
        for q in range(4):
            K.region("mxs%d" % q, "sc_f", MXS_O + q * 4, MXS_O + (q + 1) * 4)
            K.region("lss%d" % q, "sc_f", LSS_O + q * 4, LSS_O + (q + 1) * 4)
        K.region("cst", "cst", 0, 6)
        for b in range(4):
            K.region("pss%d" % b, "psall", b * 512, (b + 1) * 512)
        K.region("pab0", "psall", 2048, 2560)
        K.region("pab1", "psall", 2560, 3072)
        K.region("ptr", "ps_tr", 0, 1024)
        K.region("pctx", "ps_ctx", 0, 512)

        pair_stream = []
        for seq in range(NSEQ):
            for call in (1, 2):
                if call == 2 and phases < 3:
                    continue
                for grp in range(2):
                    for p in range(len(PAIRS)):
                        pair_stream.append((seq, call, grp, p))
        attn_barrier = {}
        if phases >= 2:
            for seq in range(NSEQ):
                for j, (s2, c2, _, _) in enumerate(pair_stream):
                    if s2 > seq or (s2 == seq and c2 == 2):
                        attn_barrier[j] = seq
                        if j + 1 < len(pair_stream):
                            attn_barrier[j + 1] = seq
                        break

        def pair_slots(i):
            return ((2 * i) % 4, (2 * i + 1) % 4)

        def slot_dma(g, si, kind, ei):
            gid = ei if kind == "r" else E_R + ei
            sem = "w%d" % si
            off = si * SLOT
            K.rw(g, ["slot%d" % si])
            ev = K.tick(sem, g.dma_start(
                big[:, off:off + SLOT], ffnw_d[gid]), 16)
            K.cw(["slot%d" % si], ev)

        def prologue():
            def g_pro(g):
                g.iota(ioc[:], [[1, 128]], channel_multiplier=0,
                       allow_small_or_imprecise_dtypes=True)
                g.iota(iop[:], [[1, 1]], channel_multiplier=1,
                       allow_small_or_imprecise_dtypes=True)
                g.memset(ones1[:], 1.0)
                g.memset(cst[:, 0:1], EPS_LN)
                g.memset(cst[:, 1:2], EPS_QK)
                g.memset(cst[:, 2:3], 1.0 / D)
                g.memset(cst[:, 3:4], -1.0)
                ev = K.tick("gp", g.memset(cst[:, 4:5], -0.5))
                K.cw(["gt", "cst"], ev)
                K.tick("wm", g.dma_start(rb1[:], rb_d[:]), 16)
                K.tick("wm", g.dma_start(qs8[:], qs_d[:]), 16)
                K.tick("wm", g.dma_start(ks8[:], ks_d[:]), 16)
                ev = K.tick("wm", g.dma_start(rwsb[:], rw_d[:]), 16)
                K.cw(["qstat"], ev)
                for i in (0, 1):
                    s0, s1 = pair_slots(i)
                    (k0, e0), (k1, e1) = PAIRS[pair_stream[i][3]]
                    slot_dma(g, s0, k0, e0)
                    slot_dma(g, s1, k1, e1)
            K.body("gpsimd", g_pro)

            def v_pro(v):
                K.wait(v, K.rw_["gt"])
                K.wait(v, K.rw_["qstat"])
                ins = v.tensor_scalar(id_f[:], ioc[:], iop[:], None, op0=ALU.is_equal)
                K.cut(v, ins, "ve")
                v.tensor_copy(id_bf[:], id_f[:])
                ins = v.tensor_scalar(sc_f[:, MASK_O:MASK_O + 128], ioc[:], iop[:],
                                      NEG_BIG, op0=ALU.is_gt, op1=ALU.mult)
                K.cut(v, ins, "ve")
                v.tensor_copy(mask_bf[:], sc_f[:, MASK_O:MASK_O + 128])
                ev = K.tick("ve", v.tensor_scalar_mul(qs8[:], qs8[:], 0.125))
                K.cw(["gt"], ev)
            K.body("vector", v_pro)

            def t_pro(t):
                K.wait(t, K.rw_["gt"])
                K.rw(t, ["pab0"])
                ev = K.tick("pe", t.matmul(ps_ab[:, 0:E_R], ones1[:], rb1[:],
                                           start=True, stop=True))
                K.cw(["pab0"], ev)
            K.body("tensor", t_pro)

            def v_pro2(v):
                K.rr(v, ["pab0"])
                ev = K.tick("ve", v.tensor_copy(rbb[:], ps_ab[:, 0:E_R]))
                K.cr(["pab0"], ev)
                K.cw(["qstat"], ev)
            K.body("vector", v_pro2)

        def ln_group(tiles, eps, alt_scratch=False):
            res_regs = ["res%d" % t for t in tiles]

            def v_sum(v):
                K.rr(v, res_regs)
                K.rw(v, ["lnstat"])
                ins = None
                for i, t in enumerate(tiles):
                    ins = v.reduce_sum(sc_f[:, STAT_O + i:STAT_O + i + 1],
                                       resid[:, t * D:(t + 1) * D], axis=AX.X)
                ev = K.tick("ve", ins)
                K.cw(["lnstat"], ev)
                K.cr(res_regs, ev)
            K.body("vector", v_sum)

            def s_sq(s):
                scr = ["atT2", "atT3"] if alt_scratch else ["xqn"]
                out = (sc_bf[:, 1024:1024 + D] if alt_scratch
                       else sc_f[:, XQN_O:XQN_O + D])
                K.rr(s, res_regs + ["lnstat"])
                K.rw(s, scr + ["lnstat"])
                ins = None
                for i, t in enumerate(tiles):
                    ins = s.activation(out, resid[:, t * D:(t + 1) * D],
                                       AF.Square,
                                       accum_out=sc_f[:, STAT_O + 8 + i:STAT_O + 9 + i])
                ev = K.tick("se", ins)
                K.cw(scr + ["lnstat"], ev)
                K.cr(res_regs, ev)
            K.body("scalar", s_sq)

            def v_fin(v):
                mean8 = sc_f[:, STAT_O:STAT_O + 8]
                var8 = sc_f[:, STAT_O + 8:STAT_O + 16]
                m2 = sc_f[:, STAT_O + 32:STAT_O + 40]
                K.rr(v, ["lnstat"])
                K.rw(v, ["lnstat"])
                ins = v.tensor_scalar_mul(mean8[:], mean8[:], 1.0 / D)
                K.cut(v, ins, "ve")
                v.tensor_tensor(m2[:], mean8[:], mean8[:], op=ALU.mult)
                ins = v.tensor_scalar_mul(var8[:], var8[:], 1.0 / D)
                K.cut(v, ins, "ve")
                ev = K.tick("ve", v.tensor_tensor(var8[:], var8[:], m2[:],
                                                  op=ALU.subtract))
                K.cw(["lnstat"], ev)
            K.body("vector", v_fin)

            def s_rt(s):
                sig8 = sc_f[:, STAT_O + 16:STAT_O + 24]
                var8 = sc_f[:, STAT_O + 8:STAT_O + 16]
                rstd8 = sc_f[:, STAT_O + 24:STAT_O + 32]
                K.rr(s, ["lnstat", "cst"])
                K.rw(s, ["lnstat"])
                epsap = cst[:, 0:1] if eps == EPS_LN else cst[:, 1:2]
                ins = s.activation(sig8[:], var8[:], AF.Ln, bias=epsap, scale=1.0)
                K.cut(s, ins, "se")
                ev = K.tick("se", s.activation(rstd8[:], sig8[:], AF.Exp,
                                               bias=0.0, scale=cst[:, 4:5]))
                K.cw(["lnstat"], ev)
            K.body("scalar", s_rt)


        def xnT_tile(seq, call, grp, i, t, dest, dest_off):
            def v_n(v):
                rstd8 = sc_f[:, STAT_O + 24:STAT_O + 32]
                sig8 = sc_f[:, STAT_O + 16:STAT_O + 24]
                mean8 = sc_f[:, STAT_O:STAT_O + 8]
                K.rr(v, ["lnstat"])
                K.rw(v, ["xn2"])
                K.rr(v, ["res%d" % t])
                ins = v.tensor_scalar(
                    sc_bf[:, XN2_O:XN2_O + D], resid[:, t * D:(t + 1) * D],
                    mean8[:, i:i + 1], rstd8[:, i:i + 1],
                    op0=ALU.subtract, op1=ALU.mult)
                ev = K.tick("ve", ins)
                K.cw(["xn2"], ev)
                K.cr(["res%d" % t], ev)
                K.cr(["lnstat"], ev)
            K.body("vector", v_n)

            def t_tr(t_eng):
                K.rr(t_eng, ["xn2", "gt"])
                K.rw(t_eng, ["ptr"])
                ins = None
                for cc in range(8):
                    ins = t_eng.transpose(
                        ps_tr[:, cc * 128:(cc + 1) * 128],
                        sc_bf[:, XN2_O + cc * 128:XN2_O + (cc + 1) * 128],
                        id_bf[:])
                ev = K.tick("pe", ins)
                K.cw(["ptr"], ev)
                K.cr(["xn2"], ev)
            K.body("tensor", t_tr)

            def v_ev2(v):
                K.rr(v, ["ptr"])
                K.rw(v, [dest])
                ev2 = K.tick("ve", v.tensor_copy(
                    (r1 if dest.startswith("r1") else big)[
                        :, dest_off:dest_off + 1024],
                    ps_tr[:]))
                K.cw([dest], ev2)
                K.cr(["ptr"], ev2)
            K.body("vector", v_ev2)

        def ffn2(seq, call):
            for grp in range(2):
                t0 = 8 * grp
                tiles = [t0 + i for i in range(8)]

                if grp == 0:
                    ln_group(tiles, EPS_LN)

                for i, t in enumerate(tiles):
                    xnT_tile(seq, call, grp, i, t, "r1t%d" % i, i * 1024)

                    def t_rt(t_eng, i=i):
                        K.rr(t_eng, ["r1t%d" % i])
                        K.wait(t_eng, K.rw_["qstat"])
                        K.rw(t_eng, ["pab0"])
                        ins = None
                        for c in range(NC):
                            ins = t_eng.matmul(
                                ps_ab[:, 0:E_R],
                                r1[:, i * 1024 + c * 128:i * 1024 + (c + 1) * 128],
                                rwsb[:, c * E_R:(c + 1) * E_R],
                                start=(c == 0), stop=(c == NC - 1))
                        ev = K.tick("pe", ins)
                        K.cw(["pab0"], ev)
                        K.cr(["r1t%d" % i], ev)
                    K.body("tensor", t_rt)

                    def v_g1(v, i=i):
                        lg = sc_f[:, LG_O + i * 8:LG_O + (i + 1) * 8]
                        nm = sc_f[:, NM_O + i:NM_O + i + 1]
                        nt2 = sc_f[:, NT2_O + i:NT2_O + i + 1]
                        eq = sc_f[:, GT_O:GT_O + 8]
                        l2 = sc_f[:, GT_O + 8:GT_O + 16]
                        K.rr(v, ["pab0"])
                        K.rw(v, ["lg%d" % i, "nm%d" % i, "nt%d" % i, "gt"])
                        ins = v.tensor_tensor(lg[:], ps_ab[:, 0:E_R], rbb[:], op=ALU.add)
                        K.cut(v, ins, "ve")
                        ev0 = K.cut(v, v.reduce_max(nm[:], lg[:], axis=AX.X,
                                                    negate=True), "ve")
                        K.cr(["pab0"], ev0)
                        ins = v.tensor_scalar(eq[:], lg[:], nm[:], 0.0,
                                              op0=ALU.add, op1=ALU.is_ge)
                        K.cut(v, ins, "ve")
                        ins = v.scalar_tensor_tensor(l2[:], eq[:], NEG_BIG, lg[:],
                                                     op0=ALU.mult, op1=ALU.add)
                        K.cut(v, ins, "ve")
                        ev = K.tick("ve", v.reduce_max(nt2[:], l2[:], axis=AX.X,
                                                       negate=True))
                        K.cw(["lg%d" % i, "nm%d" % i, "nt%d" % i, "gt"], ev)
                    K.body("vector", v_g1)

                def s_g(s):
                    for i in range(8):
                        lg = sc_f[:, LG_O + i * 8:LG_O + (i + 1) * 8]
                        nm = sc_f[:, NM_O + i:NM_O + i + 1]
                        xt = sc_f[:, XT_O + i * 8:XT_O + (i + 1) * 8]
                        K.rr(s, ["lg%d" % i, "nm%d" % i])
                        K.rw(s, ["xt%d" % i])
                        ev = K.tick("se", s.activation(xt[:], lg[:], AF.Exp,
                                                       bias=nm[:], scale=1.0))
                        K.cw(["xt%d" % i], ev)
                        K.cr(["lg%d" % i, "nm%d" % i], ev)
                K.body("scalar", s_g)

                def v_g2(v):
                    for i in range(8):
                        lg = sc_f[:, LG_O + i * 8:LG_O + (i + 1) * 8]
                        nt2 = sc_f[:, NT2_O + i:NT2_O + i + 1]
                        xt = sc_f[:, XT_O + i * 8:XT_O + (i + 1) * 8]
                        ind = sc_f[:, GT_O + 16:GT_O + 24]
                        ge = sc_f[:, GT_O + 24:GT_O + 32]
                        ssum = sc_f[:, GT_O + 14:GT_O + 15]
                        rs = sc_f[:, GT_O + 15:GT_O + 16]
                        gcol = sc_f[:, GATES_O + i * 8:GATES_O + (i + 1) * 8]
                        K.rr(v, ["xt%d" % i, "lg%d" % i, "nt%d" % i])
                        K.rw(v, ["gt", "gates"])
                        ins = v.tensor_scalar(ind[:], lg[:], nt2[:], 0.0,
                                              op0=ALU.add, op1=ALU.is_ge)
                        K.cut(v, ins, "ve")
                        ins = v.scalar_tensor_tensor(ge[:], xt[:], 1.0, ind[:],
                                                     op0=ALU.bypass, op1=ALU.mult,
                                                     accum_out=ssum[:])
                        K.cut(v, ins, "ve")
                        ins = v.reciprocal(rs[:], ssum[:])
                        K.cut(v, ins, "ve")
                        ins = v.tensor_scalar(gcol[:], ge[:], rs[:], 0.5,
                                              op0=ALU.mult, op1=ALU.mult)
                        ev = K.tick("ve", ins)
                        K.cw(["gt", "gates"], ev)
                        K.cr(["xt%d" % i, "lg%d" % i, "nt%d" % i], ev)
                K.body("vector", v_g2)

                base_i = pair_stream.index((seq, call, grp, 0))

                def mk_u1(p, ti, slots):
                    exps = PAIRS[p]
                    def u1():
                        def t_ab(t_eng):
                            for ei in range(2):
                                off = slots[ei] * SLOT
                                a_ps = ps_s[:, 1024:1536] if ei == 0 else ps_ab[:, 0:512]
                                b_ps = ps_s[:, 1536:2048] if ei == 0 else ps_ab[:, 512:1024]
                                a_rg = ["pss2"] if ei == 0 else ["pab0"]
                                b_rg = ["pss3"] if ei == 0 else ["pab1"]
                                K.wait(t_eng, K.rw_["slot%d" % slots[ei]])
                                K.rr(t_eng, ["r1t%d" % ti, "slot%d" % slots[ei]])
                                K.rw(t_eng, a_rg)
                                ins = None
                                for c in range(NC):
                                    lhs = r1[:, ti * 1024 + c * 128:ti * 1024 + (c + 1) * 128]
                                    ins = t_eng.matmul(
                                        a_ps, lhs,
                                        big[:, off + c * 512:off + (c + 1) * 512],
                                        start=(c == 0), stop=(c == NC - 1))
                                ev = K.tick("pe", ins)
                                K.cw(a_rg, ev)
                                K.rw(t_eng, b_rg)
                                ins = None
                                for c in range(NC):
                                    lhs = r1[:, ti * 1024 + c * 128:ti * 1024 + (c + 1) * 128]
                                    ins = t_eng.matmul(
                                        b_ps, lhs,
                                        big[:, off + 4096 + c * 512:off + 4096 + (c + 1) * 512],
                                        start=(c == 0), stop=(c == NC - 1))
                                ev = K.tick("pe", ins)
                                K.cw(b_rg, ev)
                                K.cr(["r1t%d" % ti, "slot%d" % slots[ei]], ev)
                        K.body("tensor", t_ab)

                        def s_silu2(s):
                            for ei in range(2):
                                a_ps = ps_s[:, 1024:1536] if ei == 0 else ps_ab[:, 0:512]
                                a_rg = ["pss2"] if ei == 0 else ["pab0"]
                                sa = sc_f[:, XQN_O + ei * 512:XQN_O + (ei + 1) * 512]
                                K.rr(s, a_rg + ["xqn"])
                                K.rw(s, ["xqn"])
                                ev = K.tick("se", s.activation(sa[:], a_ps, AF.Silu,
                                                               bias=0.0, scale=1.0))
                                K.cw(["xqn"], ev)
                                K.cr(a_rg, ev)
                        K.body("scalar", s_silu2)

                        def v_stt2(v):
                            for ei in range(2):
                                kind, eidx = exps[ei]
                                b_ps = ps_s[:, 1536:2048] if ei == 0 else ps_ab[:, 512:1024]
                                b_rg = ["pss3"] if ei == 0 else ["pab1"]
                                sa = sc_f[:, XQN_O + ei * 512:XQN_O + (ei + 1) * 512]
                                ho = H1G_O[ei]
                                K.rr(v, ["xqn"] + b_rg)
                                if kind == "r":
                                    K.rr(v, ["gates"])
                                K.rw(v, ["h1g%d" % ei])
                                gsc = (sc_f[:, GATES_O + ti * 8 + eidx:
                                            GATES_O + ti * 8 + eidx + 1]
                                       if kind == "r" else 0.5)
                                ins = v.scalar_tensor_tensor(
                                    sc_bf[:, ho:ho + 512], sa[:], gsc, b_ps,
                                    op0=ALU.mult, op1=ALU.mult)
                                ev = K.tick("ve", ins)
                                K.cw(["h1g%d" % ei], ev)
                                K.cr(["xqn"] + b_rg, ev)
                                if kind == "r":
                                    K.cr(["gates"], ev)
                        K.body("vector", v_stt2)
                    return u1

                def mk_u2(p, ti, slots):
                    def u2():
                        def t_h1(t_eng):
                            K.rr(t_eng, ["h1g0", "h1g1"])
                            K.rw(t_eng, ["ptr"])
                            ins = None
                            for ei in range(2):
                                ho = H1G_O[ei]
                                for c in range(4):
                                    ins = t_eng.transpose(
                                        ps_tr[:, ei * 512 + c * 128:
                                              ei * 512 + (c + 1) * 128],
                                        sc_bf[:, ho + c * 128:ho + (c + 1) * 128],
                                        id_bf[:])
                            ev = K.tick("pe", ins)
                            K.cw(["ptr"], ev)
                            K.cr(["h1g0", "h1g1"], ev)
                        K.body("tensor", t_h1)

                        def v_h1T(v):
                            K.rr(v, ["ptr"])
                            K.rw(v, ["h1gT0", "h1gT1"])
                            v.tensor_copy(
                                sc_bf[:, H1GT_O[0]:H1GT_O[0] + 512], ps_tr[:, 0:512])
                            ev = K.tick("ve", v.tensor_copy(
                                sc_bf[:, H1GT_O[1]:H1GT_O[1] + 512], ps_tr[:, 512:1024]))
                            K.cw(["h1gT0", "h1gT1"], ev)
                            K.cr(["ptr"], ev)
                        K.body("vector", v_h1T)
                    return u2

                def mk_u3(p, ti, slots, t):
                    def u3():
                        def t_w2both(t_eng):
                            for ei in range(2):
                                off = slots[ei] * SLOT
                                hto = H1GT_O[ei]
                                K.rr(t_eng, ["h1gT%d" % ei, "slot%d" % slots[ei]])
                                if ei == 0:
                                    K.rw(t_eng, ["pss0", "pss1"])
                                ins = None
                                for hc in range(4):
                                    lhs = sc_bf[:, hto + hc * 128:hto + (hc + 1) * 128]
                                    for dh in range(2):
                                        ins = t_eng.matmul(
                                            ps_s[:, dh * 512:(dh + 1) * 512], lhs,
                                            big[:, off + 8192 + hc * 1024 + dh * 512:
                                                off + 8192 + hc * 1024 + (dh + 1) * 512],
                                            start=(hc == 0 and ei == 0),
                                            stop=(hc == 3 and ei == 1))
                                ev = K.tick("pe", ins)
                                if ei == 1:
                                    K.cw(["pss0", "pss1"], ev)
                                K.cr(["h1gT%d" % ei, "slot%d" % slots[ei]], ev)
                        K.body("tensor", t_w2both)

                        def v_yadd(v):
                            K.rr(v, ["pss0", "pss1"])
                            K.rw(v, ["res%d" % t])
                            ins = v.tensor_tensor(
                                resid[:, t * D:(t + 1) * D],
                                resid[:, t * D:(t + 1) * D],
                                ps_s[:, 0:1024], op=ALU.add)
                            ev = K.tick("ve", ins)
                            K.cw(["res%d" % t], ev)
                            K.cr(["pss0", "pss1"], ev)
                        K.body("vector", v_yadd)
                    return u3

                def mk_gw(i_str):
                    def gw():
                        def g_w(g):
                            j = i_str + 2
                            if j < len(pair_stream) and j not in attn_barrier:
                                sj0, sj1 = pair_slots(j)
                                (k0, e0), (k1, e1) = PAIRS[pair_stream[j][3]]
                                slot_dma(g, sj0, k0, e0)
                                slot_dma(g, sj1, k1, e1)
                        K.body("gpsimd", g_w)
                    return gw

                U1, U2, U3, GW = [], [], [], {}
                for p in range(len(PAIRS)):
                    i_str = base_i + p
                    slots = pair_slots(i_str)
                    for ti in range(8):
                        U1.append(mk_u1(p, ti, slots))
                        U2.append(mk_u2(p, ti, slots))
                        U3.append(mk_u3(p, ti, slots, t0 + ti))
                    GW[8 * p + 7] = mk_gw(i_str)

                n = len(U1)
                U1[0]()
                U2[0]()
                for k in range(n):
                    if k + 1 < n:
                        U1[k + 1]()
                    U3[k]()
                    if k + 1 < n:
                        U2[k + 1]()
                    if k in GW:
                        GW[k]()
                    if grp == 0 and k == 33:
                        ln_group([8 + j for j in range(8)], EPS_LN,
                                 alt_scratch=True)

        def qkv_half(seq, half):
            t0 = 8 * half
            tiles = [t0 + i for i in range(8)]

            def g_w(g, seq=seq, half=half):
                K.rw(g, ["wwa"])
                ev = K.tick("wa", g.dma_start(
                    big[:, WWA_O:WWA_O + 8192], wqkv_d[0]), 16)
                K.cw(["wwa"], ev)
                K.rw(g, ["wwb"])
                ev = K.tick("wb", g.dma_start(
                    big[:, WWB_O:WWB_O + 8192], wqkv_d[1]), 16)
                K.cw(["wwb"], ev)
                if half == 0:
                    K.rw(g, ["woutsb"])
                    ev = K.tick("wm", g.dma_start(
                        sc_bf[:, WOUT_O:WOUT_O + 8192], wout_d[:]), 16)
                    K.cw(["woutsb"], ev)
            K.body("gpsimd", g_w)

            ln_group(tiles, EPS_LN)

            for i, t in enumerate(tiles):
                xnT_tile(seq, 0, half, i, t, "vsb%d" % t, V_O + t * 1024)

            def qk_units(wname, i, t):
                par = i % 2
                qo = QSTAT_O + 8 * par
                qrg = "qstat" if par == 0 else "qstatB"
                if wname == "q":
                    dst = ps_s[:, 0:1024] if par == 0 else ps_s[:, 1024:2048]
                    dst_rg = ["pss0", "pss1"] if par == 0 else ["pss2", "pss3"]
                    wrg, wcol = "wwa", WWA_O
                elif wname == "k":
                    dst = ps_s[:, 0:1024] if par == 0 else ps_s[:, 1024:2048]
                    dst_rg = ["pss0", "pss1"] if par == 0 else ["pss2", "pss3"]
                    wrg, wcol = "wwb", WWB_O
                else:
                    dst = ps_ab if par == 0 else ps_s[:, 0:1024]
                    dst_rg = ["pab0", "pab1"] if par == 0 else ["pss0", "pss1"]
                    wrg, wcol = "wwa", WWA_O

                def u1():
                    def t_mm(t_eng):
                        K.rr(t_eng, ["vsb%d" % t, wrg])
                        K.rw(t_eng, dst_rg)
                        xt = V_O + t * 1024
                        ins = None
                        for c in range(NC):
                            lhs = big[:, xt + c * 128:xt + (c + 1) * 128]
                            for dh in range(2):
                                ins = t_eng.matmul(
                                    dst[:, dh * 512:(dh + 1) * 512], lhs,
                                    big[:, wcol + c * 1024 + dh * 512:
                                        wcol + c * 1024 + (dh + 1) * 512],
                                    start=(c == 0), stop=(c == NC - 1))
                        ev = K.tick("pe", ins)
                        K.cw(dst_rg, ev)
                        K.cr(["vsb%d" % t, wrg], ev)
                    K.body("tensor", t_mm)

                def u2():
                    if wname in ("q", "k"):
                        def v_st(v):
                            qsum = sc_f[:, qo:qo + 1]
                            K.rr(v, dst_rg)
                            K.rw(v, [qrg])
                            ev = K.tick("ve", v.reduce_sum(qsum[:], dst[:],
                                                           axis=AX.X))
                            K.cw([qrg], ev)
                            K.cr(dst_rg, ev)
                        K.body("vector", v_st)

                        def s_sq(s):
                            qsum = sc_f[:, qo:qo + 1]
                            qss = sc_f[:, qo + 1:qo + 2]
                            mn = sc_f[:, qo + 2:qo + 3]
                            m2n = sc_f[:, qo + 3:qo + 4]
                            vr = sc_f[:, qo + 4:qo + 5]
                            sg = sc_f[:, qo + 5:qo + 6]
                            K.rr(s, dst_rg + [qrg, "xqn", "cst"])
                            K.rw(s, ["xqn", qrg])
                            ins = s.activation(sc_f[:, XQN_O:XQN_O + D], dst[:],
                                               AF.Square, accum_out=qss[:])
                            K.cut(s, ins, "se")
                            ins = s.mul(mn[:], qsum[:], cst[:, 2:3])
                            K.cut(s, ins, "se")
                            ins = s.mul(m2n[:], mn[:], mn[:])
                            K.cut(s, ins, "se")
                            ins = s.mul(m2n[:], m2n[:], cst[:, 3:4])
                            K.cut(s, ins, "se")
                            ins = s.add(vr[:], m2n[:], cst[:, 1:2])
                            K.cut(s, ins, "se")
                            ins = s.activation(sg[:], qss[:], AF.Ln,
                                               bias=vr[:], scale=cst[:, 2:3])
                            K.cut(s, ins, "se")
                            rsd = sc_f[:, qo + 6:qo + 7]
                            ev = K.tick("se", s.activation(rsd[:], sg[:], AF.Exp,
                                                           bias=0.0,
                                                           scale=cst[:, 4:5]))
                            K.cw(["xqn", qrg], ev)
                            K.cr(dst_rg, ev)
                        K.body("scalar", s_sq)

                        def v_nm(v):
                            rs = sc_f[:, qo + 6:qo + 7]
                            mn = sc_f[:, qo + 2:qo + 3]
                            K.rr(v, [qrg] + dst_rg + ["xqn"])
                            K.rw(v, ["xqn"])
                            ins = v.tensor_scalar(sc_f[:, XQN_O:XQN_O + D], dst[:],
                                                  mn[:], rs[:],
                                                  op0=ALU.subtract, op1=ALU.mult)
                            ev = K.tick("ve", ins)
                            K.cw(["xqn"], ev)
                            K.cr(dst_rg + [qrg], ev)
                        K.body("vector", v_nm)
                    else:
                        def v_vev(v):
                            K.rr(v, dst_rg)
                            K.rw(v, ["vsb%d" % t])
                            ev = K.tick("ve", v.tensor_copy(
                                big[:, V_O + t * 1024:V_O + (t + 1) * 1024],
                                dst[:]))
                            K.cw(["vsb%d" % t], ev)
                            K.cr(dst_rg, ev)
                        K.body("vector", v_vev)

                def u3():
                    if wname == "v":
                        return
                    scv = qs8 if wname == "q" else ks8
                    for hf in range(2):
                        def t_tr(t_eng, hf=hf):
                            K.rr(t_eng, ["xqn", "gt"])
                            K.rw(t_eng, ["pctx"])
                            ins = None
                            for c in range(4):
                                cc = 4 * hf + c
                                ins = t_eng.transpose(
                                    ps_ctx[:, c * 128:(c + 1) * 128],
                                    sc_f[:, XQN_O + cc * 128:XQN_O + (cc + 1) * 128],
                                    id_f[:])
                            ev = K.tick("pe", ins)
                            K.cw(["pctx"], ev)
                            if hf == 1:
                                K.cr(["xqn"], ev)
                        K.body("tensor", t_tr)

                        def v_qe(v, hf=hf):
                            K.rr(v, ["pctx"])
                            drg = (["r1t%d" % j for j in range(8)]
                                   if wname == "q" else ["knt"])
                            K.rw(v, drg)
                            ins = None
                            for c in range(4):
                                cc = 4 * hf + c
                                if wname == "q":
                                    d_ap = r1[:, cc * 1024 + i * 128:
                                              cc * 1024 + (i + 1) * 128]
                                else:
                                    d_ap = big[:, KNT_O + cc * 2048 + t * 128:
                                               KNT_O + cc * 2048 + (t + 1) * 128]
                                ins = v.tensor_scalar_mul(
                                    d_ap, ps_ctx[:, c * 128:(c + 1) * 128],
                                    scv[:, cc:cc + 1])
                            ev = K.tick("ve", ins)
                            K.cw(drg, ev)
                            K.cr(["pctx"], ev)
                        K.body("vector", v_qe)
                return u1, u2, u3

            for wname in "qkv":
                if wname == "v":
                    def g_wv(g):
                        K.rw(g, ["wwa"])
                        ev = K.tick("wa", g.dma_start(
                            big[:, WWA_O:WWA_O + 8192], wqkv_d[2]), 16)
                        K.cw(["wwa"], ev)
                    K.body("gpsimd", g_wv)

                us = [qk_units(wname, i, t) for i, t in enumerate(tiles)]
                us[0][0]()
                us[0][1]()
                for i in range(len(us)):
                    if i + 1 < len(us):
                        us[i + 1][0]()
                    us[i][2]()
                    if i + 1 < len(us):
                        us[i + 1][1]()

        def attn_qgroup(seq, qg):
            nkc = 4 * (qg + 1)
            qts = [4 * qg + j for j in range(4)]
            if qg == 0:
                waves = [[0, 1, 2, 3]]
                ro_of = {0: 0, 1: 512, 2: 1024, 3: 1536}
            elif qg == 1:
                waves = [[0, 1], [2, 3]]
                ro_of = {0: 0, 1: 1024, 2: 0, 3: 1024}
            else:
                waves = [[0], [1], [2], [3]]
                ro_of = {0: 0, 1: 0, 2: 0, 3: 0}

            def att_ap(pk, slot, lo, hi):
                if pk == 0:
                    return big[:, ATT_O + slot * 512 + lo:
                               ATT_O + slot * 512 + hi]
                return sc_bf[:, 1024 + slot * 512 + lo:1024 + slot * 512 + hi]

            def make_tct(h, c, po):
                def mk_pair(kc0):
                    kcs = [kc for kc in (kc0, kc0 + 1) if kc < nkc]
                    vqls = {kc: [ql for ql, qt in enumerate(qts) if qt >= kc]
                            for kc in kcs}
                    pk = (kc0 // 2) % 2
                    args = [("atT%d" % (pk * 2 + kc % 2)) for kc in kcs]

                    def t_at(t_eng, kcs=kcs, vqls=vqls):
                        regs = sorted({"asb%d" % ql for kc in kcs
                                       for ql in vqls[kc]})
                        K.rr(t_eng, regs + ["gt"])
                        K.rw(t_eng, ["ptr"])
                        ins = None
                        for j2, kc in enumerate(kcs):
                            for j, ql in enumerate(vqls[kc]):
                                ins = t_eng.transpose(
                                    ps_tr[:, j2 * 512 + j * 128:
                                          j2 * 512 + (j + 1) * 128],
                                    big[:, ASB_O + ql * 2048 + kc * 128:
                                        ASB_O + ql * 2048 + (kc + 1) * 128],
                                    id_bf[:])
                        ev = K.tick("pe", ins)
                        K.cw(["ptr"], ev)
                        K.cr(regs, ev)

                    def v_ev(v, kcs=kcs, vqls=vqls, pk=pk, args=args):
                        K.rr(v, ["ptr"])
                        K.rw(v, args)
                        ins = None
                        for j2, kc in enumerate(kcs):
                            q0 = vqls[kc][0] * 128
                            n = 512 - q0
                            ins = v.tensor_copy(
                                att_ap(pk, kc % 2, q0, 512),
                                ps_tr[:, j2 * 512:j2 * 512 + n])
                        ev = K.tick("ve", ins)
                        K.cw(args, ev)
                        K.cr(["ptr"], ev)

                    def t_cx(t_eng, kcs=kcs, vqls=vqls, pk=pk, c=c, po=po,
                             h=h):
                        for kc in kcs:
                            q0 = vqls[kc][0] * 128
                            n = 512 - q0
                            rg = "atT%d" % (pk * 2 + kc % 2)
                            K.rr(t_eng, [rg, "vsb%d" % kc])
                            if kc == 0:
                                K.rw(t_eng, ["pctx"])
                            ins = t_eng.matmul(
                                ps_ctx[po:po + 64, q0:q0 + n],
                                big[:, V_O + kc * 1024 + h * 64:
                                    V_O + kc * 1024 + h * 64 + 64],
                                att_ap(pk, kc % 2, q0, 512),
                                start=(kc == 0), stop=(kc == nkc - 1),
                                tile_position=(0, po))
                            ev = K.tick("pe", ins)
                            if kc == nkc - 1:
                                K.cw(["pctx"], ev)
                            K.cr([rg, "vsb%d" % kc], ev)
                    return t_at, v_ev, t_cx

                def emit_tct():
                    pairs = [mk_pair(kc0) for kc0 in range(0, nkc, 2)]
                    K.body("tensor", pairs[0][0])
                    K.body("vector", pairs[0][1])
                    for k in range(1, len(pairs)):
                        K.body("tensor", pairs[k][0])
                        K.body("vector", pairs[k][1])
                        K.body("tensor", pairs[k - 1][2])
                    K.body("tensor", pairs[-1][2])

                    def v_cx(v, c=c, po=po):
                        K.rr(v, ["pctx"])
                        K.rw(v, ["ctxT"])
                        ev = K.tick("ve", v.tensor_copy(
                            big[po:po + 64,
                                CTXT_O + c * 512:CTXT_O + (c + 1) * 512],
                            ps_ctx[po:po + 64, :]))
                        K.cw(["ctxT"], ev)
                        K.cr(["pctx"], ev)
                    K.body("vector", v_cx)
                return emit_tct

            BANKS = ["pss0", "pss1", "pss2", "pss3", "pab0", "pab1"]

            nkg0 = (128 * (qts[0] + 1) + 511) // 512
            step = {1: 1, 2: 2, 3: 3, 4: 2}[nkg0]
            mod = {1: 6, 2: 6, 3: 6, 4: 4}[nkg0]
            stream = []
            cur = 0
            prevb = set()
            for h in range(NH):
                for wi, wave in enumerate(waves):
                    alloc = {}
                    plan = {}
                    for ql in wave:
                        keys = 128 * (qts[ql] + 1)
                        nkg = (keys + 511) // 512
                        assert nkg == nkg0
                        bks = list(range(cur, cur + nkg))
                        alloc[ql] = bks
                        ejs = [j for j in range(nkg) if bks[j] not in prevb]
                        ljs = [j for j in range(nkg) if bks[j] in prevb]
                        assert ejs == list(range(min(ejs), max(ejs) + 1)) \
                            if ejs else True
                        plan[ql] = (ejs, ljs)
                        cur = (cur + step) % mod
                    prevb = {b for ql in wave for b in alloc[ql]}
                    stream.append((h, wi, wave, alloc, plan))

            def region_of(alloc, ql, js, keys):
                lo = alloc[ql][0] * 512 + js[0] * 512
                hi = alloc[ql][0] * 512 + min(keys, (js[-1] + 1) * 512)
                return lo, hi

            def emit_sc(item, which):
                h, wi, wave, alloc, plan = item
                c = h // 2
                po = 64 * (h % 2)
                part = [(ql, j) for ql in wave for j in plan[ql][which]]
                if not part:
                    return

                def t_sc(t_eng, part=part, alloc=alloc, c=c, po=po):
                    by_ql = {}
                    for ql, j in part:
                        by_ql.setdefault(ql, []).append(j)
                    for ql, js in by_ql.items():
                        qt = qts[ql]
                        keys = 128 * (qt + 1)
                        nkg = len(alloc[ql])
                        base = alloc[ql][0]
                        rgs = [BANKS[alloc[ql][j]] for j in js]
                        K.rr(t_eng, ["r1t%d" % c, "knt"])
                        K.rw(t_eng, rgs)
                        lhs = r1[po:po + 64, c * 1024 + (qt % 8) * 128:
                                 c * 1024 + (qt % 8 + 1) * 128]
                        ins = None
                        for j in js:
                            n = min(512, keys - j * 512)
                            o = base * 512 + j * 512
                            last = (j == nkg - 1)
                            ins = t_eng.matmul(
                                ps_all[:, o:o + n], lhs,
                                big[po:po + 64,
                                    KNT_O + c * 2048 + j * 512:
                                    KNT_O + c * 2048 + j * 512 + n],
                                start=True, stop=not last)
                            if last:
                                ins = t_eng.matmul(
                                    ps_all[:, o + n - 128:o + n],
                                    id_bf[:], mask_bf[:],
                                    start=False, stop=True)
                        ev = K.tick("pe", ins)
                        K.cw(rgs, ev)
                        K.cr(["r1t%d" % c, "knt"], ev)
                K.body("tensor", t_sc)

            def emit_mxa(item):
                h, wi, wave, alloc, plan = item
                if not any(plan[ql][0] for ql in wave):
                    return

                def v_mxa(v, wave=wave, alloc=alloc, plan=plan):
                    for ql in wave:
                        ejs, ljs = plan[ql]
                        if not ejs:
                            continue
                        qt = qts[ql]
                        keys = 128 * (qt + 1)
                        lo, hi = region_of(alloc, ql, ejs, keys)
                        rgs = [BANKS[alloc[ql][j]] for j in ejs]
                        K.rr(v, rgs)
                        if ljs:
                            mx = sc_f[:, MXS_O + ql * 4:MXS_O + ql * 4 + 1]
                            K.rw(v, ["mxs%d" % ql])
                            ins = v.reduce_max(mx[:], ps_all[:, lo:hi],
                                               axis=AX.X, negate=False)
                            ev = K.tick("ve", ins)
                            K.cw(["mxs%d" % ql], ev)
                        else:
                            nm = sc_f[:, SMX_O + ql:SMX_O + ql + 1]
                            K.rw(v, ["negm%d" % ql])
                            ins = v.reduce_max(nm[:], ps_all[:, lo:hi],
                                               axis=AX.X, negate=True)
                            ev = K.tick("ve", ins)
                            K.cw(["negm%d" % ql], ev)
                        K.cr(rgs, ev)
                K.body("vector", v_mxa)

            def emit_mxb(item):
                h, wi, wave, alloc, plan = item
                if not any(plan[ql][1] for ql in wave):
                    return

                def v_mxb(v, wave=wave, alloc=alloc, plan=plan):
                    for ql in wave:
                        ejs, ljs = plan[ql]
                        if not ljs:
                            continue
                        qt = qts[ql]
                        keys = 128 * (qt + 1)
                        lo, hi = region_of(alloc, ql, ljs, keys)
                        rgs = [BANKS[alloc[ql][j]] for j in ljs]
                        nm = sc_f[:, SMX_O + ql:SMX_O + ql + 1]
                        K.rr(v, rgs)
                        K.rw(v, ["negm%d" % ql, "mxs%d" % ql])
                        if not ejs:
                            ins = v.reduce_max(nm[:], ps_all[:, lo:hi],
                                               axis=AX.X, negate=True)
                        else:
                            mx1 = sc_f[:, MXS_O + ql * 4 + 1:
                                       MXS_O + ql * 4 + 2]
                            ins = v.reduce_max(mx1[:], ps_all[:, lo:hi],
                                               axis=AX.X, negate=False)
                            K.cut(v, ins, "ve")
                            ins = v.reduce_max(
                                nm[:], sc_f[:, MXS_O + ql * 4:
                                            MXS_O + ql * 4 + 2],
                                axis=AX.X, negate=True)
                        ev = K.tick("ve", ins)
                        K.cw(["negm%d" % ql, "mxs%d" % ql], ev)
                        K.cr(rgs, ev)
                K.body("vector", v_mxb)

            prev_tct = None
            for i, item in enumerate(stream):
                h, wi, wave, alloc, plan = item
                if i == 0:
                    emit_sc(item, 0)
                    emit_mxa(item)
                if i + 1 < len(stream):
                    emit_sc(stream[i + 1], 0)
                emit_mxb(item)

                if wi == 0 and prev_tct is not None:
                    prev_tct()
                    prev_tct = None

                def s_ex(s, wave=wave, alloc=alloc):
                    for ql in wave:
                        qt = qts[ql]
                        keys = 128 * (qt + 1)
                        ro = alloc[ql][0] * 512
                        rgs = [BANKS[b] for b in alloc[ql]]
                        nm = sc_f[:, SMX_O + ql:SMX_O + ql + 1]
                        ls = sc_f[:, SMX_O + 4 + ql:SMX_O + 5 + ql]
                        K.rr(s, rgs + ["negm%d" % ql])
                        K.rw(s, ["asb%d" % ql, "lsum%d" % ql])
                        ins = s.activation(
                            big[:, ASB_O + ql * 2048:ASB_O + ql * 2048 + keys],
                            ps_all[:, ro:ro + keys], AF.Exp,
                            bias=nm[:], scale=1.0, accum_out=ls[:])
                        ev = K.tick("se", ins)
                        K.cw(["asb%d" % ql, "lsum%d" % ql], ev)
                        K.cr(rgs + ["negm%d" % ql], ev)
                K.body("scalar", s_ex)

                if i + 1 < len(stream):
                    emit_sc(stream[i + 1], 1)
                    emit_mxa(stream[i + 1])

                def v_sc2(v, wave=wave):
                    ins = None
                    for ql in wave:
                        ls = sc_f[:, SMX_O + 4 + ql:SMX_O + 5 + ql]
                        rl = sc_f[:, SMX_O + 8 + ql:SMX_O + 9 + ql]
                        K.rr(v, ["lsum%d" % ql])
                        K.rw(v, ["rl%d" % ql])
                        ins = v.reciprocal(rl[:], ls[:])
                    K.cut(v, ins, "ve")
                    for ql in wave:
                        qt = qts[ql]
                        keys = 128 * (qt + 1)
                        rl = sc_f[:, SMX_O + 8 + ql:SMX_O + 9 + ql]
                        K.rr(v, ["rl%d" % ql, "asb%d" % ql])
                        K.rw(v, ["asb%d" % ql])
                        ins = v.tensor_scalar_mul(
                            big[:, ASB_O + ql * 2048:ASB_O + ql * 2048 + keys],
                            big[:, ASB_O + ql * 2048:ASB_O + ql * 2048 + keys],
                            rl[:])
                        ev = K.tick("ve", ins)
                        K.cw(["asb%d" % ql, "rl%d" % ql], ev)
                        K.cr(["lsum%d" % ql], ev)
                K.body("vector", v_sc2)

                if wi == len(waves) - 1:
                    assert prev_tct is None
                    prev_tct = make_tct(h, h // 2, 64 * (h % 2))
            prev_tct()

            for ql, qt in enumerate(qts):
                def t_wo(t_eng, ql=ql):
                    K.rr(t_eng, ["ctxT", "woutsb"])
                    K.rw(t_eng, ["pab0", "pab1"])
                    ins = None
                    for c2 in range(NC):
                        lhs = big[:, CTXT_O + c2 * 512 + ql * 128:
                                  CTXT_O + c2 * 512 + (ql + 1) * 128]
                        for dh in range(2):
                            ins = t_eng.matmul(
                                ps_ab[:, dh * 512:(dh + 1) * 512], lhs,
                                sc_bf[:, WOUT_O + c2 * 1024 + dh * 512:
                                      WOUT_O + c2 * 1024 + (dh + 1) * 512],
                                start=(c2 == 0), stop=(c2 == NC - 1))
                    ev = K.tick("pe", ins)
                    K.cw(["pab0", "pab1"], ev)
                    K.cr(["ctxT", "woutsb"], ev)
                K.body("tensor", t_wo)

                def v_wa(v, qt=qt):
                    K.rr(v, ["pab0", "pab1"])
                    K.rw(v, ["res%d" % qt])
                    ins = v.tensor_tensor(resid[:, qt * D:(qt + 1) * D],
                                          resid[:, qt * D:(qt + 1) * D],
                                          ps_ab, op=ALU.add)
                    ev = K.tick("ve", ins)
                    K.cw(["res%d" % qt], ev)
                    K.cr(["pab0", "pab1"], ev)
                K.body("vector", v_wa)

        K.block = None

        def emit(block):
            K.block = block
            prologue()
            for seq in range(NSEQ):
                def sx(g, seq=seq):
                    K.rw(g, ["res%d" % t for t in range(NT)])
                    ev = K.tick("sq_x", g.dma_start(
                        resid[:].rearrange("p (t f) -> p t f", t=NT),
                        x_d[seq].rearrange("(t p) f -> p t f", p=128)), 16)
                    K.cw(["res%d" % t for t in range(NT)], ev)
                K.body("sync", sx)

                ffn2(seq, 1)
                if phases >= 2:
                    for half in range(2):
                        qkv_half(seq, half)
                        for qgl in range(2):
                            attn_qgroup(seq, 2 * half + qgl)
                    def g_post(g, seq=seq):
                        for j in sorted(attn_barrier):
                            if attn_barrier[j] == seq:
                                sj0, sj1 = pair_slots(j)
                                (k0, e0), (k1, e1) = PAIRS[pair_stream[j][3]]
                                slot_dma(g, sj0, k0, e0)
                                slot_dma(g, sj1, k1, e1)
                    K.body("gpsimd", g_post)
                if phases >= 3:
                    ffn2(seq, 2)

                def sy(g, seq=seq):
                    for t in range(NT):
                        K.rr(g, ["res%d" % t])
                    ev = K.tick("sq_x", g.dma_start(
                        y_d[seq].rearrange("(t p) f -> p t f", p=128),
                        resid[:].rearrange("p (t f) -> p t f", t=NT)), 16)
                    K.cr(["res%d" % t for t in range(NT)], ev)
                    if seq == NSEQ - 1:
                        K.final_store = ev
                K.body("sync", sy)

            def g_end(g):
                K.wait(g, K.final_store)
            K.body("gpsimd", g_end)

        with nc.Block() as blk:
            emit(blk)

    return nc


def kernel(**inputs):
    xs = np.ascontiguousarray(np.asarray(inputs["L0_attn_x"], dtype=np.float32))
    wqkv = np.ascontiguousarray(np.asarray(inputs["L0_attn_W_qkv"], dtype=np.float32))
    qs = np.ascontiguousarray(np.asarray(inputs["L0_attn_q_scale"], dtype=np.float32).reshape(NC, 128).T)
    ks = np.ascontiguousarray(np.asarray(inputs["L0_attn_k_scale"], dtype=np.float32).reshape(NC, 128).T)
    wout = np.ascontiguousarray(np.asarray(inputs["L0_attn_W_out"], dtype=np.float32))
    rw = np.ascontiguousarray(np.asarray(inputs["L0_ffn_router_w"], dtype=np.float32))
    rb = np.ascontiguousarray(np.asarray(inputs["L0_ffn_router_b"], dtype=np.float32).reshape(1, E_R))
    rw1 = np.ascontiguousarray(np.asarray(inputs["L0_ffn_routed_w1"], dtype=np.float32))
    rw3 = np.ascontiguousarray(np.asarray(inputs["L0_ffn_routed_w3"], dtype=np.float32))
    rw2 = np.ascontiguousarray(np.asarray(inputs["L0_ffn_routed_w2"], dtype=np.float32))
    sw1 = np.ascontiguousarray(np.asarray(inputs["L0_ffn_shared_w1"], dtype=np.float32))
    sw3 = np.ascontiguousarray(np.asarray(inputs["L0_ffn_shared_w3"], dtype=np.float32))
    sw2 = np.ascontiguousarray(np.asarray(inputs["L0_ffn_shared_w2"], dtype=np.float32))

    def pack_dh(w):
        X = w.shape[1]
        return w.reshape(NC, 128, X).transpose(1, 0, 2).reshape(128, NC * X)

    def pack_h(w):
        X = w.shape[1]
        return w.reshape(4, 128, X).transpose(1, 0, 2).reshape(128, 4 * X)

    ffnw = np.stack([np.concatenate(
        [pack_dh(w1[e]), pack_dh(w3[e]), pack_h(w2[e])], axis=1)
        for (w1, w3, w2, es) in ((rw1, rw3, rw2, range(E_R)), (sw1, sw3, sw2, range(2)))
        for e in es])
    ffnw = np.ascontiguousarray(ffnw)
    wqkvl = np.ascontiguousarray(np.stack(
        [pack_dh(wqkv[:, m * D:(m + 1) * D]) for m in range(3)]))
    woutl = np.ascontiguousarray(pack_dh(wout))
    rwl = np.ascontiguousarray(pack_dh(rw))

    nc = build_program(phases=3)
    in_maps = []
    for core in range(8):
        in_maps.append({
            "x": np.ascontiguousarray(xs[2 * core:2 * core + 2]),
            "wqkvl": wqkvl, "q_scale": qs, "k_scale": ks, "woutl": woutl,
            "routerwl": rwl, "router_b": rb, "ffnw": ffnw,
        })
    res = run_bass_kernel_spmd(nc, in_maps, core_ids=list(range(8)))
    out = np.concatenate([r["y"] for r in res.results], axis=0)
    return out.reshape(16, S, D)


# version 28
# speedup vs baseline: 1.0305x; 1.0305x over previous
import contextlib
import numpy as np

import concourse.bass as bass
import concourse.mybir as mybir
from concourse.bass_utils import run_bass_kernel_spmd

F32 = mybir.dt.float32
BF16 = mybir.dt.bfloat16
AF = mybir.ActivationFunctionType
ALU = mybir.AluOpType
AX = mybir.AxisListType

D = 1024
H = 512
NH = 16
HD = 64
S = 2048
NT = 16
NSEQ = 2
NC = 8
E_R = 8
EPS_LN = 1e-5
EPS_QK = 1e-6
NEG_BIG = -3.0e38

PAIRS = [(("r", 0), ("r", 1)), (("r", 2), ("r", 3)), (("r", 4), ("r", 5)),
         (("r", 6), ("r", 7)), (("s", 0), ("s", 1))]

SLOT = 12288
KNT_O = 0
V_O = 16384
SUB_O = 32768
ASB_O = SUB_O
ATT_O = SUB_O + 8192
CTXT_O = SUB_O + 9216
WWA_O = SUB_O
WWB_O = SUB_O + 8192

XN2_O = 0
H1G_O = [3072, 3584]
H1GT_O = [4096, 4608]
WOUT_O = 3072
SC_BF_COLS = 11264

XQN_O = 0
MASK_O = 1024
GATES_O = 1152
STAT_O = 1216
LG_O = 1280
NM_O = 1344
NT2_O = 1352
XT_O = 1360
GT_O = 1424
SMX_O = 1456
QSTAT_O = 1472
MXS_O = 1488
LSS_O = 1504
SC_F_COLS = 1536


class KB:

    def __init__(self, semh, owner):
        self.semh = semh
        self.owner = owner
        self.cnt = {n: 0 for n in semh}
        self.lastw = {}
        self.cur = None
        self.block = None
        self.regions = {}
        self.conf = {}
        self.rw_ = {}
        self.rr_ = {}

    def region(self, name, tensor_key, lo, hi):
        assert name not in self.regions
        self.regions[name] = (tensor_key, lo, hi)
        self.conf[name] = []
        for o, (tk, l2, h2) in self.regions.items():
            if tk == tensor_key and not (hi <= l2 or h2 <= lo):
                self.conf[name].append(o)
                if o != name:
                    self.conf[o].append(name)
        self.rw_[name] = None
        self.rr_[name] = []

    def body(self, engname, fn):
        self.cur = engname
        getattr(self.block, engname)(fn)
        self.cur = None

    def tick(self, sem, instr, by=1):
        self.cnt[sem] += by
        instr.then_inc(self.semh[sem], by)
        return (sem, self.cnt[sem])

    def wait(self, eng, ev):
        if ev is None:
            return
        sem, val = ev
        lk = (self.cur, sem)
        if self.lastw.get(lk, 0) < val:
            self.lastw[lk] = val
            eng.wait_ge(self.semh[sem], val)

    def cut(self, eng, instr, sem):
        ev = self.tick(sem, instr)
        self.lastw[(self.cur, sem)] = ev[1]
        eng.wait_ge(self.semh[sem], ev[1])
        return ev

    def rw(self, eng, regs):
        for r in regs:
            for c in self.conf[r]:
                self.wait(eng, self.rw_[c])
                for ev in self.rr_[c]:
                    self.wait(eng, ev)

    def cw(self, regs, ev):
        for r in regs:
            self.rw_[r] = ev
            self.rr_[r] = []

    def rr(self, eng, regs):
        for r in regs:
            for c in self.conf[r]:
                self.wait(eng, self.rw_[c])

    def cr(self, regs, ev):
        for r in regs:
            if ev is not None and ev not in self.rr_[r]:
                self.rr_[r].append(ev)


def _psbanks(lo, hi):
    return ["pss%d" % b for b in range(lo // 512, (hi + 511) // 512)]


def build_program(phases=3, debug=False):
    nc = bass.Bass("TRN2", target_bir_lowering=False)

    x_d = nc.dram_tensor("x", [NSEQ, S, D], F32, kind="ExternalInput")
    wqkv_d = nc.dram_tensor("wqkvl", [3, 128, 8192], F32, kind="ExternalInput")
    qs_d = nc.dram_tensor("q_scale", [128, NC], F32, kind="ExternalInput")
    ks_d = nc.dram_tensor("k_scale", [128, NC], F32, kind="ExternalInput")
    wout_d = nc.dram_tensor("woutl", [128, 8192], F32, kind="ExternalInput")
    rw_d = nc.dram_tensor("routerwl", [128, NC * E_R], F32, kind="ExternalInput")
    rb_d = nc.dram_tensor("router_b", [1, E_R], F32, kind="ExternalInput")
    ffnw_d = nc.dram_tensor("ffnw", [10, 128, SLOT], F32, kind="ExternalInput")
    y_d = nc.dram_tensor("y", [NSEQ, S, D], F32, kind="ExternalOutput")
    dbg = {}
    if debug:
        dbg["smx"] = nc.dram_tensor("dbg_smx", [128, 16], F32, kind="ExternalOutput")
        dbg["asb"] = nc.dram_tensor("dbg_asb", [128, 2048], BF16, kind="ExternalOutput")
        dbg["ctxT"] = nc.dram_tensor("dbg_ctxT", [128, 4096], BF16, kind="ExternalOutput")
        dbg["qnt"] = nc.dram_tensor("dbg_qnt", [128, 8192], BF16, kind="ExternalOutput")
        dbg["knt"] = nc.dram_tensor("dbg_knt", [128, 16384], BF16, kind="ExternalOutput")
        dbg["vsb"] = nc.dram_tensor("dbg_vsb", [128, 16384], BF16, kind="ExternalOutput")

    semnames = ["sq_x", "w0", "w1", "w2", "w3", "wa", "wb", "wm",
                "gp", "pe", "ve", "se"]
    owner = {"gp": "gpsimd", "pe": "tensor", "ve": "vector", "se": "scalar"}

    with contextlib.ExitStack() as st:
        resid = st.enter_context(nc.sbuf_tensor("resid", [128, NT * D], F32))
        r1 = st.enter_context(nc.sbuf_tensor("r1", [128, 8192], BF16))
        big = st.enter_context(nc.sbuf_tensor("big", [128, 4 * SLOT], BF16))
        sc_bf = st.enter_context(nc.sbuf_tensor("sc_bf", [128, SC_BF_COLS], BF16))
        sc_f = st.enter_context(nc.sbuf_tensor("sc_f", [128, SC_F_COLS], F32))
        id_bf = st.enter_context(nc.sbuf_tensor("id_bf", [128, 128], BF16))
        mask_bf = st.enter_context(nc.sbuf_tensor("mask_bf", [128, 128], BF16))
        id_f = st.enter_context(nc.sbuf_tensor("id_f", [128, 128], F32))
        iop = st.enter_context(nc.sbuf_tensor("iop", [128, 1], F32))
        ioc = st.enter_context(nc.sbuf_tensor("ioc", [128, 128], F32))
        ones1 = st.enter_context(nc.sbuf_tensor("ones1", [1, 128], F32))
        rb1 = st.enter_context(nc.sbuf_tensor("rb1", [1, E_R], F32))
        rbb = st.enter_context(nc.sbuf_tensor("rbb", [128, E_R], F32))
        qs8 = st.enter_context(nc.sbuf_tensor("qs8", [128, NC], F32))
        ks8 = st.enter_context(nc.sbuf_tensor("ks8", [128, NC], F32))
        rwsb = st.enter_context(nc.sbuf_tensor("rwsb", [128, NC * E_R], BF16))
        cst = st.enter_context(nc.sbuf_tensor("cst", [128, 6], F32))

        ps_all = st.enter_context(nc.psum_tensor("ps_all", [128, 3072], F32))
        ps_s = ps_all[:, 0:2048]
        ps_ab = ps_all[:, 2048:3072]
        ps_tr = st.enter_context(nc.psum_tensor("ps_tr", [128, 1024], BF16))
        ps_ctx = st.enter_context(nc.psum_tensor("ps_ctx", [128, 512], F32))

        semh = {n: st.enter_context(nc.semaphore(n)) for n in semnames}
        K = KB(semh, owner)

        for t in range(NT):
            K.region("res%d" % t, "resid", t * D, (t + 1) * D)
        for i in range(8):
            K.region("r1t%d" % i, "r1", i * 1024, (i + 1) * 1024)
        for s in range(4):
            K.region("slot%d" % s, "big", s * SLOT, (s + 1) * SLOT)
        K.region("knt", "big", KNT_O, KNT_O + 16384)
        for t in range(NT):
            K.region("vsb%d" % t, "big", V_O + t * 1024, V_O + (t + 1) * 1024)
        for q in range(4):
            K.region("asb%d" % q, "big", ASB_O + q * 2048, ASB_O + (q + 1) * 2048)
        K.region("atT0", "big", ATT_O, ATT_O + 512)
        K.region("atT1", "big", ATT_O + 512, ATT_O + 1024)
        K.region("atT2", "sc_bf", 1024, 1536)
        K.region("atT3", "sc_bf", 1536, 2048)
        K.region("ctxT", "big", CTXT_O, CTXT_O + 4096)
        K.region("wwa", "big", WWA_O, WWA_O + 8192)
        K.region("wwb", "big", WWB_O, WWB_O + 8192)
        K.region("xn2", "sc_bf", XN2_O, XN2_O + 1024)
        K.region("h1g0", "sc_bf", H1G_O[0], H1G_O[0] + 512)
        K.region("h1g1", "sc_bf", H1G_O[1], H1G_O[1] + 512)
        K.region("h1gT0", "sc_bf", H1GT_O[0], H1GT_O[0] + 512)
        K.region("h1gT1", "sc_bf", H1GT_O[1], H1GT_O[1] + 512)
        K.region("woutsb", "sc_bf", WOUT_O, WOUT_O + 8192)
        K.region("xqn", "sc_f", XQN_O, XQN_O + 1024)
        K.region("gates", "sc_f", GATES_O, GATES_O + 64)
        K.region("lnstat", "sc_f", STAT_O, STAT_O + 64)
        for t in range(8):
            K.region("lg%d" % t, "sc_f", LG_O + t * 8, LG_O + (t + 1) * 8)
            K.region("nm%d" % t, "sc_f", NM_O + t, NM_O + t + 1)
            K.region("nt%d" % t, "sc_f", NT2_O + t, NT2_O + t + 1)
            K.region("xt%d" % t, "sc_f", XT_O + t * 8, XT_O + (t + 1) * 8)
        K.region("gt", "sc_f", GT_O, GT_O + 32)
        for q in range(4):
            K.region("negm%d" % q, "sc_f", SMX_O + q, SMX_O + q + 1)
            K.region("lsum%d" % q, "sc_f", SMX_O + 4 + q, SMX_O + 5 + q)
            K.region("rl%d" % q, "sc_f", SMX_O + 8 + q, SMX_O + 9 + q)
        K.region("qstat", "sc_f", QSTAT_O, QSTAT_O + 8)
        K.region("qstatB", "sc_f", QSTAT_O + 8, QSTAT_O + 16)
        for q in range(4):
            K.region("mxs%d" % q, "sc_f", MXS_O + q * 4, MXS_O + (q + 1) * 4)
            K.region("lss%d" % q, "sc_f", LSS_O + q * 4, LSS_O + (q + 1) * 4)
        K.region("cst", "cst", 0, 6)
        for b in range(4):
            K.region("pss%d" % b, "psall", b * 512, (b + 1) * 512)
        K.region("pab0", "psall", 2048, 2560)
        K.region("pab1", "psall", 2560, 3072)
        K.region("ptr", "ps_tr", 0, 1024)
        K.region("pctx", "ps_ctx", 0, 512)

        pair_stream = []
        for seq in range(NSEQ):
            for call in (1, 2):
                if call == 2 and phases < 3:
                    continue
                for grp in range(2):
                    for p in range(len(PAIRS)):
                        pair_stream.append((seq, call, grp, p))
        attn_barrier = {}
        if phases >= 2:
            for seq in range(NSEQ):
                for j, (s2, c2, _, _) in enumerate(pair_stream):
                    if s2 > seq or (s2 == seq and c2 == 2):
                        attn_barrier[j] = seq
                        if j + 1 < len(pair_stream):
                            attn_barrier[j + 1] = seq
                        break

        def pair_slots(i):
            return ((2 * i) % 4, (2 * i + 1) % 4)

        def slot_dma(g, si, kind, ei):
            gid = ei if kind == "r" else E_R + ei
            sem = "w%d" % si
            off = si * SLOT
            K.rw(g, ["slot%d" % si])
            ev = K.tick(sem, g.dma_start(
                big[:, off:off + SLOT], ffnw_d[gid]), 16)
            K.cw(["slot%d" % si], ev)

        def prologue():
            def g_pro(g):
                g.iota(ioc[:], [[1, 128]], channel_multiplier=0,
                       allow_small_or_imprecise_dtypes=True)
                g.iota(iop[:], [[1, 1]], channel_multiplier=1,
                       allow_small_or_imprecise_dtypes=True)
                g.memset(ones1[:], 1.0)
                g.memset(cst[:, 0:1], EPS_LN)
                g.memset(cst[:, 1:2], EPS_QK)
                g.memset(cst[:, 2:3], 1.0 / D)
                g.memset(cst[:, 3:4], -1.0)
                ev = K.tick("gp", g.memset(cst[:, 4:5], -0.5))
                K.cw(["gt", "cst"], ev)
                K.tick("wm", g.dma_start(rb1[:], rb_d[:]), 16)
                K.tick("wm", g.dma_start(qs8[:], qs_d[:]), 16)
                K.tick("wm", g.dma_start(ks8[:], ks_d[:]), 16)
                ev = K.tick("wm", g.dma_start(rwsb[:], rw_d[:]), 16)
                K.cw(["qstat"], ev)
                for i in (0, 1):
                    s0, s1 = pair_slots(i)
                    (k0, e0), (k1, e1) = PAIRS[pair_stream[i][3]]
                    slot_dma(g, s0, k0, e0)
                    slot_dma(g, s1, k1, e1)
            K.body("gpsimd", g_pro)

            def v_pro(v):
                K.wait(v, K.rw_["gt"])
                K.wait(v, K.rw_["qstat"])
                ins = v.tensor_scalar(id_f[:], ioc[:], iop[:], None, op0=ALU.is_equal)
                K.cut(v, ins, "ve")
                v.tensor_copy(id_bf[:], id_f[:])
                ins = v.tensor_scalar(sc_f[:, MASK_O:MASK_O + 128], ioc[:], iop[:],
                                      NEG_BIG, op0=ALU.is_gt, op1=ALU.mult)
                K.cut(v, ins, "ve")
                v.tensor_copy(mask_bf[:], sc_f[:, MASK_O:MASK_O + 128])
                ev = K.tick("ve", v.tensor_scalar_mul(qs8[:], qs8[:], 0.125))
                K.cw(["gt"], ev)
            K.body("vector", v_pro)

            def t_pro(t):
                K.wait(t, K.rw_["gt"])
                K.rw(t, ["pab0"])
                ev = K.tick("pe", t.matmul(ps_ab[:, 0:E_R], ones1[:], rb1[:],
                                           start=True, stop=True))
                K.cw(["pab0"], ev)
            K.body("tensor", t_pro)

            def v_pro2(v):
                K.rr(v, ["pab0"])
                ev = K.tick("ve", v.tensor_copy(rbb[:], ps_ab[:, 0:E_R]))
                K.cr(["pab0"], ev)
                K.cw(["qstat"], ev)
            K.body("vector", v_pro2)

        def ln_group(tiles, eps, alt_scratch=False):
            res_regs = ["res%d" % t for t in tiles]

            def v_sum(v):
                K.rr(v, res_regs)
                K.rw(v, ["lnstat"])
                ins = None
                for i, t in enumerate(tiles):
                    ins = v.reduce_sum(sc_f[:, STAT_O + i:STAT_O + i + 1],
                                       resid[:, t * D:(t + 1) * D], axis=AX.X)
                ev = K.tick("ve", ins)
                K.cw(["lnstat"], ev)
                K.cr(res_regs, ev)
            K.body("vector", v_sum)

            def s_sq(s):
                scr = ["atT2", "atT3"] if alt_scratch else ["xqn"]
                out = (sc_bf[:, 1024:1024 + D] if alt_scratch
                       else sc_f[:, XQN_O:XQN_O + D])
                K.rr(s, res_regs + ["lnstat"])
                K.rw(s, scr + ["lnstat"])
                ins = None
                for i, t in enumerate(tiles):
                    ins = s.activation(out, resid[:, t * D:(t + 1) * D],
                                       AF.Square,
                                       accum_out=sc_f[:, STAT_O + 8 + i:STAT_O + 9 + i])
                ev = K.tick("se", ins)
                K.cw(scr + ["lnstat"], ev)
                K.cr(res_regs, ev)
            K.body("scalar", s_sq)

            def v_fin(v):
                mean8 = sc_f[:, STAT_O:STAT_O + 8]
                var8 = sc_f[:, STAT_O + 8:STAT_O + 16]
                m2 = sc_f[:, STAT_O + 32:STAT_O + 40]
                K.rr(v, ["lnstat"])
                K.rw(v, ["lnstat"])
                ins = v.tensor_scalar_mul(mean8[:], mean8[:], 1.0 / D)
                K.cut(v, ins, "ve")
                v.tensor_tensor(m2[:], mean8[:], mean8[:], op=ALU.mult)
                ins = v.tensor_scalar_mul(var8[:], var8[:], 1.0 / D)
                K.cut(v, ins, "ve")
                ev = K.tick("ve", v.tensor_tensor(var8[:], var8[:], m2[:],
                                                  op=ALU.subtract))
                K.cw(["lnstat"], ev)
            K.body("vector", v_fin)

            def s_rt(s):
                sig8 = sc_f[:, STAT_O + 16:STAT_O + 24]
                var8 = sc_f[:, STAT_O + 8:STAT_O + 16]
                rstd8 = sc_f[:, STAT_O + 24:STAT_O + 32]
                K.rr(s, ["lnstat", "cst"])
                K.rw(s, ["lnstat"])
                epsap = cst[:, 0:1] if eps == EPS_LN else cst[:, 1:2]
                ins = s.activation(sig8[:], var8[:], AF.Ln, bias=epsap, scale=1.0)
                K.cut(s, ins, "se")
                ev = K.tick("se", s.activation(rstd8[:], sig8[:], AF.Exp,
                                               bias=0.0, scale=cst[:, 4:5]))
                K.cw(["lnstat"], ev)
            K.body("scalar", s_rt)


        def xnT_tile(seq, call, grp, i, t, dest, dest_off):
            def v_n(v):
                rstd8 = sc_f[:, STAT_O + 24:STAT_O + 32]
                sig8 = sc_f[:, STAT_O + 16:STAT_O + 24]
                mean8 = sc_f[:, STAT_O:STAT_O + 8]
                K.rr(v, ["lnstat"])
                K.rw(v, ["xn2"])
                K.rr(v, ["res%d" % t])
                ins = v.tensor_scalar(
                    sc_bf[:, XN2_O:XN2_O + D], resid[:, t * D:(t + 1) * D],
                    mean8[:, i:i + 1], rstd8[:, i:i + 1],
                    op0=ALU.subtract, op1=ALU.mult)
                ev = K.tick("ve", ins)
                K.cw(["xn2"], ev)
                K.cr(["res%d" % t], ev)
                K.cr(["lnstat"], ev)
            K.body("vector", v_n)

            def t_tr(t_eng):
                K.rr(t_eng, ["xn2", "gt"])
                K.rw(t_eng, ["ptr"])
                ins = None
                for cc in range(8):
                    ins = t_eng.transpose(
                        ps_tr[:, cc * 128:(cc + 1) * 128],
                        sc_bf[:, XN2_O + cc * 128:XN2_O + (cc + 1) * 128],
                        id_bf[:])
                ev = K.tick("pe", ins)
                K.cw(["ptr"], ev)
                K.cr(["xn2"], ev)
            K.body("tensor", t_tr)

            def v_ev2(v):
                K.rr(v, ["ptr"])
                K.rw(v, [dest])
                ev2 = K.tick("ve", v.tensor_copy(
                    (r1 if dest.startswith("r1") else big)[
                        :, dest_off:dest_off + 1024],
                    ps_tr[:]))
                K.cw([dest], ev2)
                K.cr(["ptr"], ev2)
            K.body("vector", v_ev2)

        def ffn2(seq, call):
            for grp in range(2):
                t0 = 8 * grp
                tiles = [t0 + i for i in range(8)]

                if grp == 0:
                    ln_group(tiles, EPS_LN)

                for i, t in enumerate(tiles):
                    xnT_tile(seq, call, grp, i, t, "r1t%d" % i, i * 1024)

                    def t_rt(t_eng, i=i):
                        K.rr(t_eng, ["r1t%d" % i])
                        K.wait(t_eng, K.rw_["qstat"])
                        K.rw(t_eng, ["pab0"])
                        ins = None
                        for c in range(NC):
                            ins = t_eng.matmul(
                                ps_ab[:, 0:E_R],
                                r1[:, i * 1024 + c * 128:i * 1024 + (c + 1) * 128],
                                rwsb[:, c * E_R:(c + 1) * E_R],
                                start=(c == 0), stop=(c == NC - 1))
                        ev = K.tick("pe", ins)
                        K.cw(["pab0"], ev)
                        K.cr(["r1t%d" % i], ev)
                    K.body("tensor", t_rt)

                    def v_g1(v, i=i):
                        lg = sc_f[:, LG_O + i * 8:LG_O + (i + 1) * 8]
                        nm = sc_f[:, NM_O + i:NM_O + i + 1]
                        nt2 = sc_f[:, NT2_O + i:NT2_O + i + 1]
                        eq = sc_f[:, GT_O:GT_O + 8]
                        l2 = sc_f[:, GT_O + 8:GT_O + 16]
                        K.rr(v, ["pab0"])
                        K.rw(v, ["lg%d" % i, "nm%d" % i, "nt%d" % i, "gt"])
                        ins = v.tensor_tensor(lg[:], ps_ab[:, 0:E_R], rbb[:], op=ALU.add)
                        K.cut(v, ins, "ve")
                        ev0 = K.cut(v, v.reduce_max(nm[:], lg[:], axis=AX.X,
                                                    negate=True), "ve")
                        K.cr(["pab0"], ev0)
                        ins = v.tensor_scalar(eq[:], lg[:], nm[:], 0.0,
                                              op0=ALU.add, op1=ALU.is_ge)
                        K.cut(v, ins, "ve")
                        ins = v.scalar_tensor_tensor(l2[:], eq[:], NEG_BIG, lg[:],
                                                     op0=ALU.mult, op1=ALU.add)
                        K.cut(v, ins, "ve")
                        ev = K.tick("ve", v.reduce_max(nt2[:], l2[:], axis=AX.X,
                                                       negate=True))
                        K.cw(["lg%d" % i, "nm%d" % i, "nt%d" % i, "gt"], ev)
                    K.body("vector", v_g1)

                def s_g(s):
                    for i in range(8):
                        lg = sc_f[:, LG_O + i * 8:LG_O + (i + 1) * 8]
                        nm = sc_f[:, NM_O + i:NM_O + i + 1]
                        xt = sc_f[:, XT_O + i * 8:XT_O + (i + 1) * 8]
                        K.rr(s, ["lg%d" % i, "nm%d" % i])
                        K.rw(s, ["xt%d" % i])
                        ev = K.tick("se", s.activation(xt[:], lg[:], AF.Exp,
                                                       bias=nm[:], scale=1.0))
                        K.cw(["xt%d" % i], ev)
                        K.cr(["lg%d" % i, "nm%d" % i], ev)
                K.body("scalar", s_g)

                def v_g2(v):
                    for i in range(8):
                        lg = sc_f[:, LG_O + i * 8:LG_O + (i + 1) * 8]
                        nt2 = sc_f[:, NT2_O + i:NT2_O + i + 1]
                        xt = sc_f[:, XT_O + i * 8:XT_O + (i + 1) * 8]
                        ind = sc_f[:, GT_O + 16:GT_O + 24]
                        ge = sc_f[:, GT_O + 24:GT_O + 32]
                        ssum = sc_f[:, GT_O + 14:GT_O + 15]
                        rs = sc_f[:, GT_O + 15:GT_O + 16]
                        gcol = sc_f[:, GATES_O + i * 8:GATES_O + (i + 1) * 8]
                        K.rr(v, ["xt%d" % i, "lg%d" % i, "nt%d" % i])
                        K.rw(v, ["gt", "gates"])
                        ins = v.tensor_scalar(ind[:], lg[:], nt2[:], 0.0,
                                              op0=ALU.add, op1=ALU.is_ge)
                        K.cut(v, ins, "ve")
                        ins = v.scalar_tensor_tensor(ge[:], xt[:], 1.0, ind[:],
                                                     op0=ALU.bypass, op1=ALU.mult,
                                                     accum_out=ssum[:])
                        K.cut(v, ins, "ve")
                        ins = v.reciprocal(rs[:], ssum[:])
                        K.cut(v, ins, "ve")
                        ins = v.tensor_scalar(gcol[:], ge[:], rs[:], 0.5,
                                              op0=ALU.mult, op1=ALU.mult)
                        ev = K.tick("ve", ins)
                        K.cw(["gt", "gates"], ev)
                        K.cr(["xt%d" % i, "lg%d" % i, "nt%d" % i], ev)
                K.body("vector", v_g2)

                base_i = pair_stream.index((seq, call, grp, 0))

                def mk_u1(p, ti, slots):
                    exps = PAIRS[p]
                    def u1():
                        def t_ab(t_eng):
                            for ei in range(2):
                                off = slots[ei] * SLOT
                                a_ps = ps_s[:, 1024:1536] if ei == 0 else ps_ab[:, 0:512]
                                b_ps = ps_s[:, 1536:2048] if ei == 0 else ps_ab[:, 512:1024]
                                a_rg = ["pss2"] if ei == 0 else ["pab0"]
                                b_rg = ["pss3"] if ei == 0 else ["pab1"]
                                K.wait(t_eng, K.rw_["slot%d" % slots[ei]])
                                K.rr(t_eng, ["r1t%d" % ti, "slot%d" % slots[ei]])
                                K.rw(t_eng, a_rg)
                                ins = None
                                for c in range(NC):
                                    lhs = r1[:, ti * 1024 + c * 128:ti * 1024 + (c + 1) * 128]
                                    ins = t_eng.matmul(
                                        a_ps, lhs,
                                        big[:, off + c * 512:off + (c + 1) * 512],
                                        start=(c == 0), stop=(c == NC - 1))
                                ev = K.tick("pe", ins)
                                K.cw(a_rg, ev)
                                K.rw(t_eng, b_rg)
                                ins = None
                                for c in range(NC):
                                    lhs = r1[:, ti * 1024 + c * 128:ti * 1024 + (c + 1) * 128]
                                    ins = t_eng.matmul(
                                        b_ps, lhs,
                                        big[:, off + 4096 + c * 512:off + 4096 + (c + 1) * 512],
                                        start=(c == 0), stop=(c == NC - 1))
                                ev = K.tick("pe", ins)
                                K.cw(b_rg, ev)
                                K.cr(["r1t%d" % ti, "slot%d" % slots[ei]], ev)
                        K.body("tensor", t_ab)

                        def s_silu2(s):
                            for ei in range(2):
                                a_ps = ps_s[:, 1024:1536] if ei == 0 else ps_ab[:, 0:512]
                                a_rg = ["pss2"] if ei == 0 else ["pab0"]
                                sa = sc_f[:, XQN_O + ei * 512:XQN_O + (ei + 1) * 512]
                                K.rr(s, a_rg + ["xqn"])
                                K.rw(s, ["xqn"])
                                ev = K.tick("se", s.activation(sa[:], a_ps, AF.Silu,
                                                               bias=0.0, scale=1.0))
                                K.cw(["xqn"], ev)
                                K.cr(a_rg, ev)
                        K.body("scalar", s_silu2)

                        def v_stt2(v):
                            for ei in range(2):
                                kind, eidx = exps[ei]
                                b_ps = ps_s[:, 1536:2048] if ei == 0 else ps_ab[:, 512:1024]
                                b_rg = ["pss3"] if ei == 0 else ["pab1"]
                                sa = sc_f[:, XQN_O + ei * 512:XQN_O + (ei + 1) * 512]
                                ho = H1G_O[ei]
                                K.rr(v, ["xqn"] + b_rg)
                                if kind == "r":
                                    K.rr(v, ["gates"])
                                K.rw(v, ["h1g%d" % ei])
                                gsc = (sc_f[:, GATES_O + ti * 8 + eidx:
                                            GATES_O + ti * 8 + eidx + 1]
                                       if kind == "r" else 0.5)
                                ins = v.scalar_tensor_tensor(
                                    sc_bf[:, ho:ho + 512], sa[:], gsc, b_ps,
                                    op0=ALU.mult, op1=ALU.mult)
                                ev = K.tick("ve", ins)
                                K.cw(["h1g%d" % ei], ev)
                                K.cr(["xqn"] + b_rg, ev)
                                if kind == "r":
                                    K.cr(["gates"], ev)
                        K.body("vector", v_stt2)
                    return u1

                def mk_u2(p, ti, slots):
                    def u2():
                        def t_h1(t_eng):
                            K.rr(t_eng, ["h1g0", "h1g1"])
                            K.rw(t_eng, ["ptr"])
                            ins = None
                            for ei in range(2):
                                ho = H1G_O[ei]
                                for c in range(4):
                                    ins = t_eng.transpose(
                                        ps_tr[:, ei * 512 + c * 128:
                                              ei * 512 + (c + 1) * 128],
                                        sc_bf[:, ho + c * 128:ho + (c + 1) * 128],
                                        id_bf[:])
                            ev = K.tick("pe", ins)
                            K.cw(["ptr"], ev)
                            K.cr(["h1g0", "h1g1"], ev)
                        K.body("tensor", t_h1)

                        def v_h1T(v):
                            K.rr(v, ["ptr"])
                            K.rw(v, ["h1gT0", "h1gT1"])
                            v.tensor_copy(
                                sc_bf[:, H1GT_O[0]:H1GT_O[0] + 512], ps_tr[:, 0:512])
                            ev = K.tick("ve", v.tensor_copy(
                                sc_bf[:, H1GT_O[1]:H1GT_O[1] + 512], ps_tr[:, 512:1024]))
                            K.cw(["h1gT0", "h1gT1"], ev)
                            K.cr(["ptr"], ev)
                        K.body("vector", v_h1T)
                    return u2

                def mk_u3(p, ti, slots, t):
                    def u3():
                        def t_w2both(t_eng):
                            for ei in range(2):
                                off = slots[ei] * SLOT
                                hto = H1GT_O[ei]
                                K.rr(t_eng, ["h1gT%d" % ei, "slot%d" % slots[ei]])
                                if ei == 0:
                                    K.rw(t_eng, ["pss0", "pss1"])
                                ins = None
                                for hc in range(4):
                                    lhs = sc_bf[:, hto + hc * 128:hto + (hc + 1) * 128]
                                    for dh in range(2):
                                        ins = t_eng.matmul(
                                            ps_s[:, dh * 512:(dh + 1) * 512], lhs,
                                            big[:, off + 8192 + hc * 1024 + dh * 512:
                                                off + 8192 + hc * 1024 + (dh + 1) * 512],
                                            start=(hc == 0 and ei == 0),
                                            stop=(hc == 3 and ei == 1))
                                ev = K.tick("pe", ins)
                                if ei == 1:
                                    K.cw(["pss0", "pss1"], ev)
                                K.cr(["h1gT%d" % ei, "slot%d" % slots[ei]], ev)
                        K.body("tensor", t_w2both)

                        def v_yadd(v):
                            K.rr(v, ["pss0", "pss1"])
                            K.rw(v, ["res%d" % t])
                            ins = v.tensor_tensor(
                                resid[:, t * D:(t + 1) * D],
                                resid[:, t * D:(t + 1) * D],
                                ps_s[:, 0:1024], op=ALU.add)
                            ev = K.tick("ve", ins)
                            K.cw(["res%d" % t], ev)
                            K.cr(["pss0", "pss1"], ev)
                        K.body("vector", v_yadd)
                    return u3

                def mk_gw(i_str):
                    def gw():
                        def g_w(g):
                            j = i_str + 2
                            if j < len(pair_stream) and j not in attn_barrier:
                                sj0, sj1 = pair_slots(j)
                                (k0, e0), (k1, e1) = PAIRS[pair_stream[j][3]]
                                slot_dma(g, sj0, k0, e0)
                                slot_dma(g, sj1, k1, e1)
                        K.body("gpsimd", g_w)
                    return gw

                U1, U2, U3, GW = [], [], [], {}
                for p in range(len(PAIRS)):
                    i_str = base_i + p
                    slots = pair_slots(i_str)
                    for ti in range(8):
                        U1.append(mk_u1(p, ti, slots))
                        U2.append(mk_u2(p, ti, slots))
                        U3.append(mk_u3(p, ti, slots, t0 + ti))
                    GW[8 * p + 7] = mk_gw(i_str)

                n = len(U1)
                U1[0]()
                U2[0]()
                for k in range(n):
                    if k + 1 < n:
                        U1[k + 1]()
                    U3[k]()
                    if k + 1 < n:
                        U2[k + 1]()
                    if k in GW:
                        GW[k]()
                    if grp == 0 and k == 33:
                        ln_group([8 + j for j in range(8)], EPS_LN,
                                 alt_scratch=True)
                    elif grp == 1 and call == 1 and k == 33:
                        ln_group([j for j in range(8)], EPS_LN,
                                 alt_scratch=True)

        def qkv_half(seq, half):
            t0 = 8 * half
            tiles = [t0 + i for i in range(8)]

            def g_w(g, seq=seq, half=half):
                K.rw(g, ["wwa"])
                ev = K.tick("wa", g.dma_start(
                    big[:, WWA_O:WWA_O + 8192], wqkv_d[0]), 16)
                K.cw(["wwa"], ev)
                K.rw(g, ["wwb"])
                ev = K.tick("wb", g.dma_start(
                    big[:, WWB_O:WWB_O + 8192], wqkv_d[1]), 16)
                K.cw(["wwb"], ev)
                if half == 0:
                    K.rw(g, ["woutsb"])
                    ev = K.tick("wm", g.dma_start(
                        sc_bf[:, WOUT_O:WOUT_O + 8192], wout_d[:]), 16)
                    K.cw(["woutsb"], ev)
            K.body("gpsimd", g_w)

            if half == 1:
                ln_group(tiles, EPS_LN)

            for i, t in enumerate(tiles):
                xnT_tile(seq, 0, half, i, t, "vsb%d" % t, V_O + t * 1024)

            def qk_units(wname, i, t):
                par = i % 2
                qo = QSTAT_O + 8 * par
                qrg = "qstat" if par == 0 else "qstatB"
                if wname == "q":
                    dst = ps_s[:, 0:1024] if par == 0 else ps_s[:, 1024:2048]
                    dst_rg = ["pss0", "pss1"] if par == 0 else ["pss2", "pss3"]
                    wrg, wcol = "wwa", WWA_O
                elif wname == "k":
                    dst = ps_s[:, 0:1024] if par == 0 else ps_s[:, 1024:2048]
                    dst_rg = ["pss0", "pss1"] if par == 0 else ["pss2", "pss3"]
                    wrg, wcol = "wwb", WWB_O
                else:
                    dst = ps_ab if par == 0 else ps_s[:, 0:1024]
                    dst_rg = ["pab0", "pab1"] if par == 0 else ["pss0", "pss1"]
                    wrg, wcol = "wwa", WWA_O

                def u1():
                    def t_mm(t_eng):
                        K.rr(t_eng, ["vsb%d" % t, wrg])
                        K.rw(t_eng, dst_rg)
                        xt = V_O + t * 1024
                        ins = None
                        for c in range(NC):
                            lhs = big[:, xt + c * 128:xt + (c + 1) * 128]
                            for dh in range(2):
                                ins = t_eng.matmul(
                                    dst[:, dh * 512:(dh + 1) * 512], lhs,
                                    big[:, wcol + c * 1024 + dh * 512:
                                        wcol + c * 1024 + (dh + 1) * 512],
                                    start=(c == 0), stop=(c == NC - 1))
                        ev = K.tick("pe", ins)
                        K.cw(dst_rg, ev)
                        K.cr(["vsb%d" % t, wrg], ev)
                    K.body("tensor", t_mm)

                def u2():
                    if wname in ("q", "k"):
                        def v_st(v):
                            qsum = sc_f[:, qo:qo + 1]
                            K.rr(v, dst_rg)
                            K.rw(v, [qrg])
                            ev = K.tick("ve", v.reduce_sum(qsum[:], dst[:],
                                                           axis=AX.X))
                            K.cw([qrg], ev)
                            K.cr(dst_rg, ev)
                        K.body("vector", v_st)

                        def s_sq(s):
                            qsum = sc_f[:, qo:qo + 1]
                            qss = sc_f[:, qo + 1:qo + 2]
                            mn = sc_f[:, qo + 2:qo + 3]
                            m2n = sc_f[:, qo + 3:qo + 4]
                            vr = sc_f[:, qo + 4:qo + 5]
                            sg = sc_f[:, qo + 5:qo + 6]
                            K.rr(s, dst_rg + [qrg, "xqn", "cst"])
                            K.rw(s, ["xqn", qrg])
                            ins = s.activation(sc_f[:, XQN_O:XQN_O + D], dst[:],
                                               AF.Square, accum_out=qss[:])
                            K.cut(s, ins, "se")
                            ins = s.mul(mn[:], qsum[:], cst[:, 2:3])
                            K.cut(s, ins, "se")
                            ins = s.mul(m2n[:], mn[:], mn[:])
                            K.cut(s, ins, "se")
                            ins = s.mul(m2n[:], m2n[:], cst[:, 3:4])
                            K.cut(s, ins, "se")
                            ins = s.add(vr[:], m2n[:], cst[:, 1:2])
                            K.cut(s, ins, "se")
                            ins = s.activation(sg[:], qss[:], AF.Ln,
                                               bias=vr[:], scale=cst[:, 2:3])
                            K.cut(s, ins, "se")
                            rsd = sc_f[:, qo + 6:qo + 7]
                            ev = K.tick("se", s.activation(rsd[:], sg[:], AF.Exp,
                                                           bias=0.0,
                                                           scale=cst[:, 4:5]))
                            K.cw(["xqn", qrg], ev)
                            K.cr(dst_rg, ev)
                        K.body("scalar", s_sq)

                        def v_nm(v):
                            rs = sc_f[:, qo + 6:qo + 7]
                            mn = sc_f[:, qo + 2:qo + 3]
                            K.rr(v, [qrg] + dst_rg + ["xqn"])
                            K.rw(v, ["xqn"])
                            ins = v.tensor_scalar(sc_f[:, XQN_O:XQN_O + D], dst[:],
                                                  mn[:], rs[:],
                                                  op0=ALU.subtract, op1=ALU.mult)
                            ev = K.tick("ve", ins)
                            K.cw(["xqn"], ev)
                            K.cr(dst_rg + [qrg], ev)
                        K.body("vector", v_nm)
                    else:
                        def v_vev(v):
                            K.rr(v, dst_rg)
                            K.rw(v, ["vsb%d" % t])
                            ev = K.tick("ve", v.tensor_copy(
                                big[:, V_O + t * 1024:V_O + (t + 1) * 1024],
                                dst[:]))
                            K.cw(["vsb%d" % t], ev)
                            K.cr(dst_rg, ev)
                        K.body("vector", v_vev)

                def u3():
                    if wname == "v":
                        return
                    scv = qs8 if wname == "q" else ks8
                    for hf in range(2):
                        def t_tr(t_eng, hf=hf):
                            K.rr(t_eng, ["xqn", "gt"])
                            K.rw(t_eng, ["pctx"])
                            ins = None
                            for c in range(4):
                                cc = 4 * hf + c
                                ins = t_eng.transpose(
                                    ps_ctx[:, c * 128:(c + 1) * 128],
                                    sc_f[:, XQN_O + cc * 128:XQN_O + (cc + 1) * 128],
                                    id_f[:])
                            ev = K.tick("pe", ins)
                            K.cw(["pctx"], ev)
                            if hf == 1:
                                K.cr(["xqn"], ev)
                        K.body("tensor", t_tr)

                        def v_qe(v, hf=hf):
                            K.rr(v, ["pctx"])
                            drg = (["r1t%d" % j for j in range(8)]
                                   if wname == "q" else ["knt"])
                            K.rw(v, drg)
                            ins = None
                            for c in range(4):
                                cc = 4 * hf + c
                                if wname == "q":
                                    d_ap = r1[:, cc * 1024 + i * 128:
                                              cc * 1024 + (i + 1) * 128]
                                else:
                                    d_ap = big[:, KNT_O + cc * 2048 + t * 128:
                                               KNT_O + cc * 2048 + (t + 1) * 128]
                                ins = v.tensor_scalar_mul(
                                    d_ap, ps_ctx[:, c * 128:(c + 1) * 128],
                                    scv[:, cc:cc + 1])
                            ev = K.tick("ve", ins)
                            K.cw(drg, ev)
                            K.cr(["pctx"], ev)
                        K.body("vector", v_qe)
                return u1, u2, u3

            for wname in "qkv":
                if wname == "v":
                    def g_wv(g):
                        K.rw(g, ["wwa"])
                        ev = K.tick("wa", g.dma_start(
                            big[:, WWA_O:WWA_O + 8192], wqkv_d[2]), 16)
                        K.cw(["wwa"], ev)
                    K.body("gpsimd", g_wv)

                us = [qk_units(wname, i, t) for i, t in enumerate(tiles)]
                us[0][0]()
                us[0][1]()
                for i in range(len(us)):
                    if i + 1 < len(us):
                        us[i + 1][0]()
                    us[i][2]()
                    if i + 1 < len(us):
                        us[i + 1][1]()

        def attn_qgroup(seq, qg, mid_wout=None):
            nkc = 4 * (qg + 1)
            qts = [4 * qg + j for j in range(4)]
            if qg == 0:
                waves = [[0, 1, 2, 3]]
                ro_of = {0: 0, 1: 512, 2: 1024, 3: 1536}
            elif qg == 1:
                waves = [[0, 1], [2, 3]]
                ro_of = {0: 0, 1: 1024, 2: 0, 3: 1024}
            else:
                waves = [[0], [1], [2], [3]]
                ro_of = {0: 0, 1: 0, 2: 0, 3: 0}

            def att_ap(pk, slot, lo, hi):
                if pk == 0:
                    return big[:, ATT_O + slot * 512 + lo:
                               ATT_O + slot * 512 + hi]
                return sc_bf[:, 1024 + slot * 512 + lo:1024 + slot * 512 + hi]

            def make_tct(h, c, po):
                def mk_pair(kc0):
                    kcs = [kc for kc in (kc0, kc0 + 1) if kc < nkc]
                    vqls = {kc: [ql for ql, qt in enumerate(qts) if qt >= kc]
                            for kc in kcs}
                    pk = (kc0 // 2) % 2
                    args = [("atT%d" % (pk * 2 + kc % 2)) for kc in kcs]

                    def t_at(t_eng, kcs=kcs, vqls=vqls):
                        regs = sorted({"asb%d" % ql for kc in kcs
                                       for ql in vqls[kc]})
                        K.rr(t_eng, regs + ["gt"])
                        K.rw(t_eng, ["ptr"])
                        ins = None
                        for j2, kc in enumerate(kcs):
                            for j, ql in enumerate(vqls[kc]):
                                ins = t_eng.transpose(
                                    ps_tr[:, j2 * 512 + j * 128:
                                          j2 * 512 + (j + 1) * 128],
                                    big[:, ASB_O + ql * 2048 + kc * 128:
                                        ASB_O + ql * 2048 + (kc + 1) * 128],
                                    id_bf[:])
                        ev = K.tick("pe", ins)
                        K.cw(["ptr"], ev)
                        K.cr(regs, ev)

                    def v_ev(v, kcs=kcs, vqls=vqls, pk=pk, args=args):
                        K.rr(v, ["ptr"])
                        K.rw(v, args)
                        ins = None
                        for j2, kc in enumerate(kcs):
                            q0 = vqls[kc][0] * 128
                            n = 512 - q0
                            ins = v.tensor_copy(
                                att_ap(pk, kc % 2, q0, 512),
                                ps_tr[:, j2 * 512:j2 * 512 + n])
                        ev = K.tick("ve", ins)
                        K.cw(args, ev)
                        K.cr(["ptr"], ev)

                    def t_cx(t_eng, kcs=kcs, vqls=vqls, pk=pk, c=c, po=po,
                             h=h):
                        for kc in kcs:
                            q0 = vqls[kc][0] * 128
                            n = 512 - q0
                            rg = "atT%d" % (pk * 2 + kc % 2)
                            K.rr(t_eng, [rg, "vsb%d" % kc])
                            if kc == 0:
                                K.rw(t_eng, ["pctx"])
                            ins = t_eng.matmul(
                                ps_ctx[po:po + 64, q0:q0 + n],
                                big[:, V_O + kc * 1024 + h * 64:
                                    V_O + kc * 1024 + h * 64 + 64],
                                att_ap(pk, kc % 2, q0, 512),
                                start=(kc == 0), stop=(kc == nkc - 1),
                                tile_position=(0, po))
                            ev = K.tick("pe", ins)
                            if kc == nkc - 1:
                                K.cw(["pctx"], ev)
                            K.cr([rg, "vsb%d" % kc], ev)
                    return t_at, v_ev, t_cx

                def emit_tct():
                    pairs = [mk_pair(kc0) for kc0 in range(0, nkc, 2)]
                    K.body("tensor", pairs[0][0])
                    K.body("vector", pairs[0][1])
                    for k in range(1, len(pairs)):
                        K.body("tensor", pairs[k][0])
                        K.body("vector", pairs[k][1])
                        K.body("tensor", pairs[k - 1][2])
                    K.body("tensor", pairs[-1][2])

                    def v_cx(v, c=c, po=po):
                        K.rr(v, ["pctx"])
                        K.rw(v, ["ctxT"])
                        ev = K.tick("ve", v.tensor_copy(
                            big[po:po + 64,
                                CTXT_O + c * 512:CTXT_O + (c + 1) * 512],
                            ps_ctx[po:po + 64, :]))
                        K.cw(["ctxT"], ev)
                        K.cr(["pctx"], ev)
                    K.body("vector", v_cx)
                return emit_tct

            BANKS = ["pss0", "pss1", "pss2", "pss3", "pab0", "pab1"]

            nkg0 = (128 * (qts[0] + 1) + 511) // 512
            step = {1: 1, 2: 2, 3: 3, 4: 2}[nkg0]
            mod = {1: 6, 2: 6, 3: 6, 4: 4}[nkg0]
            stream = []
            cur = 0
            prevb = set()
            for h in range(NH):
                for wi, wave in enumerate(waves):
                    alloc = {}
                    plan = {}
                    for ql in wave:
                        keys = 128 * (qts[ql] + 1)
                        nkg = (keys + 511) // 512
                        assert nkg == nkg0
                        bks = list(range(cur, cur + nkg))
                        alloc[ql] = bks
                        ejs = [j for j in range(nkg) if bks[j] not in prevb]
                        ljs = [j for j in range(nkg) if bks[j] in prevb]
                        assert ejs == list(range(min(ejs), max(ejs) + 1)) \
                            if ejs else True
                        plan[ql] = (ejs, ljs)
                        cur = (cur + step) % mod
                    prevb = {b for ql in wave for b in alloc[ql]}
                    stream.append((h, wi, wave, alloc, plan))

            def region_of(alloc, ql, js, keys):
                lo = alloc[ql][0] * 512 + js[0] * 512
                hi = alloc[ql][0] * 512 + min(keys, (js[-1] + 1) * 512)
                return lo, hi

            def emit_sc(item, which):
                h, wi, wave, alloc, plan = item
                c = h // 2
                po = 64 * (h % 2)
                part = [(ql, j) for ql in wave for j in plan[ql][which]]
                if not part:
                    return

                def t_sc(t_eng, part=part, alloc=alloc, c=c, po=po):
                    by_ql = {}
                    for ql, j in part:
                        by_ql.setdefault(ql, []).append(j)
                    for ql, js in by_ql.items():
                        qt = qts[ql]
                        keys = 128 * (qt + 1)
                        nkg = len(alloc[ql])
                        base = alloc[ql][0]
                        rgs = [BANKS[alloc[ql][j]] for j in js]
                        K.rr(t_eng, ["r1t%d" % c, "knt"])
                        K.rw(t_eng, rgs)
                        lhs = r1[po:po + 64, c * 1024 + (qt % 8) * 128:
                                 c * 1024 + (qt % 8 + 1) * 128]
                        ins = None
                        for j in js:
                            n = min(512, keys - j * 512)
                            o = base * 512 + j * 512
                            last = (j == nkg - 1)
                            ins = t_eng.matmul(
                                ps_all[:, o:o + n], lhs,
                                big[po:po + 64,
                                    KNT_O + c * 2048 + j * 512:
                                    KNT_O + c * 2048 + j * 512 + n],
                                start=True, stop=not last)
                            if last:
                                ins = t_eng.matmul(
                                    ps_all[:, o + n - 128:o + n],
                                    id_bf[:], mask_bf[:],
                                    start=False, stop=True)
                        ev = K.tick("pe", ins)
                        K.cw(rgs, ev)
                        K.cr(["r1t%d" % c, "knt"], ev)
                K.body("tensor", t_sc)

            def emit_mxa(item):
                h, wi, wave, alloc, plan = item
                if not any(plan[ql][0] for ql in wave):
                    return

                def v_mxa(v, wave=wave, alloc=alloc, plan=plan):
                    for ql in wave:
                        ejs, ljs = plan[ql]
                        if not ejs:
                            continue
                        qt = qts[ql]
                        keys = 128 * (qt + 1)
                        lo, hi = region_of(alloc, ql, ejs, keys)
                        rgs = [BANKS[alloc[ql][j]] for j in ejs]
                        K.rr(v, rgs)
                        if ljs:
                            mx = sc_f[:, MXS_O + ql * 4:MXS_O + ql * 4 + 1]
                            K.rw(v, ["mxs%d" % ql])
                            ins = v.reduce_max(mx[:], ps_all[:, lo:hi],
                                               axis=AX.X, negate=False)
                            ev = K.tick("ve", ins)
                            K.cw(["mxs%d" % ql], ev)
                        else:
                            nm = sc_f[:, SMX_O + ql:SMX_O + ql + 1]
                            K.rw(v, ["negm%d" % ql])
                            ins = v.reduce_max(nm[:], ps_all[:, lo:hi],
                                               axis=AX.X, negate=True)
                            ev = K.tick("ve", ins)
                            K.cw(["negm%d" % ql], ev)
                        K.cr(rgs, ev)
                K.body("vector", v_mxa)

            def emit_mxb(item):
                h, wi, wave, alloc, plan = item
                if not any(plan[ql][1] for ql in wave):
                    return

                def v_mxb(v, wave=wave, alloc=alloc, plan=plan):
                    for ql in wave:
                        ejs, ljs = plan[ql]
                        if not ljs:
                            continue
                        qt = qts[ql]
                        keys = 128 * (qt + 1)
                        lo, hi = region_of(alloc, ql, ljs, keys)
                        rgs = [BANKS[alloc[ql][j]] for j in ljs]
                        nm = sc_f[:, SMX_O + ql:SMX_O + ql + 1]
                        K.rr(v, rgs)
                        K.rw(v, ["negm%d" % ql, "mxs%d" % ql])
                        if not ejs:
                            ins = v.reduce_max(nm[:], ps_all[:, lo:hi],
                                               axis=AX.X, negate=True)
                        else:
                            mx1 = sc_f[:, MXS_O + ql * 4 + 1:
                                       MXS_O + ql * 4 + 2]
                            ins = v.reduce_max(mx1[:], ps_all[:, lo:hi],
                                               axis=AX.X, negate=False)
                            K.cut(v, ins, "ve")
                            ins = v.reduce_max(
                                nm[:], sc_f[:, MXS_O + ql * 4:
                                            MXS_O + ql * 4 + 2],
                                axis=AX.X, negate=True)
                        ev = K.tick("ve", ins)
                        K.cw(["negm%d" % ql, "mxs%d" % ql], ev)
                        K.cr(rgs, ev)
                K.body("vector", v_mxb)

            prev_tct = None
            for i, item in enumerate(stream):
                h, wi, wave, alloc, plan = item
                if i == 0:
                    emit_sc(item, 0)
                    emit_mxa(item)
                if i + 1 < len(stream):
                    emit_sc(stream[i + 1], 0)
                emit_mxb(item)

                if wi == 0 and prev_tct is not None:
                    prev_tct()
                    prev_tct = None

                def s_ex(s, wave=wave, alloc=alloc):
                    for ql in wave:
                        qt = qts[ql]
                        keys = 128 * (qt + 1)
                        ro = alloc[ql][0] * 512
                        rgs = [BANKS[b] for b in alloc[ql]]
                        nm = sc_f[:, SMX_O + ql:SMX_O + ql + 1]
                        ls = sc_f[:, SMX_O + 4 + ql:SMX_O + 5 + ql]
                        K.rr(s, rgs + ["negm%d" % ql])
                        K.rw(s, ["asb%d" % ql, "lsum%d" % ql])
                        ins = s.activation(
                            big[:, ASB_O + ql * 2048:ASB_O + ql * 2048 + keys],
                            ps_all[:, ro:ro + keys], AF.Exp,
                            bias=nm[:], scale=1.0, accum_out=ls[:])
                        ev = K.tick("se", ins)
                        K.cw(["asb%d" % ql, "lsum%d" % ql], ev)
                        K.cr(rgs + ["negm%d" % ql], ev)
                K.body("scalar", s_ex)

                if i + 1 < len(stream):
                    emit_sc(stream[i + 1], 1)
                    emit_mxa(stream[i + 1])

                def v_sc2(v, wave=wave):
                    ins = None
                    for ql in wave:
                        ls = sc_f[:, SMX_O + 4 + ql:SMX_O + 5 + ql]
                        rl = sc_f[:, SMX_O + 8 + ql:SMX_O + 9 + ql]
                        K.rr(v, ["lsum%d" % ql])
                        K.rw(v, ["rl%d" % ql])
                        ins = v.reciprocal(rl[:], ls[:])
                    K.cut(v, ins, "ve")
                    for ql in wave:
                        qt = qts[ql]
                        keys = 128 * (qt + 1)
                        rl = sc_f[:, SMX_O + 8 + ql:SMX_O + 9 + ql]
                        K.rr(v, ["rl%d" % ql, "asb%d" % ql])
                        K.rw(v, ["asb%d" % ql])
                        ins = v.tensor_scalar_mul(
                            big[:, ASB_O + ql * 2048:ASB_O + ql * 2048 + keys],
                            big[:, ASB_O + ql * 2048:ASB_O + ql * 2048 + keys],
                            rl[:])
                        ev = K.tick("ve", ins)
                        K.cw(["asb%d" % ql, "rl%d" % ql], ev)
                        K.cr(["lsum%d" % ql], ev)
                K.body("vector", v_sc2)

                if wi == len(waves) - 1:
                    assert prev_tct is None
                    prev_tct = make_tct(h, h // 2, 64 * (h % 2))
            prev_tct()

            for ql, qt in enumerate(qts):
                def t_wo(t_eng, ql=ql):
                    K.rr(t_eng, ["ctxT", "woutsb"])
                    K.rw(t_eng, ["pab0", "pab1"])
                    ins = None
                    for c2 in range(NC):
                        lhs = big[:, CTXT_O + c2 * 512 + ql * 128:
                                  CTXT_O + c2 * 512 + (ql + 1) * 128]
                        for dh in range(2):
                            ins = t_eng.matmul(
                                ps_ab[:, dh * 512:(dh + 1) * 512], lhs,
                                sc_bf[:, WOUT_O + c2 * 1024 + dh * 512:
                                      WOUT_O + c2 * 1024 + (dh + 1) * 512],
                                start=(c2 == 0), stop=(c2 == NC - 1))
                    ev = K.tick("pe", ins)
                    K.cw(["pab0", "pab1"], ev)
                    K.cr(["ctxT", "woutsb"], ev)
                K.body("tensor", t_wo)

                def v_wa(v, qt=qt):
                    K.rr(v, ["pab0", "pab1"])
                    K.rw(v, ["res%d" % qt])
                    ins = v.tensor_tensor(resid[:, qt * D:(qt + 1) * D],
                                          resid[:, qt * D:(qt + 1) * D],
                                          ps_ab, op=ALU.add)
                    ev = K.tick("ve", ins)
                    K.cw(["res%d" % qt], ev)
                    K.cr(["pab0", "pab1"], ev)
                K.body("vector", v_wa)

                if ql == 1 and mid_wout is not None:
                    mid_wout()

        K.block = None

        def emit(block):
            K.block = block
            prologue()
            for seq in range(NSEQ):
                def sx(g, seq=seq):
                    K.rw(g, ["res%d" % t for t in range(NT)])
                    ev = K.tick("sq_x", g.dma_start(
                        resid[:].rearrange("p (t f) -> p t f", t=NT),
                        x_d[seq].rearrange("(t p) f -> p t f", p=128)), 16)
                    K.cw(["res%d" % t for t in range(NT)], ev)
                K.body("sync", sx)

                ffn2(seq, 1)
                if phases >= 2:
                    for half in range(2):
                        qkv_half(seq, half)
                        for qgl in range(2):
                            qg = 2 * half + qgl
                            attn_qgroup(seq, qg)
                    def g_post(g, seq=seq):
                        for j in sorted(attn_barrier):
                            if attn_barrier[j] == seq:
                                sj0, sj1 = pair_slots(j)
                                (k0, e0), (k1, e1) = PAIRS[pair_stream[j][3]]
                                slot_dma(g, sj0, k0, e0)
                                slot_dma(g, sj1, k1, e1)
                    K.body("gpsimd", g_post)
                if phases >= 3:
                    ffn2(seq, 2)

                def sy(g, seq=seq):
                    for t in range(NT):
                        K.rr(g, ["res%d" % t])
                    ev = K.tick("sq_x", g.dma_start(
                        y_d[seq].rearrange("(t p) f -> p t f", p=128),
                        resid[:].rearrange("p (t f) -> p t f", t=NT)), 16)
                    K.cr(["res%d" % t for t in range(NT)], ev)
                    if seq == NSEQ - 1:
                        K.final_store = ev
                K.body("sync", sy)

            def g_end(g):
                K.wait(g, K.final_store)
            K.body("gpsimd", g_end)

        with nc.Block() as blk:
            emit(blk)

    return nc


def kernel(**inputs):
    xs = np.ascontiguousarray(np.asarray(inputs["L0_attn_x"], dtype=np.float32))
    wqkv = np.ascontiguousarray(np.asarray(inputs["L0_attn_W_qkv"], dtype=np.float32))
    qs = np.ascontiguousarray(np.asarray(inputs["L0_attn_q_scale"], dtype=np.float32).reshape(NC, 128).T)
    ks = np.ascontiguousarray(np.asarray(inputs["L0_attn_k_scale"], dtype=np.float32).reshape(NC, 128).T)
    wout = np.ascontiguousarray(np.asarray(inputs["L0_attn_W_out"], dtype=np.float32))
    rw = np.ascontiguousarray(np.asarray(inputs["L0_ffn_router_w"], dtype=np.float32))
    rb = np.ascontiguousarray(np.asarray(inputs["L0_ffn_router_b"], dtype=np.float32).reshape(1, E_R))
    rw1 = np.ascontiguousarray(np.asarray(inputs["L0_ffn_routed_w1"], dtype=np.float32))
    rw3 = np.ascontiguousarray(np.asarray(inputs["L0_ffn_routed_w3"], dtype=np.float32))
    rw2 = np.ascontiguousarray(np.asarray(inputs["L0_ffn_routed_w2"], dtype=np.float32))
    sw1 = np.ascontiguousarray(np.asarray(inputs["L0_ffn_shared_w1"], dtype=np.float32))
    sw3 = np.ascontiguousarray(np.asarray(inputs["L0_ffn_shared_w3"], dtype=np.float32))
    sw2 = np.ascontiguousarray(np.asarray(inputs["L0_ffn_shared_w2"], dtype=np.float32))

    def pack_dh(w):
        X = w.shape[1]
        return w.reshape(NC, 128, X).transpose(1, 0, 2).reshape(128, NC * X)

    def pack_h(w):
        X = w.shape[1]
        return w.reshape(4, 128, X).transpose(1, 0, 2).reshape(128, 4 * X)

    ffnw = np.stack([np.concatenate(
        [pack_dh(w1[e]), pack_dh(w3[e]), pack_h(w2[e])], axis=1)
        for (w1, w3, w2, es) in ((rw1, rw3, rw2, range(E_R)), (sw1, sw3, sw2, range(2)))
        for e in es])
    ffnw = np.ascontiguousarray(ffnw)
    wqkvl = np.ascontiguousarray(np.stack(
        [pack_dh(wqkv[:, m * D:(m + 1) * D]) for m in range(3)]))
    woutl = np.ascontiguousarray(pack_dh(wout))
    rwl = np.ascontiguousarray(pack_dh(rw))

    nc = build_program(phases=3)
    in_maps = []
    for core in range(8):
        in_maps.append({
            "x": np.ascontiguousarray(xs[2 * core:2 * core + 2]),
            "wqkvl": wqkvl, "q_scale": qs, "k_scale": ks, "woutl": woutl,
            "routerwl": rwl, "router_b": rb, "ffnw": ffnw,
        })
    res = run_bass_kernel_spmd(nc, in_maps, core_ids=list(range(8)))
    out = np.concatenate([r["y"] for r in res.results], axis=0)
    return out.reshape(16, S, D)


# version 30
# speedup vs baseline: 1.0485x; 1.0174x over previous
import contextlib
import numpy as np

import concourse.bass as bass
import concourse.mybir as mybir
from concourse.bass_utils import run_bass_kernel_spmd

F32 = mybir.dt.float32
BF16 = mybir.dt.bfloat16
AF = mybir.ActivationFunctionType
ALU = mybir.AluOpType
AX = mybir.AxisListType

D = 1024
H = 512
NH = 16
HD = 64
S = 2048
NT = 16
NSEQ = 2
NC = 8
E_R = 8
EPS_LN = 1e-5
EPS_QK = 1e-6
NEG_BIG = -3.0e38

PAIRS = [(("r", 0), ("r", 1)), (("r", 2), ("r", 3)), (("r", 4), ("r", 5)),
         (("r", 6), ("r", 7)), (("s", 0), ("s", 1))]

SLOT = 12288
KNT_O = 0
V_O = 16384
SUB_O = 32768
ASB_O = SUB_O
ATT_O = SUB_O + 8192
CTXT_O = SUB_O + 9216
WWA_O = SUB_O
WWB_O = SUB_O + 8192

XN2_O = 0
H1G_O = [3072, 3584]
H1GT_O = [4096, 4608]
WOUT_O = 3072
SC_BF_COLS = 11264

XQN_O = 0
MASK_O = 1024
GATES_O = 1152
STAT_O = 1216
LG_O = 1280
NM_O = 1344
NT2_O = 1352
XT_O = 1360
GT_O = 1424
SMX_O = 1456
QSTAT_O = 1472
MXS_O = 1488
LSS_O = 1504
SC_F_COLS = 1536


class KB:

    def __init__(self, semh, owner):
        self.semh = semh
        self.owner = owner
        self.cnt = {n: 0 for n in semh}
        self.lastw = {}
        self.cur = None
        self.block = None
        self.regions = {}
        self.conf = {}
        self.rw_ = {}
        self.rr_ = {}

    def region(self, name, tensor_key, lo, hi):
        assert name not in self.regions
        self.regions[name] = (tensor_key, lo, hi)
        self.conf[name] = []
        for o, (tk, l2, h2) in self.regions.items():
            if tk == tensor_key and not (hi <= l2 or h2 <= lo):
                self.conf[name].append(o)
                if o != name:
                    self.conf[o].append(name)
        self.rw_[name] = None
        self.rr_[name] = []

    def body(self, engname, fn):
        self.cur = engname
        getattr(self.block, engname)(fn)
        self.cur = None

    def tick(self, sem, instr, by=1):
        self.cnt[sem] += by
        instr.then_inc(self.semh[sem], by)
        return (sem, self.cnt[sem])

    def wait(self, eng, ev):
        if ev is None:
            return
        sem, val = ev
        lk = (self.cur, sem)
        if self.lastw.get(lk, 0) < val:
            self.lastw[lk] = val
            eng.wait_ge(self.semh[sem], val)

    def cut(self, eng, instr, sem):
        ev = self.tick(sem, instr)
        self.lastw[(self.cur, sem)] = ev[1]
        eng.wait_ge(self.semh[sem], ev[1])
        return ev

    def rw(self, eng, regs):
        for r in regs:
            for c in self.conf[r]:
                self.wait(eng, self.rw_[c])
                for ev in self.rr_[c]:
                    self.wait(eng, ev)

    def cw(self, regs, ev):
        for r in regs:
            self.rw_[r] = ev
            self.rr_[r] = []

    def rr(self, eng, regs):
        for r in regs:
            for c in self.conf[r]:
                self.wait(eng, self.rw_[c])

    def cr(self, regs, ev):
        for r in regs:
            if ev is not None and ev not in self.rr_[r]:
                self.rr_[r].append(ev)


def _psbanks(lo, hi):
    return ["pss%d" % b for b in range(lo // 512, (hi + 511) // 512)]


def build_program(phases=3, debug=False):
    nc = bass.Bass("TRN2", target_bir_lowering=False)

    x_d = nc.dram_tensor("x", [NSEQ, S, D], F32, kind="ExternalInput")
    wqkv_d = nc.dram_tensor("wqkvl", [3, 128, 8192], F32, kind="ExternalInput")
    qs_d = nc.dram_tensor("q_scale", [128, NC], F32, kind="ExternalInput")
    ks_d = nc.dram_tensor("k_scale", [128, NC], F32, kind="ExternalInput")
    wout_d = nc.dram_tensor("woutl", [128, 8192], F32, kind="ExternalInput")
    rw_d = nc.dram_tensor("routerwl", [128, NC * E_R], F32, kind="ExternalInput")
    rb_d = nc.dram_tensor("router_b", [1, E_R], F32, kind="ExternalInput")
    ffnw_d = nc.dram_tensor("ffnw", [10, 128, SLOT], F32, kind="ExternalInput")
    y_d = nc.dram_tensor("y", [NSEQ, S, D], F32, kind="ExternalOutput")
    dbg = {}
    if debug:
        dbg["smx"] = nc.dram_tensor("dbg_smx", [128, 16], F32, kind="ExternalOutput")
        dbg["asb"] = nc.dram_tensor("dbg_asb", [128, 2048], BF16, kind="ExternalOutput")
        dbg["ctxT"] = nc.dram_tensor("dbg_ctxT", [128, 4096], BF16, kind="ExternalOutput")
        dbg["qnt"] = nc.dram_tensor("dbg_qnt", [128, 8192], BF16, kind="ExternalOutput")
        dbg["knt"] = nc.dram_tensor("dbg_knt", [128, 16384], BF16, kind="ExternalOutput")
        dbg["vsb"] = nc.dram_tensor("dbg_vsb", [128, 16384], BF16, kind="ExternalOutput")

    semnames = ["sq_x", "w0", "w1", "w2", "w3", "wa", "wb", "wm",
                "gp", "pe", "ve", "se"]
    owner = {"gp": "gpsimd", "pe": "tensor", "ve": "vector", "se": "scalar"}

    with contextlib.ExitStack() as st:
        resid = st.enter_context(nc.sbuf_tensor("resid", [128, NT * D], F32))
        r1 = st.enter_context(nc.sbuf_tensor("r1", [128, 8192], BF16))
        big = st.enter_context(nc.sbuf_tensor("big", [128, 4 * SLOT], BF16))
        sc_bf = st.enter_context(nc.sbuf_tensor("sc_bf", [128, SC_BF_COLS], BF16))
        sc_f = st.enter_context(nc.sbuf_tensor("sc_f", [128, SC_F_COLS], F32))
        id_bf = st.enter_context(nc.sbuf_tensor("id_bf", [128, 128], BF16))
        mask_bf = st.enter_context(nc.sbuf_tensor("mask_bf", [128, 128], BF16))
        id_f = st.enter_context(nc.sbuf_tensor("id_f", [128, 128], F32))
        iop = st.enter_context(nc.sbuf_tensor("iop", [128, 1], F32))
        ioc = st.enter_context(nc.sbuf_tensor("ioc", [128, 128], F32))
        ones1 = st.enter_context(nc.sbuf_tensor("ones1", [1, 128], F32))
        rb1 = st.enter_context(nc.sbuf_tensor("rb1", [1, E_R], F32))
        rbb = st.enter_context(nc.sbuf_tensor("rbb", [128, E_R], F32))
        qs8 = st.enter_context(nc.sbuf_tensor("qs8", [128, NC], F32))
        ks8 = st.enter_context(nc.sbuf_tensor("ks8", [128, NC], F32))
        rwsb = st.enter_context(nc.sbuf_tensor("rwsb", [128, NC * E_R], BF16))
        cst = st.enter_context(nc.sbuf_tensor("cst", [128, 6], F32))

        ps_all = st.enter_context(nc.psum_tensor("ps_all", [128, 3072], F32))
        ps_s = ps_all[:, 0:2048]
        ps_ab = ps_all[:, 2048:3072]
        ps_tr = st.enter_context(nc.psum_tensor("ps_tr", [128, 1024], BF16))
        ps_ctx = st.enter_context(nc.psum_tensor("ps_ctx", [128, 512], F32))

        semh = {n: st.enter_context(nc.semaphore(n)) for n in semnames}
        K = KB(semh, owner)

        for t in range(NT):
            K.region("res%d" % t, "resid", t * D, (t + 1) * D)
        for i in range(8):
            K.region("r1t%d" % i, "r1", i * 1024, (i + 1) * 1024)
        for s in range(4):
            K.region("slot%d" % s, "big", s * SLOT, (s + 1) * SLOT)
        K.region("knt", "big", KNT_O, KNT_O + 16384)
        for t in range(NT):
            K.region("vsb%d" % t, "big", V_O + t * 1024, V_O + (t + 1) * 1024)
        for q in range(4):
            K.region("asb%d" % q, "big", ASB_O + q * 2048, ASB_O + (q + 1) * 2048)
        K.region("atT0", "big", ATT_O, ATT_O + 512)
        K.region("atT1", "big", ATT_O + 512, ATT_O + 1024)
        K.region("atT2", "sc_bf", 1024, 1536)
        K.region("atT3", "sc_bf", 1536, 2048)
        K.region("ctxT", "big", CTXT_O, CTXT_O + 4096)
        K.region("wwa", "big", WWA_O, WWA_O + 8192)
        K.region("wwb", "big", WWB_O, WWB_O + 8192)
        K.region("xn2", "sc_bf", XN2_O, XN2_O + 1024)
        K.region("h1g0", "sc_bf", H1G_O[0], H1G_O[0] + 512)
        K.region("h1g1", "sc_bf", H1G_O[1], H1G_O[1] + 512)
        K.region("h1gT0", "sc_bf", H1GT_O[0], H1GT_O[0] + 512)
        K.region("h1gT1", "sc_bf", H1GT_O[1], H1GT_O[1] + 512)
        K.region("woutsb", "sc_bf", WOUT_O, WOUT_O + 8192)
        K.region("xqn", "sc_f", XQN_O, XQN_O + 1024)
        K.region("gates", "sc_f", GATES_O, GATES_O + 64)
        K.region("lnstat", "sc_f", STAT_O, STAT_O + 64)
        for t in range(8):
            K.region("lg%d" % t, "sc_f", LG_O + t * 8, LG_O + (t + 1) * 8)
            K.region("nm%d" % t, "sc_f", NM_O + t, NM_O + t + 1)
            K.region("nt%d" % t, "sc_f", NT2_O + t, NT2_O + t + 1)
            K.region("xt%d" % t, "sc_f", XT_O + t * 8, XT_O + (t + 1) * 8)
        K.region("gt", "sc_f", GT_O, GT_O + 32)
        for q in range(4):
            K.region("negm%d" % q, "sc_f", SMX_O + q, SMX_O + q + 1)
            K.region("lsum%d" % q, "sc_f", SMX_O + 4 + q, SMX_O + 5 + q)
            K.region("rl%d" % q, "sc_f", SMX_O + 8 + q, SMX_O + 9 + q)
        K.region("qstat", "sc_f", QSTAT_O, QSTAT_O + 8)
        K.region("qstatB", "sc_f", QSTAT_O + 8, QSTAT_O + 16)
        K.region("qstatC", "sc_f", 1520, 1528)
        K.region("qstatD", "sc_f", 1528, 1536)
        for q in range(4):
            K.region("mxs%d" % q, "sc_f", MXS_O + q * 4, MXS_O + (q + 1) * 4)
            K.region("lss%d" % q, "sc_f", LSS_O + q * 4, LSS_O + (q + 1) * 4)
        K.region("cst", "cst", 0, 6)
        for b in range(4):
            K.region("pss%d" % b, "psall", b * 512, (b + 1) * 512)
        K.region("pab0", "psall", 2048, 2560)
        K.region("pab1", "psall", 2560, 3072)
        K.region("ptr", "ps_tr", 0, 1024)
        K.region("pctx", "ps_ctx", 0, 512)

        pair_stream = []
        for seq in range(NSEQ):
            for call in (1, 2):
                if call == 2 and phases < 3:
                    continue
                for grp in range(2):
                    for p in range(len(PAIRS)):
                        pair_stream.append((seq, call, grp, p))
        attn_barrier = {}
        if phases >= 2:
            for seq in range(NSEQ):
                for j, (s2, c2, _, _) in enumerate(pair_stream):
                    if s2 > seq or (s2 == seq and c2 == 2):
                        attn_barrier[j] = seq
                        if j + 1 < len(pair_stream):
                            attn_barrier[j + 1] = seq
                        break

        def pair_slots(i):
            return ((2 * i) % 4, (2 * i + 1) % 4)

        def slot_dma(g, si, kind, ei):
            gid = ei if kind == "r" else E_R + ei
            sem = "w%d" % si
            off = si * SLOT
            K.rw(g, ["slot%d" % si])
            ev = K.tick(sem, g.dma_start(
                big[:, off:off + SLOT], ffnw_d[gid]), 16)
            K.cw(["slot%d" % si], ev)

        def prologue():
            def g_pro(g):
                g.iota(ioc[:], [[1, 128]], channel_multiplier=0,
                       allow_small_or_imprecise_dtypes=True)
                g.iota(iop[:], [[1, 1]], channel_multiplier=1,
                       allow_small_or_imprecise_dtypes=True)
                g.memset(ones1[:], 1.0)
                g.memset(cst[:, 0:1], EPS_LN)
                g.memset(cst[:, 1:2], EPS_QK)
                g.memset(cst[:, 2:3], 1.0 / D)
                g.memset(cst[:, 3:4], -1.0)
                g.memset(cst[:, 5:6], -1.0 / D)
                ev = K.tick("gp", g.memset(cst[:, 4:5], -0.5))
                K.cw(["gt", "cst"], ev)
                K.tick("wm", g.dma_start(rb1[:], rb_d[:]), 16)
                K.tick("wm", g.dma_start(qs8[:], qs_d[:]), 16)
                K.tick("wm", g.dma_start(ks8[:], ks_d[:]), 16)
                ev = K.tick("wm", g.dma_start(rwsb[:], rw_d[:]), 16)
                K.cw(["qstat"], ev)
                for i in (0, 1):
                    s0, s1 = pair_slots(i)
                    (k0, e0), (k1, e1) = PAIRS[pair_stream[i][3]]
                    slot_dma(g, s0, k0, e0)
                    slot_dma(g, s1, k1, e1)
            K.body("gpsimd", g_pro)

            def v_pro(v):
                K.wait(v, K.rw_["gt"])
                K.wait(v, K.rw_["qstat"])
                ins = v.tensor_scalar(id_f[:], ioc[:], iop[:], None, op0=ALU.is_equal)
                K.cut(v, ins, "ve")
                v.tensor_copy(id_bf[:], id_f[:])
                ins = v.tensor_scalar(sc_f[:, MASK_O:MASK_O + 128], ioc[:], iop[:],
                                      NEG_BIG, op0=ALU.is_gt, op1=ALU.mult)
                K.cut(v, ins, "ve")
                v.tensor_copy(mask_bf[:], sc_f[:, MASK_O:MASK_O + 128])
                ev = K.tick("ve", v.tensor_scalar_mul(qs8[:], qs8[:], 0.125))
                K.cw(["gt"], ev)
            K.body("vector", v_pro)

            def t_pro(t):
                K.wait(t, K.rw_["gt"])
                K.rw(t, ["pab0"])
                ev = K.tick("pe", t.matmul(ps_ab[:, 0:E_R], ones1[:], rb1[:],
                                           start=True, stop=True))
                K.cw(["pab0"], ev)
            K.body("tensor", t_pro)

            def v_pro2(v):
                K.rr(v, ["pab0"])
                ev = K.tick("ve", v.tensor_copy(rbb[:], ps_ab[:, 0:E_R]))
                K.cr(["pab0"], ev)
                K.cw(["qstat"], ev)
            K.body("vector", v_pro2)

        def ln_group(tiles, eps, alt_scratch=False):
            res_regs = ["res%d" % t for t in tiles]

            def v_sum(v):
                K.rr(v, res_regs)
                K.rw(v, ["lnstat"])
                ins = None
                for i, t in enumerate(tiles):
                    ins = v.reduce_sum(sc_f[:, STAT_O + i:STAT_O + i + 1],
                                       resid[:, t * D:(t + 1) * D], axis=AX.X)
                ev = K.tick("ve", ins)
                K.cw(["lnstat"], ev)
                K.cr(res_regs, ev)
            K.body("vector", v_sum)

            def s_sq(s):
                scr = ["atT2", "atT3"] if alt_scratch else ["xqn"]
                out = (sc_bf[:, 1024:1024 + D] if alt_scratch
                       else sc_f[:, XQN_O:XQN_O + D])
                K.rr(s, res_regs + ["lnstat"])
                K.rw(s, scr + ["lnstat"])
                ins = None
                for i, t in enumerate(tiles):
                    ins = s.activation(out, resid[:, t * D:(t + 1) * D],
                                       AF.Square,
                                       accum_out=sc_f[:, STAT_O + 8 + i:STAT_O + 9 + i])
                ev = K.tick("se", ins)
                K.cw(scr + ["lnstat"], ev)
                K.cr(res_regs, ev)
            K.body("scalar", s_sq)

            def v_fin(v):
                mean8 = sc_f[:, STAT_O:STAT_O + 8]
                var8 = sc_f[:, STAT_O + 8:STAT_O + 16]
                m2 = sc_f[:, STAT_O + 32:STAT_O + 40]
                K.rr(v, ["lnstat"])
                K.rw(v, ["lnstat"])
                ins = v.tensor_scalar_mul(mean8[:], mean8[:], 1.0 / D)
                K.cut(v, ins, "ve")
                v.tensor_tensor(m2[:], mean8[:], mean8[:], op=ALU.mult)
                ins = v.tensor_scalar_mul(var8[:], var8[:], 1.0 / D)
                K.cut(v, ins, "ve")
                ev = K.tick("ve", v.tensor_tensor(var8[:], var8[:], m2[:],
                                                  op=ALU.subtract))
                K.cw(["lnstat"], ev)
            K.body("vector", v_fin)

            def s_rt(s):
                sig8 = sc_f[:, STAT_O + 16:STAT_O + 24]
                var8 = sc_f[:, STAT_O + 8:STAT_O + 16]
                rstd8 = sc_f[:, STAT_O + 24:STAT_O + 32]
                K.rr(s, ["lnstat", "cst"])
                K.rw(s, ["lnstat"])
                epsap = cst[:, 0:1] if eps == EPS_LN else cst[:, 1:2]
                ins = s.activation(sig8[:], var8[:], AF.Ln, bias=epsap, scale=1.0)
                K.cut(s, ins, "se")
                ev = K.tick("se", s.activation(rstd8[:], sig8[:], AF.Exp,
                                               bias=0.0, scale=cst[:, 4:5]))
                K.cw(["lnstat"], ev)
            K.body("scalar", s_rt)


        def xnT_tile(seq, call, grp, i, t, dest, dest_off):
            def v_n(v):
                rstd8 = sc_f[:, STAT_O + 24:STAT_O + 32]
                sig8 = sc_f[:, STAT_O + 16:STAT_O + 24]
                mean8 = sc_f[:, STAT_O:STAT_O + 8]
                K.rr(v, ["lnstat"])
                K.rw(v, ["xn2"])
                K.rr(v, ["res%d" % t])
                ins = v.tensor_scalar(
                    sc_bf[:, XN2_O:XN2_O + D], resid[:, t * D:(t + 1) * D],
                    mean8[:, i:i + 1], rstd8[:, i:i + 1],
                    op0=ALU.subtract, op1=ALU.mult)
                ev = K.tick("ve", ins)
                K.cw(["xn2"], ev)
                K.cr(["res%d" % t], ev)
                K.cr(["lnstat"], ev)
            K.body("vector", v_n)

            def t_tr(t_eng):
                K.rr(t_eng, ["xn2", "gt"])
                K.rw(t_eng, ["ptr"])
                ins = None
                for cc in range(8):
                    ins = t_eng.transpose(
                        ps_tr[:, cc * 128:(cc + 1) * 128],
                        sc_bf[:, XN2_O + cc * 128:XN2_O + (cc + 1) * 128],
                        id_bf[:])
                ev = K.tick("pe", ins)
                K.cw(["ptr"], ev)
                K.cr(["xn2"], ev)
            K.body("tensor", t_tr)

            def v_ev2(v):
                K.rr(v, ["ptr"])
                K.rw(v, [dest])
                ev2 = K.tick("ve", v.tensor_copy(
                    (r1 if dest.startswith("r1") else big)[
                        :, dest_off:dest_off + 1024],
                    ps_tr[:]))
                K.cw([dest], ev2)
                K.cr(["ptr"], ev2)
            K.body("vector", v_ev2)

        def ffn2(seq, call):
            for grp in range(2):
                t0 = 8 * grp
                tiles = [t0 + i for i in range(8)]

                if grp == 0:
                    ln_group(tiles, EPS_LN)

                for i, t in enumerate(tiles):
                    xnT_tile(seq, call, grp, i, t, "r1t%d" % i, i * 1024)

                    def t_rt(t_eng, i=i):
                        K.rr(t_eng, ["r1t%d" % i])
                        K.wait(t_eng, K.rw_["qstat"])
                        K.rw(t_eng, ["pab0"])
                        ins = None
                        for c in range(NC):
                            ins = t_eng.matmul(
                                ps_ab[:, 0:E_R],
                                r1[:, i * 1024 + c * 128:i * 1024 + (c + 1) * 128],
                                rwsb[:, c * E_R:(c + 1) * E_R],
                                start=(c == 0), stop=(c == NC - 1))
                        ev = K.tick("pe", ins)
                        K.cw(["pab0"], ev)
                        K.cr(["r1t%d" % i], ev)
                    K.body("tensor", t_rt)

                    def v_g1(v, i=i):
                        lg = sc_f[:, LG_O + i * 8:LG_O + (i + 1) * 8]
                        nm = sc_f[:, NM_O + i:NM_O + i + 1]
                        nt2 = sc_f[:, NT2_O + i:NT2_O + i + 1]
                        eq = sc_f[:, GT_O:GT_O + 8]
                        l2 = sc_f[:, GT_O + 8:GT_O + 16]
                        K.rr(v, ["pab0"])
                        K.rw(v, ["lg%d" % i, "nm%d" % i, "nt%d" % i, "gt"])
                        ins = v.tensor_tensor(lg[:], ps_ab[:, 0:E_R], rbb[:], op=ALU.add)
                        K.cut(v, ins, "ve")
                        ev0 = K.cut(v, v.reduce_max(nm[:], lg[:], axis=AX.X,
                                                    negate=True), "ve")
                        K.cr(["pab0"], ev0)
                        ins = v.tensor_scalar(eq[:], lg[:], nm[:], 0.0,
                                              op0=ALU.add, op1=ALU.is_ge)
                        K.cut(v, ins, "ve")
                        ins = v.scalar_tensor_tensor(l2[:], eq[:], NEG_BIG, lg[:],
                                                     op0=ALU.mult, op1=ALU.add)
                        K.cut(v, ins, "ve")
                        ev = K.tick("ve", v.reduce_max(nt2[:], l2[:], axis=AX.X,
                                                       negate=True))
                        K.cw(["lg%d" % i, "nm%d" % i, "nt%d" % i, "gt"], ev)
                    K.body("vector", v_g1)

                def s_g(s):
                    for i in range(8):
                        lg = sc_f[:, LG_O + i * 8:LG_O + (i + 1) * 8]
                        nm = sc_f[:, NM_O + i:NM_O + i + 1]
                        xt = sc_f[:, XT_O + i * 8:XT_O + (i + 1) * 8]
                        K.rr(s, ["lg%d" % i, "nm%d" % i])
                        K.rw(s, ["xt%d" % i])
                        ev = K.tick("se", s.activation(xt[:], lg[:], AF.Exp,
                                                       bias=nm[:], scale=1.0))
                        K.cw(["xt%d" % i], ev)
                        K.cr(["lg%d" % i, "nm%d" % i], ev)
                K.body("scalar", s_g)

                def v_g2(v):
                    for i in range(8):
                        lg = sc_f[:, LG_O + i * 8:LG_O + (i + 1) * 8]
                        nt2 = sc_f[:, NT2_O + i:NT2_O + i + 1]
                        xt = sc_f[:, XT_O + i * 8:XT_O + (i + 1) * 8]
                        ind = sc_f[:, GT_O + 16:GT_O + 24]
                        ge = sc_f[:, GT_O + 24:GT_O + 32]
                        ssum = sc_f[:, GT_O + 14:GT_O + 15]
                        rs = sc_f[:, GT_O + 15:GT_O + 16]
                        gcol = sc_f[:, GATES_O + i * 8:GATES_O + (i + 1) * 8]
                        K.rr(v, ["xt%d" % i, "lg%d" % i, "nt%d" % i])
                        K.rw(v, ["gt", "gates"])
                        ins = v.tensor_scalar(ind[:], lg[:], nt2[:], 0.0,
                                              op0=ALU.add, op1=ALU.is_ge)
                        K.cut(v, ins, "ve")
                        ins = v.scalar_tensor_tensor(ge[:], xt[:], 1.0, ind[:],
                                                     op0=ALU.bypass, op1=ALU.mult,
                                                     accum_out=ssum[:])
                        K.cut(v, ins, "ve")
                        ins = v.reciprocal(rs[:], ssum[:])
                        K.cut(v, ins, "ve")
                        ins = v.tensor_scalar(gcol[:], ge[:], rs[:], 0.5,
                                              op0=ALU.mult, op1=ALU.mult)
                        ev = K.tick("ve", ins)
                        K.cw(["gt", "gates"], ev)
                        K.cr(["xt%d" % i, "lg%d" % i, "nt%d" % i], ev)
                K.body("vector", v_g2)

                base_i = pair_stream.index((seq, call, grp, 0))

                def mk_u1(p, ti, slots):
                    exps = PAIRS[p]
                    def u1():
                        def t_ab(t_eng):
                            for ei in range(2):
                                off = slots[ei] * SLOT
                                a_ps = ps_s[:, 1024:1536] if ei == 0 else ps_ab[:, 0:512]
                                b_ps = ps_s[:, 1536:2048] if ei == 0 else ps_ab[:, 512:1024]
                                a_rg = ["pss2"] if ei == 0 else ["pab0"]
                                b_rg = ["pss3"] if ei == 0 else ["pab1"]
                                K.wait(t_eng, K.rw_["slot%d" % slots[ei]])
                                K.rr(t_eng, ["r1t%d" % ti, "slot%d" % slots[ei]])
                                K.rw(t_eng, a_rg)
                                ins = None
                                for c in range(NC):
                                    lhs = r1[:, ti * 1024 + c * 128:ti * 1024 + (c + 1) * 128]
                                    ins = t_eng.matmul(
                                        a_ps, lhs,
                                        big[:, off + c * 512:off + (c + 1) * 512],
                                        start=(c == 0), stop=(c == NC - 1))
                                ev = K.tick("pe", ins)
                                K.cw(a_rg, ev)
                                K.rw(t_eng, b_rg)
                                ins = None
                                for c in range(NC):
                                    lhs = r1[:, ti * 1024 + c * 128:ti * 1024 + (c + 1) * 128]
                                    ins = t_eng.matmul(
                                        b_ps, lhs,
                                        big[:, off + 4096 + c * 512:off + 4096 + (c + 1) * 512],
                                        start=(c == 0), stop=(c == NC - 1))
                                ev = K.tick("pe", ins)
                                K.cw(b_rg, ev)
                                K.cr(["r1t%d" % ti, "slot%d" % slots[ei]], ev)
                        K.body("tensor", t_ab)

                        def s_silu2(s):
                            for ei in range(2):
                                a_ps = ps_s[:, 1024:1536] if ei == 0 else ps_ab[:, 0:512]
                                a_rg = ["pss2"] if ei == 0 else ["pab0"]
                                sa = sc_f[:, XQN_O + ei * 512:XQN_O + (ei + 1) * 512]
                                K.rr(s, a_rg + ["xqn"])
                                K.rw(s, ["xqn"])
                                ev = K.tick("se", s.activation(sa[:], a_ps, AF.Silu,
                                                               bias=0.0, scale=1.0))
                                K.cw(["xqn"], ev)
                                K.cr(a_rg, ev)
                        K.body("scalar", s_silu2)

                        def v_stt2(v):
                            for ei in range(2):
                                kind, eidx = exps[ei]
                                b_ps = ps_s[:, 1536:2048] if ei == 0 else ps_ab[:, 512:1024]
                                b_rg = ["pss3"] if ei == 0 else ["pab1"]
                                sa = sc_f[:, XQN_O + ei * 512:XQN_O + (ei + 1) * 512]
                                ho = H1G_O[ei]
                                K.rr(v, ["xqn"] + b_rg)
                                if kind == "r":
                                    K.rr(v, ["gates"])
                                K.rw(v, ["h1g%d" % ei])
                                gsc = (sc_f[:, GATES_O + ti * 8 + eidx:
                                            GATES_O + ti * 8 + eidx + 1]
                                       if kind == "r" else 0.5)
                                ins = v.scalar_tensor_tensor(
                                    sc_bf[:, ho:ho + 512], sa[:], gsc, b_ps,
                                    op0=ALU.mult, op1=ALU.mult)
                                ev = K.tick("ve", ins)
                                K.cw(["h1g%d" % ei], ev)
                                K.cr(["xqn"] + b_rg, ev)
                                if kind == "r":
                                    K.cr(["gates"], ev)
                        K.body("vector", v_stt2)
                    return u1

                def mk_u2(p, ti, slots):
                    def u2():
                        def t_h1(t_eng):
                            K.rr(t_eng, ["h1g0", "h1g1"])
                            K.rw(t_eng, ["ptr"])
                            ins = None
                            for ei in range(2):
                                ho = H1G_O[ei]
                                for c in range(4):
                                    ins = t_eng.transpose(
                                        ps_tr[:, ei * 512 + c * 128:
                                              ei * 512 + (c + 1) * 128],
                                        sc_bf[:, ho + c * 128:ho + (c + 1) * 128],
                                        id_bf[:])
                            ev = K.tick("pe", ins)
                            K.cw(["ptr"], ev)
                            K.cr(["h1g0", "h1g1"], ev)
                        K.body("tensor", t_h1)

                        def v_h1T(v):
                            K.rr(v, ["ptr"])
                            K.rw(v, ["h1gT0", "h1gT1"])
                            v.tensor_copy(
                                sc_bf[:, H1GT_O[0]:H1GT_O[0] + 512], ps_tr[:, 0:512])
                            ev = K.tick("ve", v.tensor_copy(
                                sc_bf[:, H1GT_O[1]:H1GT_O[1] + 512], ps_tr[:, 512:1024]))
                            K.cw(["h1gT0", "h1gT1"], ev)
                            K.cr(["ptr"], ev)
                        K.body("vector", v_h1T)
                    return u2

                def mk_u3(p, ti, slots, t):
                    def u3():
                        def t_w2both(t_eng):
                            for ei in range(2):
                                off = slots[ei] * SLOT
                                hto = H1GT_O[ei]
                                K.rr(t_eng, ["h1gT%d" % ei, "slot%d" % slots[ei]])
                                if ei == 0:
                                    K.rw(t_eng, ["pss0", "pss1"])
                                ins = None
                                for hc in range(4):
                                    lhs = sc_bf[:, hto + hc * 128:hto + (hc + 1) * 128]
                                    for dh in range(2):
                                        ins = t_eng.matmul(
                                            ps_s[:, dh * 512:(dh + 1) * 512], lhs,
                                            big[:, off + 8192 + hc * 1024 + dh * 512:
                                                off + 8192 + hc * 1024 + (dh + 1) * 512],
                                            start=(hc == 0 and ei == 0),
                                            stop=(hc == 3 and ei == 1))
                                ev = K.tick("pe", ins)
                                if ei == 1:
                                    K.cw(["pss0", "pss1"], ev)
                                K.cr(["h1gT%d" % ei, "slot%d" % slots[ei]], ev)
                        K.body("tensor", t_w2both)

                        def v_yadd(v):
                            K.rr(v, ["pss0", "pss1"])
                            K.rw(v, ["res%d" % t])
                            ins = v.tensor_tensor(
                                resid[:, t * D:(t + 1) * D],
                                resid[:, t * D:(t + 1) * D],
                                ps_s[:, 0:1024], op=ALU.add)
                            ev = K.tick("ve", ins)
                            K.cw(["res%d" % t], ev)
                            K.cr(["pss0", "pss1"], ev)
                        K.body("vector", v_yadd)
                    return u3

                def mk_gw(i_str):
                    def gw():
                        def g_w(g):
                            j = i_str + 2
                            if j < len(pair_stream) and j not in attn_barrier:
                                sj0, sj1 = pair_slots(j)
                                (k0, e0), (k1, e1) = PAIRS[pair_stream[j][3]]
                                slot_dma(g, sj0, k0, e0)
                                slot_dma(g, sj1, k1, e1)
                        K.body("gpsimd", g_w)
                    return gw

                U1, U2, U3, GW = [], [], [], {}
                for p in range(len(PAIRS)):
                    i_str = base_i + p
                    slots = pair_slots(i_str)
                    for ti in range(8):
                        U1.append(mk_u1(p, ti, slots))
                        U2.append(mk_u2(p, ti, slots))
                        U3.append(mk_u3(p, ti, slots, t0 + ti))
                    GW[8 * p + 7] = mk_gw(i_str)

                n = len(U1)
                U1[0]()
                U2[0]()
                for k in range(n):
                    if k + 1 < n:
                        U1[k + 1]()
                    U3[k]()
                    if k + 1 < n:
                        U2[k + 1]()
                    if k in GW:
                        GW[k]()
                    if grp == 0 and k == 33:
                        ln_group([8 + j for j in range(8)], EPS_LN,
                                 alt_scratch=True)
                    elif grp == 1 and call == 1 and k == 33:
                        ln_group([j for j in range(8)], EPS_LN,
                                 alt_scratch=True)

        def qkv_half(seq, half):
            t0 = 8 * half
            tiles = [t0 + i for i in range(8)]

            def g_w(g, seq=seq, half=half):
                K.rw(g, ["wwa"])
                ev = K.tick("wa", g.dma_start(
                    big[:, WWA_O:WWA_O + 8192], wqkv_d[0]), 16)
                K.cw(["wwa"], ev)
                K.rw(g, ["wwb"])
                ev = K.tick("wb", g.dma_start(
                    big[:, WWB_O:WWB_O + 8192], wqkv_d[1]), 16)
                K.cw(["wwb"], ev)
                if half == 0:
                    K.rw(g, ["woutsb"])
                    ev = K.tick("wm", g.dma_start(
                        sc_bf[:, WOUT_O:WOUT_O + 8192], wout_d[:]), 16)
                    K.cw(["woutsb"], ev)
            K.body("gpsimd", g_w)

            if half == 1:
                ln_group(tiles, EPS_LN)

            for i, t in enumerate(tiles):
                xnT_tile(seq, 0, half, i, t, "vsb%d" % t, V_O + t * 1024)

            QSL = [(QSTAT_O, "qstat"), (QSTAT_O + 8, "qstatB"),
                   (1520, "qstatC"), (1528, "qstatD")]

            def qk_units(wname, j, i, t):
                g3 = j % 3
                if g3 == 0:
                    dst = ps_s[:, 0:1024]
                    dst_rg = ["pss0", "pss1"]
                elif g3 == 1:
                    dst = ps_s[:, 1024:2048]
                    dst_rg = ["pss2", "pss3"]
                else:
                    dst = ps_ab
                    dst_rg = ["pab0", "pab1"]
                if wname == "v":
                    par = i % 2
                    dst = ps_ab if par == 0 else ps_s[:, 0:1024]
                    dst_rg = ["pab0", "pab1"] if par == 0 else ["pss0", "pss1"]
                qo, qrg = QSL[{"q": 0, "k": 2, "v": 0}[wname] + i % 2]
                wrg, wcol = (("wwb", WWB_O) if wname == "k"
                             else ("wwa", WWA_O))

                def u1():
                    def t_mm(t_eng):
                        K.rr(t_eng, ["vsb%d" % t, wrg])
                        K.rw(t_eng, dst_rg)
                        xt = V_O + t * 1024
                        ins = None
                        for c in range(NC):
                            lhs = big[:, xt + c * 128:xt + (c + 1) * 128]
                            for dh in range(2):
                                ins = t_eng.matmul(
                                    dst[:, dh * 512:(dh + 1) * 512], lhs,
                                    big[:, wcol + c * 1024 + dh * 512:
                                        wcol + c * 1024 + (dh + 1) * 512],
                                    start=(c == 0), stop=(c == NC - 1))
                        ev = K.tick("pe", ins)
                        K.cw(dst_rg, ev)
                        K.cr(["vsb%d" % t, wrg], ev)
                    K.body("tensor", t_mm)

                def u2a():
                    if wname in ("q", "k"):
                        def v_st(v):
                            qsum = sc_f[:, qo:qo + 1]
                            K.rr(v, dst_rg)
                            K.rw(v, [qrg])
                            ev = K.tick("ve", v.reduce_sum(qsum[:], dst[:],
                                                           axis=AX.X))
                            K.cw([qrg], ev)
                            K.cr(dst_rg, ev)
                        K.body("vector", v_st)
                    else:
                        def v_vev(v):
                            K.rr(v, dst_rg)
                            K.rw(v, ["vsb%d" % t])
                            ev = K.tick("ve", v.tensor_copy(
                                big[:, V_O + t * 1024:V_O + (t + 1) * 1024],
                                dst[:]))
                            K.cw(["vsb%d" % t], ev)
                            K.cr(dst_rg, ev)
                        K.body("vector", v_vev)

                def u2b():
                    if wname not in ("q", "k"):
                        return
                    if True:
                        def s_sq(s):
                            qsum = sc_f[:, qo:qo + 1]
                            qss = sc_f[:, qo + 1:qo + 2]
                            mn = sc_f[:, qo + 2:qo + 3]
                            nmn = sc_f[:, qo + 3:qo + 4]
                            vr = sc_f[:, qo + 4:qo + 5]
                            sg = sc_f[:, qo + 5:qo + 6]
                            K.rr(s, dst_rg + [qrg, "cst"])
                            K.rw(s, ["atT2", "atT3", qrg])
                            ins = s.activation(sc_bf[:, 1024:1024 + D], dst[:],
                                               AF.Square, accum_out=qss[:])
                            s.mul(mn[:], qsum[:], cst[:, 2:3])
                            ins = s.mul(nmn[:], qsum[:], cst[:, 5:6])
                            K.cut(s, ins, "se")
                            ins = s.mul(vr[:], mn[:], nmn[:])
                            K.cut(s, ins, "se")
                            ins = s.add(vr[:], vr[:], cst[:, 1:2])
                            K.cut(s, ins, "se")
                            ins = s.activation(sg[:], qss[:], AF.Ln,
                                               bias=vr[:], scale=cst[:, 2:3])
                            K.cut(s, ins, "se")
                            rsd = sc_f[:, qo + 6:qo + 7]
                            ev = K.tick("se", s.activation(rsd[:], sg[:], AF.Exp,
                                                           bias=0.0,
                                                           scale=cst[:, 4:5]))
                            K.cw(["atT2", "atT3", qrg], ev)
                            K.cr(dst_rg, ev)
                        K.body("scalar", s_sq)

                        def v_nm(v):
                            rs = sc_f[:, qo + 6:qo + 7]
                            mn = sc_f[:, qo + 2:qo + 3]
                            K.rr(v, [qrg] + dst_rg + ["xn2"])
                            K.rw(v, ["xn2"])
                            ins = v.tensor_scalar(sc_bf[:, XN2_O:XN2_O + D],
                                                  dst[:], mn[:], rs[:],
                                                  op0=ALU.subtract,
                                                  op1=ALU.mult)
                            ev = K.tick("ve", ins)
                            K.cw(["xn2"], ev)
                            K.cr(dst_rg + [qrg], ev)
                        K.body("vector", v_nm)

                def u3():
                    if wname == "v":
                        return
                    scv = qs8 if wname == "q" else ks8

                    def t_tr(t_eng):
                        K.rr(t_eng, ["xn2", "gt"])
                        K.rw(t_eng, ["ptr"])
                        ins = None
                        for cc in range(8):
                            ins = t_eng.transpose(
                                ps_tr[:, cc * 128:(cc + 1) * 128],
                                sc_bf[:, XN2_O + cc * 128:XN2_O + (cc + 1) * 128],
                                id_bf[:])
                        ev = K.tick("pe", ins)
                        K.cw(["ptr"], ev)
                        K.cr(["xn2"], ev)
                    K.body("tensor", t_tr)

                    def v_qe(v):
                        K.rr(v, ["ptr"])
                        drg = (["r1t%d" % j2 for j2 in range(8)]
                               if wname == "q" else ["knt"])
                        K.rw(v, drg)
                        ins = None
                        for cc in range(8):
                            if wname == "q":
                                d_ap = r1[:, cc * 1024 + i * 128:
                                          cc * 1024 + (i + 1) * 128]
                            else:
                                d_ap = big[:, KNT_O + cc * 2048 + t * 128:
                                           KNT_O + cc * 2048 + (t + 1) * 128]
                            ins = v.tensor_scalar_mul(
                                d_ap, ps_tr[:, cc * 128:(cc + 1) * 128],
                                scv[:, cc:cc + 1])
                        ev = K.tick("ve", ins)
                        K.cw(drg, ev)
                        K.cr(["ptr"], ev)
                    K.body("vector", v_qe)
                return u1, u2a, u2b, u3

            us = [qk_units("q" if j % 2 == 0 else "k", j, j // 2,
                           tiles[j // 2]) for j in range(16)]
            us[0][0]()
            us[0][1]()
            us[0][2]()
            for i in range(len(us)):
                if i + 1 < len(us):
                    us[i + 1][0]()
                    us[i + 1][1]()
                us[i][3]()
                if i + 1 < len(us):
                    us[i + 1][2]()

            def g_wv(g):
                K.rw(g, ["wwa"])
                ev = K.tick("wa", g.dma_start(
                    big[:, WWA_O:WWA_O + 8192], wqkv_d[2]), 16)
                K.cw(["wwa"], ev)
            K.body("gpsimd", g_wv)

            vs = [qk_units("v", 0, i, t) for i, t in enumerate(tiles)]
            vs[0][0]()
            for i in range(len(vs)):
                if i + 1 < len(vs):
                    vs[i + 1][0]()
                vs[i][1]()

        def attn_qgroup(seq, qg, mid_wout=None):
            nkc = 4 * (qg + 1)
            qts = [4 * qg + j for j in range(4)]
            if qg == 0:
                waves = [[0, 1, 2, 3]]
                ro_of = {0: 0, 1: 512, 2: 1024, 3: 1536}
            elif qg == 1:
                waves = [[0, 1], [2, 3]]
                ro_of = {0: 0, 1: 1024, 2: 0, 3: 1024}
            else:
                waves = [[0], [1], [2], [3]]
                ro_of = {0: 0, 1: 0, 2: 0, 3: 0}

            def att_ap(pk, slot, lo, hi):
                if pk == 0:
                    return big[:, ATT_O + slot * 512 + lo:
                               ATT_O + slot * 512 + hi]
                return sc_bf[:, 1024 + slot * 512 + lo:1024 + slot * 512 + hi]

            def make_tct(h, c, po):
                def mk_pair(kc0):
                    kcs = [kc for kc in (kc0, kc0 + 1) if kc < nkc]
                    vqls = {kc: [ql for ql, qt in enumerate(qts) if qt >= kc]
                            for kc in kcs}
                    pk = (kc0 // 2) % 2
                    args = [("atT%d" % (pk * 2 + kc % 2)) for kc in kcs]

                    def t_at(t_eng, kcs=kcs, vqls=vqls):
                        regs = sorted({"asb%d" % ql for kc in kcs
                                       for ql in vqls[kc]})
                        K.rr(t_eng, regs + ["gt"])
                        K.rw(t_eng, ["ptr"])
                        ins = None
                        for j2, kc in enumerate(kcs):
                            for j, ql in enumerate(vqls[kc]):
                                ins = t_eng.transpose(
                                    ps_tr[:, j2 * 512 + j * 128:
                                          j2 * 512 + (j + 1) * 128],
                                    big[:, ASB_O + ql * 2048 + kc * 128:
                                        ASB_O + ql * 2048 + (kc + 1) * 128],
                                    id_bf[:])
                        ev = K.tick("pe", ins)
                        K.cw(["ptr"], ev)
                        K.cr(regs, ev)

                    def v_ev(v, kcs=kcs, vqls=vqls, pk=pk, args=args):
                        K.rr(v, ["ptr"])
                        K.rw(v, args)
                        ins = None
                        for j2, kc in enumerate(kcs):
                            q0 = vqls[kc][0] * 128
                            n = 512 - q0
                            ins = v.tensor_copy(
                                att_ap(pk, kc % 2, q0, 512),
                                ps_tr[:, j2 * 512:j2 * 512 + n])
                        ev = K.tick("ve", ins)
                        K.cw(args, ev)
                        K.cr(["ptr"], ev)

                    def t_cx(t_eng, kcs=kcs, vqls=vqls, pk=pk, c=c, po=po,
                             h=h):
                        for kc in kcs:
                            q0 = vqls[kc][0] * 128
                            n = 512 - q0
                            rg = "atT%d" % (pk * 2 + kc % 2)
                            K.rr(t_eng, [rg, "vsb%d" % kc])
                            if kc == 0:
                                K.rw(t_eng, ["pctx"])
                            ins = t_eng.matmul(
                                ps_ctx[po:po + 64, q0:q0 + n],
                                big[:, V_O + kc * 1024 + h * 64:
                                    V_O + kc * 1024 + h * 64 + 64],
                                att_ap(pk, kc % 2, q0, 512),
                                start=(kc == 0), stop=(kc == nkc - 1),
                                tile_position=(0, po))
                            ev = K.tick("pe", ins)
                            if kc == nkc - 1:
                                K.cw(["pctx"], ev)
                            K.cr([rg, "vsb%d" % kc], ev)
                    return t_at, v_ev, t_cx

                def emit_tct():
                    pairs = [mk_pair(kc0) for kc0 in range(0, nkc, 2)]
                    K.body("tensor", pairs[0][0])
                    K.body("vector", pairs[0][1])
                    for k in range(1, len(pairs)):
                        K.body("tensor", pairs[k][0])
                        K.body("vector", pairs[k][1])
                        K.body("tensor", pairs[k - 1][2])
                    K.body("tensor", pairs[-1][2])

                    def v_cx(v, c=c, po=po):
                        K.rr(v, ["pctx"])
                        K.rw(v, ["ctxT"])
                        ev = K.tick("ve", v.tensor_copy(
                            big[po:po + 64,
                                CTXT_O + c * 512:CTXT_O + (c + 1) * 512],
                            ps_ctx[po:po + 64, :]))
                        K.cw(["ctxT"], ev)
                        K.cr(["pctx"], ev)
                    K.body("vector", v_cx)
                return emit_tct

            BANKS = ["pss0", "pss1", "pss2", "pss3", "pab0", "pab1"]

            nkg0 = (128 * (qts[0] + 1) + 511) // 512
            step = {1: 1, 2: 2, 3: 3, 4: 2}[nkg0]
            mod = {1: 6, 2: 6, 3: 6, 4: 4}[nkg0]
            stream = []
            cur = 0
            prevb = set()
            for h in range(NH):
                for wi, wave in enumerate(waves):
                    alloc = {}
                    plan = {}
                    for ql in wave:
                        keys = 128 * (qts[ql] + 1)
                        nkg = (keys + 511) // 512
                        assert nkg == nkg0
                        bks = list(range(cur, cur + nkg))
                        alloc[ql] = bks
                        ejs = [j for j in range(nkg) if bks[j] not in prevb]
                        ljs = [j for j in range(nkg) if bks[j] in prevb]
                        assert ejs == list(range(min(ejs), max(ejs) + 1)) \
                            if ejs else True
                        plan[ql] = (ejs, ljs)
                        cur = (cur + step) % mod
                    prevb = {b for ql in wave for b in alloc[ql]}
                    stream.append((h, wi, wave, alloc, plan))

            def region_of(alloc, ql, js, keys):
                lo = alloc[ql][0] * 512 + js[0] * 512
                hi = alloc[ql][0] * 512 + min(keys, (js[-1] + 1) * 512)
                return lo, hi

            def emit_sc(item, which):
                h, wi, wave, alloc, plan = item
                c = h // 2
                po = 64 * (h % 2)
                part = [(ql, j) for ql in wave for j in plan[ql][which]]
                if not part:
                    return

                def t_sc(t_eng, part=part, alloc=alloc, c=c, po=po):
                    by_ql = {}
                    for ql, j in part:
                        by_ql.setdefault(ql, []).append(j)
                    for ql, js in by_ql.items():
                        qt = qts[ql]
                        keys = 128 * (qt + 1)
                        nkg = len(alloc[ql])
                        base = alloc[ql][0]
                        rgs = [BANKS[alloc[ql][j]] for j in js]
                        K.rr(t_eng, ["r1t%d" % c, "knt"])
                        K.rw(t_eng, rgs)
                        lhs = r1[po:po + 64, c * 1024 + (qt % 8) * 128:
                                 c * 1024 + (qt % 8 + 1) * 128]
                        ins = None
                        for j in js:
                            n = min(512, keys - j * 512)
                            o = base * 512 + j * 512
                            last = (j == nkg - 1)
                            ins = t_eng.matmul(
                                ps_all[:, o:o + n], lhs,
                                big[po:po + 64,
                                    KNT_O + c * 2048 + j * 512:
                                    KNT_O + c * 2048 + j * 512 + n],
                                start=True, stop=not last)
                            if last:
                                ins = t_eng.matmul(
                                    ps_all[:, o + n - 128:o + n],
                                    id_bf[:], mask_bf[:],
                                    start=False, stop=True)
                        ev = K.tick("pe", ins)
                        K.cw(rgs, ev)
                        K.cr(["r1t%d" % c, "knt"], ev)
                K.body("tensor", t_sc)

            def emit_mxa(item):
                h, wi, wave, alloc, plan = item
                if not any(plan[ql][0] for ql in wave):
                    return

                def v_mxa(v, wave=wave, alloc=alloc, plan=plan):
                    for ql in wave:
                        ejs, ljs = plan[ql]
                        if not ejs:
                            continue
                        qt = qts[ql]
                        keys = 128 * (qt + 1)
                        lo, hi = region_of(alloc, ql, ejs, keys)
                        rgs = [BANKS[alloc[ql][j]] for j in ejs]
                        K.rr(v, rgs)
                        if ljs:
                            mx = sc_f[:, MXS_O + ql * 4:MXS_O + ql * 4 + 1]
                            K.rw(v, ["mxs%d" % ql])
                            ins = v.reduce_max(mx[:], ps_all[:, lo:hi],
                                               axis=AX.X, negate=False)
                            ev = K.tick("ve", ins)
                            K.cw(["mxs%d" % ql], ev)
                        else:
                            nm = sc_f[:, SMX_O + ql:SMX_O + ql + 1]
                            K.rw(v, ["negm%d" % ql])
                            ins = v.reduce_max(nm[:], ps_all[:, lo:hi],
                                               axis=AX.X, negate=True)
                            ev = K.tick("ve", ins)
                            K.cw(["negm%d" % ql], ev)
                        K.cr(rgs, ev)
                K.body("vector", v_mxa)

            def emit_mxb(item):
                h, wi, wave, alloc, plan = item
                if not any(plan[ql][1] for ql in wave):
                    return

                def v_mxb(v, wave=wave, alloc=alloc, plan=plan):
                    for ql in wave:
                        ejs, ljs = plan[ql]
                        if not ljs:
                            continue
                        qt = qts[ql]
                        keys = 128 * (qt + 1)
                        lo, hi = region_of(alloc, ql, ljs, keys)
                        rgs = [BANKS[alloc[ql][j]] for j in ljs]
                        nm = sc_f[:, SMX_O + ql:SMX_O + ql + 1]
                        K.rr(v, rgs)
                        K.rw(v, ["negm%d" % ql, "mxs%d" % ql])
                        if not ejs:
                            ins = v.reduce_max(nm[:], ps_all[:, lo:hi],
                                               axis=AX.X, negate=True)
                        else:
                            mx1 = sc_f[:, MXS_O + ql * 4 + 1:
                                       MXS_O + ql * 4 + 2]
                            ins = v.reduce_max(mx1[:], ps_all[:, lo:hi],
                                               axis=AX.X, negate=False)
                            K.cut(v, ins, "ve")
                            ins = v.reduce_max(
                                nm[:], sc_f[:, MXS_O + ql * 4:
                                            MXS_O + ql * 4 + 2],
                                axis=AX.X, negate=True)
                        ev = K.tick("ve", ins)
                        K.cw(["negm%d" % ql, "mxs%d" % ql], ev)
                        K.cr(rgs, ev)
                K.body("vector", v_mxb)

            prev_tct = None
            for i, item in enumerate(stream):
                h, wi, wave, alloc, plan = item
                if i == 0:
                    emit_sc(item, 0)
                    emit_mxa(item)
                if i + 1 < len(stream):
                    emit_sc(stream[i + 1], 0)
                emit_mxb(item)

                if wi == 0 and prev_tct is not None:
                    prev_tct()
                    prev_tct = None

                def s_ex(s, wave=wave, alloc=alloc):
                    for ql in wave:
                        qt = qts[ql]
                        keys = 128 * (qt + 1)
                        ro = alloc[ql][0] * 512
                        rgs = [BANKS[b] for b in alloc[ql]]
                        nm = sc_f[:, SMX_O + ql:SMX_O + ql + 1]
                        ls = sc_f[:, SMX_O + 4 + ql:SMX_O + 5 + ql]
                        K.rr(s, rgs + ["negm%d" % ql])
                        K.rw(s, ["asb%d" % ql, "lsum%d" % ql])
                        ins = s.activation(
                            big[:, ASB_O + ql * 2048:ASB_O + ql * 2048 + keys],
                            ps_all[:, ro:ro + keys], AF.Exp,
                            bias=nm[:], scale=1.0, accum_out=ls[:])
                        ev = K.tick("se", ins)
                        K.cw(["asb%d" % ql, "lsum%d" % ql], ev)
                        K.cr(rgs + ["negm%d" % ql], ev)
                K.body("scalar", s_ex)

                if i + 1 < len(stream):
                    emit_sc(stream[i + 1], 1)
                    emit_mxa(stream[i + 1])

                def v_sc2(v, wave=wave):
                    ins = None
                    for ql in wave:
                        ls = sc_f[:, SMX_O + 4 + ql:SMX_O + 5 + ql]
                        rl = sc_f[:, SMX_O + 8 + ql:SMX_O + 9 + ql]
                        K.rr(v, ["lsum%d" % ql])
                        K.rw(v, ["rl%d" % ql])
                        ins = v.reciprocal(rl[:], ls[:])
                    K.cut(v, ins, "ve")
                    for ql in wave:
                        qt = qts[ql]
                        keys = 128 * (qt + 1)
                        rl = sc_f[:, SMX_O + 8 + ql:SMX_O + 9 + ql]
                        K.rr(v, ["rl%d" % ql, "asb%d" % ql])
                        K.rw(v, ["asb%d" % ql])
                        ins = v.tensor_scalar_mul(
                            big[:, ASB_O + ql * 2048:ASB_O + ql * 2048 + keys],
                            big[:, ASB_O + ql * 2048:ASB_O + ql * 2048 + keys],
                            rl[:])
                        ev = K.tick("ve", ins)
                        K.cw(["asb%d" % ql, "rl%d" % ql], ev)
                        K.cr(["lsum%d" % ql], ev)
                K.body("vector", v_sc2)

                if wi == len(waves) - 1:
                    assert prev_tct is None
                    prev_tct = make_tct(h, h // 2, 64 * (h % 2))
            prev_tct()

            for ql, qt in enumerate(qts):
                def t_wo(t_eng, ql=ql):
                    K.rr(t_eng, ["ctxT", "woutsb"])
                    K.rw(t_eng, ["pab0", "pab1"])
                    ins = None
                    for c2 in range(NC):
                        lhs = big[:, CTXT_O + c2 * 512 + ql * 128:
                                  CTXT_O + c2 * 512 + (ql + 1) * 128]
                        for dh in range(2):
                            ins = t_eng.matmul(
                                ps_ab[:, dh * 512:(dh + 1) * 512], lhs,
                                sc_bf[:, WOUT_O + c2 * 1024 + dh * 512:
                                      WOUT_O + c2 * 1024 + (dh + 1) * 512],
                                start=(c2 == 0), stop=(c2 == NC - 1))
                    ev = K.tick("pe", ins)
                    K.cw(["pab0", "pab1"], ev)
                    K.cr(["ctxT", "woutsb"], ev)
                K.body("tensor", t_wo)

                def v_wa(v, qt=qt):
                    K.rr(v, ["pab0", "pab1"])
                    K.rw(v, ["res%d" % qt])
                    ins = v.tensor_tensor(resid[:, qt * D:(qt + 1) * D],
                                          resid[:, qt * D:(qt + 1) * D],
                                          ps_ab, op=ALU.add)
                    ev = K.tick("ve", ins)
                    K.cw(["res%d" % qt], ev)
                    K.cr(["pab0", "pab1"], ev)
                K.body("vector", v_wa)

                if ql == 1 and mid_wout is not None:
                    mid_wout()

        K.block = None

        def emit(block):
            K.block = block
            prologue()
            for seq in range(NSEQ):
                def sx(g, seq=seq):
                    K.rw(g, ["res%d" % t for t in range(NT)])
                    ev = K.tick("sq_x", g.dma_start(
                        resid[:].rearrange("p (t f) -> p t f", t=NT),
                        x_d[seq].rearrange("(t p) f -> p t f", p=128)), 16)
                    K.cw(["res%d" % t for t in range(NT)], ev)
                K.body("sync", sx)

                ffn2(seq, 1)
                if phases >= 2:
                    for half in range(2):
                        qkv_half(seq, half)
                        for qgl in range(2):
                            qg = 2 * half + qgl
                            attn_qgroup(seq, qg)
                    def g_post(g, seq=seq):
                        for j in sorted(attn_barrier):
                            if attn_barrier[j] == seq:
                                sj0, sj1 = pair_slots(j)
                                (k0, e0), (k1, e1) = PAIRS[pair_stream[j][3]]
                                slot_dma(g, sj0, k0, e0)
                                slot_dma(g, sj1, k1, e1)
                    K.body("gpsimd", g_post)
                if phases >= 3:
                    ffn2(seq, 2)

                def sy(g, seq=seq):
                    for t in range(NT):
                        K.rr(g, ["res%d" % t])
                    ev = K.tick("sq_x", g.dma_start(
                        y_d[seq].rearrange("(t p) f -> p t f", p=128),
                        resid[:].rearrange("p (t f) -> p t f", t=NT)), 16)
                    K.cr(["res%d" % t for t in range(NT)], ev)
                    if seq == NSEQ - 1:
                        K.final_store = ev
                K.body("sync", sy)

            def g_end(g):
                K.wait(g, K.final_store)
            K.body("gpsimd", g_end)

        with nc.Block() as blk:
            emit(blk)

    return nc


def kernel(**inputs):
    xs = np.ascontiguousarray(np.asarray(inputs["L0_attn_x"], dtype=np.float32))
    wqkv = np.ascontiguousarray(np.asarray(inputs["L0_attn_W_qkv"], dtype=np.float32))
    qs = np.ascontiguousarray(np.asarray(inputs["L0_attn_q_scale"], dtype=np.float32).reshape(NC, 128).T)
    ks = np.ascontiguousarray(np.asarray(inputs["L0_attn_k_scale"], dtype=np.float32).reshape(NC, 128).T)
    wout = np.ascontiguousarray(np.asarray(inputs["L0_attn_W_out"], dtype=np.float32))
    rw = np.ascontiguousarray(np.asarray(inputs["L0_ffn_router_w"], dtype=np.float32))
    rb = np.ascontiguousarray(np.asarray(inputs["L0_ffn_router_b"], dtype=np.float32).reshape(1, E_R))
    rw1 = np.ascontiguousarray(np.asarray(inputs["L0_ffn_routed_w1"], dtype=np.float32))
    rw3 = np.ascontiguousarray(np.asarray(inputs["L0_ffn_routed_w3"], dtype=np.float32))
    rw2 = np.ascontiguousarray(np.asarray(inputs["L0_ffn_routed_w2"], dtype=np.float32))
    sw1 = np.ascontiguousarray(np.asarray(inputs["L0_ffn_shared_w1"], dtype=np.float32))
    sw3 = np.ascontiguousarray(np.asarray(inputs["L0_ffn_shared_w3"], dtype=np.float32))
    sw2 = np.ascontiguousarray(np.asarray(inputs["L0_ffn_shared_w2"], dtype=np.float32))

    def pack_dh(w):
        X = w.shape[1]
        return w.reshape(NC, 128, X).transpose(1, 0, 2).reshape(128, NC * X)

    def pack_h(w):
        X = w.shape[1]
        return w.reshape(4, 128, X).transpose(1, 0, 2).reshape(128, 4 * X)

    ffnw = np.stack([np.concatenate(
        [pack_dh(w1[e]), pack_dh(w3[e]), pack_h(w2[e])], axis=1)
        for (w1, w3, w2, es) in ((rw1, rw3, rw2, range(E_R)), (sw1, sw3, sw2, range(2)))
        for e in es])
    ffnw = np.ascontiguousarray(ffnw)
    wqkvl = np.ascontiguousarray(np.stack(
        [pack_dh(wqkv[:, m * D:(m + 1) * D]) for m in range(3)]))
    woutl = np.ascontiguousarray(pack_dh(wout))
    rwl = np.ascontiguousarray(pack_dh(rw))

    nc = build_program(phases=3)
    in_maps = []
    for core in range(8):
        in_maps.append({
            "x": np.ascontiguousarray(xs[2 * core:2 * core + 2]),
            "wqkvl": wqkvl, "q_scale": qs, "k_scale": ks, "woutl": woutl,
            "routerwl": rwl, "router_b": rb, "ffnw": ffnw,
        })
    res = run_bass_kernel_spmd(nc, in_maps, core_ids=list(range(8)))
    out = np.concatenate([r["y"] for r in res.results], axis=0)
    return out.reshape(16, S, D)


# version 31 (final)
# speedup vs baseline: 1.0736x; 1.0240x over previous
import contextlib
import numpy as np

import concourse.bass as bass
import concourse.mybir as mybir
from concourse.bass_utils import run_bass_kernel_spmd

F32 = mybir.dt.float32
BF16 = mybir.dt.bfloat16
AF = mybir.ActivationFunctionType
ALU = mybir.AluOpType
AX = mybir.AxisListType

D = 1024
H = 512
NH = 16
HD = 64
S = 2048
NT = 16
NSEQ = 2
NC = 8
E_R = 8
EPS_LN = 1e-5
EPS_QK = 1e-6
NEG_BIG = -3.0e38

PAIRS = [(("r", 0), ("r", 1)), (("r", 2), ("r", 3)), (("r", 4), ("r", 5)),
         (("r", 6), ("r", 7)), (("s", 0), ("s", 1))]

SLOT = 12288
KNT_O = 0
V_O = 16384
SUB_O = 32768
ASB_O = SUB_O
ATT_O = SUB_O + 8192
CTXT_O = SUB_O + 9216
WWA_O = SUB_O
WWB_O = SUB_O + 8192

XN2_O = 0
H1G_O = [3072, 3584]
H1GT_O = [4096, 4608]
WOUT_O = 3072
SC_BF_COLS = 11264

XQN_O = 0
MASK_O = 1024
GATES_O = 1152
STAT_O = 1216
LG_O = 1280
NM_O = 1344
NT2_O = 1352
XT_O = 1360
GT_O = 1424
SMX_O = 1456
QSTAT_O = 1472
MXS_O = 1488
LSS_O = 1504
SC_F_COLS = 1536


class KB:

    def __init__(self, semh, owner):
        self.semh = semh
        self.owner = owner
        self.cnt = {n: 0 for n in semh}
        self.lastw = {}
        self.cur = None
        self.block = None
        self.regions = {}
        self.conf = {}
        self.rw_ = {}
        self.rr_ = {}

    def region(self, name, tensor_key, lo, hi):
        assert name not in self.regions
        self.regions[name] = (tensor_key, lo, hi)
        self.conf[name] = []
        for o, (tk, l2, h2) in self.regions.items():
            if tk == tensor_key and not (hi <= l2 or h2 <= lo):
                self.conf[name].append(o)
                if o != name:
                    self.conf[o].append(name)
        self.rw_[name] = None
        self.rr_[name] = []

    def body(self, engname, fn):
        self.cur = engname
        getattr(self.block, engname)(fn)
        self.cur = None

    def tick(self, sem, instr, by=1):
        self.cnt[sem] += by
        instr.then_inc(self.semh[sem], by)
        return (sem, self.cnt[sem])

    def wait(self, eng, ev):
        if ev is None:
            return
        sem, val = ev
        lk = (self.cur, sem)
        if self.lastw.get(lk, 0) < val:
            self.lastw[lk] = val
            eng.wait_ge(self.semh[sem], val)

    def cut(self, eng, instr, sem):
        ev = self.tick(sem, instr)
        self.lastw[(self.cur, sem)] = ev[1]
        eng.wait_ge(self.semh[sem], ev[1])
        return ev

    def rw(self, eng, regs):
        for r in regs:
            for c in self.conf[r]:
                self.wait(eng, self.rw_[c])
                for ev in self.rr_[c]:
                    self.wait(eng, ev)

    def cw(self, regs, ev):
        for r in regs:
            self.rw_[r] = ev
            self.rr_[r] = []

    def rr(self, eng, regs):
        for r in regs:
            for c in self.conf[r]:
                self.wait(eng, self.rw_[c])

    def cr(self, regs, ev):
        for r in regs:
            if ev is not None and ev not in self.rr_[r]:
                self.rr_[r].append(ev)


def _psbanks(lo, hi):
    return ["pss%d" % b for b in range(lo // 512, (hi + 511) // 512)]


def build_program(phases=3, debug=False):
    nc = bass.Bass("TRN2", target_bir_lowering=False)

    x_d = nc.dram_tensor("x", [NSEQ, S, D], F32, kind="ExternalInput")
    wqkv_d = nc.dram_tensor("wqkvl", [3, 128, 8192], F32, kind="ExternalInput")
    qs_d = nc.dram_tensor("q_scale", [128, NC], F32, kind="ExternalInput")
    ks_d = nc.dram_tensor("k_scale", [128, NC], F32, kind="ExternalInput")
    wout_d = nc.dram_tensor("woutl", [128, 8192], F32, kind="ExternalInput")
    rw_d = nc.dram_tensor("routerwl", [128, NC * E_R], F32, kind="ExternalInput")
    rb_d = nc.dram_tensor("router_b", [1, E_R], F32, kind="ExternalInput")
    ffnw_d = nc.dram_tensor("ffnw", [10, 128, SLOT], F32, kind="ExternalInput")
    y_d = nc.dram_tensor("y", [NSEQ, S, D], F32, kind="ExternalOutput")
    dbg = {}
    if debug:
        dbg["smx"] = nc.dram_tensor("dbg_smx", [128, 16], F32, kind="ExternalOutput")
        dbg["asb"] = nc.dram_tensor("dbg_asb", [128, 2048], BF16, kind="ExternalOutput")
        dbg["ctxT"] = nc.dram_tensor("dbg_ctxT", [128, 4096], BF16, kind="ExternalOutput")
        dbg["qnt"] = nc.dram_tensor("dbg_qnt", [128, 8192], BF16, kind="ExternalOutput")
        dbg["knt"] = nc.dram_tensor("dbg_knt", [128, 16384], BF16, kind="ExternalOutput")
        dbg["vsb"] = nc.dram_tensor("dbg_vsb", [128, 16384], BF16, kind="ExternalOutput")

    semnames = ["sq_x", "w0", "w1", "w2", "w3", "wa", "wb", "wm",
                "gp", "pe", "ve", "se"]
    owner = {"gp": "gpsimd", "pe": "tensor", "ve": "vector", "se": "scalar"}

    with contextlib.ExitStack() as st:
        resid = st.enter_context(nc.sbuf_tensor("resid", [128, NT * D], F32))
        r1 = st.enter_context(nc.sbuf_tensor("r1", [128, 8192], BF16))
        big = st.enter_context(nc.sbuf_tensor("big", [128, 4 * SLOT], BF16))
        sc_bf = st.enter_context(nc.sbuf_tensor("sc_bf", [128, SC_BF_COLS], BF16))
        sc_f = st.enter_context(nc.sbuf_tensor("sc_f", [128, SC_F_COLS], F32))
        id_bf = st.enter_context(nc.sbuf_tensor("id_bf", [128, 128], BF16))
        mask_bf = st.enter_context(nc.sbuf_tensor("mask_bf", [128, 128], BF16))
        id_f = st.enter_context(nc.sbuf_tensor("id_f", [128, 128], F32))
        iop = st.enter_context(nc.sbuf_tensor("iop", [128, 1], F32))
        ioc = st.enter_context(nc.sbuf_tensor("ioc", [128, 128], F32))
        ones1 = st.enter_context(nc.sbuf_tensor("ones1", [1, 128], F32))
        rb1 = st.enter_context(nc.sbuf_tensor("rb1", [1, E_R], F32))
        rbb = st.enter_context(nc.sbuf_tensor("rbb", [128, E_R], F32))
        qs8 = st.enter_context(nc.sbuf_tensor("qs8", [128, NC], F32))
        ks8 = st.enter_context(nc.sbuf_tensor("ks8", [128, NC], F32))
        rwsb = st.enter_context(nc.sbuf_tensor("rwsb", [128, NC * E_R], BF16))
        cst = st.enter_context(nc.sbuf_tensor("cst", [128, 6], F32))

        ps_all = st.enter_context(nc.psum_tensor("ps_all", [128, 3072], F32))
        ps_s = ps_all[:, 0:2048]
        ps_ab = ps_all[:, 2048:3072]
        ps_tr = st.enter_context(nc.psum_tensor("ps_tr", [128, 1024], BF16))
        ps_ctx = st.enter_context(nc.psum_tensor("ps_ctx", [128, 512], F32))

        semh = {n: st.enter_context(nc.semaphore(n)) for n in semnames}
        K = KB(semh, owner)

        for t in range(NT):
            K.region("res%d" % t, "resid", t * D, (t + 1) * D)
        for i in range(8):
            K.region("r1t%d" % i, "r1", i * 1024, (i + 1) * 1024)
        for s in range(4):
            K.region("slot%d" % s, "big", s * SLOT, (s + 1) * SLOT)
        K.region("knt", "big", KNT_O, KNT_O + 16384)
        for t in range(NT):
            K.region("vsb%d" % t, "big", V_O + t * 1024, V_O + (t + 1) * 1024)
        for q in range(4):
            K.region("asb%d" % q, "big", ASB_O + q * 2048, ASB_O + (q + 1) * 2048)
        K.region("atT0", "big", ATT_O, ATT_O + 512)
        K.region("atT1", "big", ATT_O + 512, ATT_O + 1024)
        K.region("atT2", "sc_bf", 1024, 1536)
        K.region("atT3", "sc_bf", 1536, 2048)
        K.region("ctxT", "big", CTXT_O, CTXT_O + 4096)
        K.region("wwa", "big", WWA_O, WWA_O + 8192)
        K.region("wwb", "big", WWB_O, WWB_O + 8192)
        K.region("xn2", "sc_bf", XN2_O, XN2_O + 1024)
        K.region("h1g0", "sc_bf", H1G_O[0], H1G_O[0] + 512)
        K.region("h1g1", "sc_bf", H1G_O[1], H1G_O[1] + 512)
        K.region("h1gT0", "sc_bf", H1GT_O[0], H1GT_O[0] + 512)
        K.region("h1gT1", "sc_bf", H1GT_O[1], H1GT_O[1] + 512)
        K.region("woutsb", "sc_bf", WOUT_O, WOUT_O + 8192)
        K.region("xqn", "sc_f", XQN_O, XQN_O + 1024)
        K.region("gates", "sc_f", GATES_O, GATES_O + 64)
        K.region("lnstat", "sc_f", STAT_O, STAT_O + 64)
        for t in range(8):
            K.region("lg%d" % t, "sc_f", LG_O + t * 8, LG_O + (t + 1) * 8)
            K.region("nm%d" % t, "sc_f", NM_O + t, NM_O + t + 1)
            K.region("nt%d" % t, "sc_f", NT2_O + t, NT2_O + t + 1)
            K.region("xt%d" % t, "sc_f", XT_O + t * 8, XT_O + (t + 1) * 8)
        K.region("gt", "sc_f", GT_O, GT_O + 32)
        for q in range(4):
            K.region("negm%d" % q, "sc_f", SMX_O + q, SMX_O + q + 1)
            K.region("lsum%d" % q, "sc_f", SMX_O + 4 + q, SMX_O + 5 + q)
            K.region("rl%d" % q, "sc_f", SMX_O + 8 + q, SMX_O + 9 + q)
        K.region("qstat", "sc_f", QSTAT_O, QSTAT_O + 8)
        K.region("qstatB", "sc_f", QSTAT_O + 8, QSTAT_O + 16)
        K.region("qstatC", "sc_f", 1520, 1528)
        K.region("qstatD", "sc_f", 1528, 1536)
        for q in range(4):
            K.region("mxs%d" % q, "sc_f", MXS_O + q * 4, MXS_O + (q + 1) * 4)
            K.region("lss%d" % q, "sc_f", LSS_O + q * 4, LSS_O + (q + 1) * 4)
        K.region("cst", "cst", 0, 6)
        for b in range(4):
            K.region("pss%d" % b, "psall", b * 512, (b + 1) * 512)
        K.region("pab0", "psall", 2048, 2560)
        K.region("pab1", "psall", 2560, 3072)
        K.region("ptr", "ps_tr", 0, 1024)
        K.region("pctx", "ps_ctx", 0, 512)

        pair_stream = []
        for seq in range(NSEQ):
            for call in (1, 2):
                if call == 2 and phases < 3:
                    continue
                for grp in range(2):
                    for p in range(len(PAIRS)):
                        pair_stream.append((seq, call, grp, p))
        attn_barrier = {}
        if phases >= 2:
            for seq in range(NSEQ):
                for j, (s2, c2, _, _) in enumerate(pair_stream):
                    if s2 > seq or (s2 == seq and c2 == 2):
                        attn_barrier[j] = seq
                        if j + 1 < len(pair_stream):
                            attn_barrier[j + 1] = seq
                        break

        def pair_slots(i):
            return ((2 * i) % 4, (2 * i + 1) % 4)

        def slot_dma(g, si, kind, ei):
            gid = ei if kind == "r" else E_R + ei
            sem = "w%d" % si
            off = si * SLOT
            K.rw(g, ["slot%d" % si])
            ev = K.tick(sem, g.dma_start(
                big[:, off:off + SLOT], ffnw_d[gid]), 16)
            K.cw(["slot%d" % si], ev)

        def prologue():
            def g_pro(g):
                g.iota(ioc[:], [[1, 128]], channel_multiplier=0,
                       allow_small_or_imprecise_dtypes=True)
                g.iota(iop[:], [[1, 1]], channel_multiplier=1,
                       allow_small_or_imprecise_dtypes=True)
                g.memset(ones1[:], 1.0)
                g.memset(cst[:, 0:1], EPS_LN)
                g.memset(cst[:, 1:2], EPS_QK)
                g.memset(cst[:, 2:3], 1.0 / D)
                g.memset(cst[:, 3:4], -1.0)
                g.memset(cst[:, 5:6], -1.0 / D)
                ev = K.tick("gp", g.memset(cst[:, 4:5], -0.5))
                K.cw(["gt", "cst"], ev)
                K.tick("wm", g.dma_start(rb1[:], rb_d[:]), 16)
                K.tick("wm", g.dma_start(qs8[:], qs_d[:]), 16)
                K.tick("wm", g.dma_start(ks8[:], ks_d[:]), 16)
                ev = K.tick("wm", g.dma_start(rwsb[:], rw_d[:]), 16)
                K.cw(["qstat"], ev)
            K.body("gpsimd", g_pro)

            def g_pro2(g):
                g.wait_ge(semh["sq_x"], 16)
                for i in (0, 1):
                    s0, s1 = pair_slots(i)
                    (k0, e0), (k1, e1) = PAIRS[pair_stream[i][3]]
                    slot_dma(g, s0, k0, e0)
                    slot_dma(g, s1, k1, e1)
            K.body("gpsimd", g_pro2)

            def v_pro(v):
                K.wait(v, K.rw_["gt"])
                K.wait(v, K.rw_["qstat"])
                ins = v.tensor_scalar(id_f[:], ioc[:], iop[:], None, op0=ALU.is_equal)
                K.cut(v, ins, "ve")
                v.tensor_copy(id_bf[:], id_f[:])
                ins = v.tensor_scalar(sc_f[:, MASK_O:MASK_O + 128], ioc[:], iop[:],
                                      NEG_BIG, op0=ALU.is_gt, op1=ALU.mult)
                K.cut(v, ins, "ve")
                v.tensor_copy(mask_bf[:], sc_f[:, MASK_O:MASK_O + 128])
                ev = K.tick("ve", v.tensor_scalar_mul(qs8[:], qs8[:], 0.125))
                K.cw(["gt"], ev)
            K.body("vector", v_pro)

            def t_pro(t):
                K.wait(t, K.rw_["gt"])
                K.rw(t, ["pab0"])
                ev = K.tick("pe", t.matmul(ps_ab[:, 0:E_R], ones1[:], rb1[:],
                                           start=True, stop=True))
                K.cw(["pab0"], ev)
            K.body("tensor", t_pro)

            def v_pro2(v):
                K.rr(v, ["pab0"])
                ev = K.tick("ve", v.tensor_copy(rbb[:], ps_ab[:, 0:E_R]))
                K.cr(["pab0"], ev)
                K.cw(["qstat"], ev)
            K.body("vector", v_pro2)

        def ln_group(tiles, eps, alt_scratch=False):
            res_regs = ["res%d" % t for t in tiles]

            def v_sum(v):
                K.rr(v, res_regs)
                K.rw(v, ["lnstat"])
                ins = None
                for i, t in enumerate(tiles):
                    ins = v.reduce_sum(sc_f[:, STAT_O + i:STAT_O + i + 1],
                                       resid[:, t * D:(t + 1) * D], axis=AX.X)
                ev = K.tick("ve", ins)
                K.cw(["lnstat"], ev)
                K.cr(res_regs, ev)
            K.body("vector", v_sum)

            def s_sq(s):
                scr = ["atT2", "atT3"] if alt_scratch else ["xqn"]
                out = (sc_bf[:, 1024:1024 + D] if alt_scratch
                       else sc_f[:, XQN_O:XQN_O + D])
                K.rr(s, res_regs + ["lnstat"])
                K.rw(s, scr + ["lnstat"])
                ins = None
                for i, t in enumerate(tiles):
                    ins = s.activation(out, resid[:, t * D:(t + 1) * D],
                                       AF.Square,
                                       accum_out=sc_f[:, STAT_O + 8 + i:STAT_O + 9 + i])
                ev = K.tick("se", ins)
                K.cw(scr + ["lnstat"], ev)
                K.cr(res_regs, ev)
            K.body("scalar", s_sq)

            def v_fin(v):
                mean8 = sc_f[:, STAT_O:STAT_O + 8]
                var8 = sc_f[:, STAT_O + 8:STAT_O + 16]
                m2 = sc_f[:, STAT_O + 32:STAT_O + 40]
                K.rr(v, ["lnstat"])
                K.rw(v, ["lnstat"])
                ins = v.tensor_scalar_mul(mean8[:], mean8[:], 1.0 / D)
                K.cut(v, ins, "ve")
                v.tensor_tensor(m2[:], mean8[:], mean8[:], op=ALU.mult)
                ins = v.tensor_scalar_mul(var8[:], var8[:], 1.0 / D)
                K.cut(v, ins, "ve")
                ev = K.tick("ve", v.tensor_tensor(var8[:], var8[:], m2[:],
                                                  op=ALU.subtract))
                K.cw(["lnstat"], ev)
            K.body("vector", v_fin)

            def s_rt(s):
                sig8 = sc_f[:, STAT_O + 16:STAT_O + 24]
                var8 = sc_f[:, STAT_O + 8:STAT_O + 16]
                rstd8 = sc_f[:, STAT_O + 24:STAT_O + 32]
                K.rr(s, ["lnstat", "cst"])
                K.rw(s, ["lnstat"])
                epsap = cst[:, 0:1] if eps == EPS_LN else cst[:, 1:2]
                ins = s.activation(sig8[:], var8[:], AF.Ln, bias=epsap, scale=1.0)
                K.cut(s, ins, "se")
                ev = K.tick("se", s.activation(rstd8[:], sig8[:], AF.Exp,
                                               bias=0.0, scale=cst[:, 4:5]))
                K.cw(["lnstat"], ev)
            K.body("scalar", s_rt)


        def xnT_tile(seq, call, grp, i, t, dest, dest_off):
            def v_n(v):
                rstd8 = sc_f[:, STAT_O + 24:STAT_O + 32]
                sig8 = sc_f[:, STAT_O + 16:STAT_O + 24]
                mean8 = sc_f[:, STAT_O:STAT_O + 8]
                K.rr(v, ["lnstat"])
                K.rw(v, ["xn2"])
                K.rr(v, ["res%d" % t])
                ins = v.tensor_scalar(
                    sc_bf[:, XN2_O:XN2_O + D], resid[:, t * D:(t + 1) * D],
                    mean8[:, i:i + 1], rstd8[:, i:i + 1],
                    op0=ALU.subtract, op1=ALU.mult)
                ev = K.tick("ve", ins)
                K.cw(["xn2"], ev)
                K.cr(["res%d" % t], ev)
                K.cr(["lnstat"], ev)
            K.body("vector", v_n)

            def t_tr(t_eng):
                K.rr(t_eng, ["xn2", "gt"])
                K.rw(t_eng, ["ptr"])
                ins = None
                for cc in range(8):
                    ins = t_eng.transpose(
                        ps_tr[:, cc * 128:(cc + 1) * 128],
                        sc_bf[:, XN2_O + cc * 128:XN2_O + (cc + 1) * 128],
                        id_bf[:])
                ev = K.tick("pe", ins)
                K.cw(["ptr"], ev)
                K.cr(["xn2"], ev)
            K.body("tensor", t_tr)

            def v_ev2(v):
                K.rr(v, ["ptr"])
                K.rw(v, [dest])
                ev2 = K.tick("ve", v.tensor_copy(
                    (r1 if dest.startswith("r1") else big)[
                        :, dest_off:dest_off + 1024],
                    ps_tr[:]))
                K.cw([dest], ev2)
                K.cr(["ptr"], ev2)
            K.body("vector", v_ev2)

        def ffn2(seq, call):
            for grp in range(2):
                t0 = 8 * grp
                tiles = [t0 + i for i in range(8)]

                if grp == 0:
                    ln_group(tiles, EPS_LN)

                for i, t in enumerate(tiles):
                    xnT_tile(seq, call, grp, i, t, "r1t%d" % i, i * 1024)

                    def t_rt(t_eng, i=i):
                        K.rr(t_eng, ["r1t%d" % i])
                        K.wait(t_eng, K.rw_["qstat"])
                        K.rw(t_eng, ["pab0"])
                        ins = None
                        for c in range(NC):
                            ins = t_eng.matmul(
                                ps_ab[:, 0:E_R],
                                r1[:, i * 1024 + c * 128:i * 1024 + (c + 1) * 128],
                                rwsb[:, c * E_R:(c + 1) * E_R],
                                start=(c == 0), stop=(c == NC - 1))
                        ev = K.tick("pe", ins)
                        K.cw(["pab0"], ev)
                        K.cr(["r1t%d" % i], ev)
                    K.body("tensor", t_rt)

                    def v_g1(v, i=i):
                        lg = sc_f[:, LG_O + i * 8:LG_O + (i + 1) * 8]
                        nm = sc_f[:, NM_O + i:NM_O + i + 1]
                        nt2 = sc_f[:, NT2_O + i:NT2_O + i + 1]
                        eq = sc_f[:, GT_O:GT_O + 8]
                        l2 = sc_f[:, GT_O + 8:GT_O + 16]
                        K.rr(v, ["pab0"])
                        K.rw(v, ["lg%d" % i, "nm%d" % i, "nt%d" % i, "gt"])
                        ins = v.tensor_tensor(lg[:], ps_ab[:, 0:E_R], rbb[:], op=ALU.add)
                        K.cut(v, ins, "ve")
                        ev0 = K.cut(v, v.reduce_max(nm[:], lg[:], axis=AX.X,
                                                    negate=True), "ve")
                        K.cr(["pab0"], ev0)
                        ins = v.tensor_scalar(eq[:], lg[:], nm[:], 0.0,
                                              op0=ALU.add, op1=ALU.is_ge)
                        K.cut(v, ins, "ve")
                        ins = v.scalar_tensor_tensor(l2[:], eq[:], NEG_BIG, lg[:],
                                                     op0=ALU.mult, op1=ALU.add)
                        K.cut(v, ins, "ve")
                        ev = K.tick("ve", v.reduce_max(nt2[:], l2[:], axis=AX.X,
                                                       negate=True))
                        K.cw(["lg%d" % i, "nm%d" % i, "nt%d" % i, "gt"], ev)
                    K.body("vector", v_g1)

                def s_g(s):
                    for i in range(8):
                        lg = sc_f[:, LG_O + i * 8:LG_O + (i + 1) * 8]
                        nm = sc_f[:, NM_O + i:NM_O + i + 1]
                        xt = sc_f[:, XT_O + i * 8:XT_O + (i + 1) * 8]
                        K.rr(s, ["lg%d" % i, "nm%d" % i])
                        K.rw(s, ["xt%d" % i])
                        ev = K.tick("se", s.activation(xt[:], lg[:], AF.Exp,
                                                       bias=nm[:], scale=1.0))
                        K.cw(["xt%d" % i], ev)
                        K.cr(["lg%d" % i, "nm%d" % i], ev)
                K.body("scalar", s_g)

                def v_g2(v):
                    for i in range(8):
                        lg = sc_f[:, LG_O + i * 8:LG_O + (i + 1) * 8]
                        nt2 = sc_f[:, NT2_O + i:NT2_O + i + 1]
                        xt = sc_f[:, XT_O + i * 8:XT_O + (i + 1) * 8]
                        ind = sc_f[:, GT_O + 16:GT_O + 24]
                        ge = sc_f[:, GT_O + 24:GT_O + 32]
                        ssum = sc_f[:, GT_O + 14:GT_O + 15]
                        rs = sc_f[:, GT_O + 15:GT_O + 16]
                        gcol = sc_f[:, GATES_O + i * 8:GATES_O + (i + 1) * 8]
                        K.rr(v, ["xt%d" % i, "lg%d" % i, "nt%d" % i])
                        K.rw(v, ["gt", "gates"])
                        ins = v.tensor_scalar(ind[:], lg[:], nt2[:], 0.0,
                                              op0=ALU.add, op1=ALU.is_ge)
                        K.cut(v, ins, "ve")
                        ins = v.scalar_tensor_tensor(ge[:], xt[:], 1.0, ind[:],
                                                     op0=ALU.bypass, op1=ALU.mult,
                                                     accum_out=ssum[:])
                        K.cut(v, ins, "ve")
                        ins = v.reciprocal(rs[:], ssum[:])
                        K.cut(v, ins, "ve")
                        ins = v.tensor_scalar(gcol[:], ge[:], rs[:], 0.5,
                                              op0=ALU.mult, op1=ALU.mult)
                        ev = K.tick("ve", ins)
                        K.cw(["gt", "gates"], ev)
                        K.cr(["xt%d" % i, "lg%d" % i, "nt%d" % i], ev)
                K.body("vector", v_g2)

                base_i = pair_stream.index((seq, call, grp, 0))

                def mk_u1(p, ti, slots):
                    exps = PAIRS[p]
                    def u1():
                        def t_ab(t_eng):
                            for ei in range(2):
                                off = slots[ei] * SLOT
                                a_ps = ps_s[:, 1024:1536] if ei == 0 else ps_ab[:, 0:512]
                                b_ps = ps_s[:, 1536:2048] if ei == 0 else ps_ab[:, 512:1024]
                                a_rg = ["pss2"] if ei == 0 else ["pab0"]
                                b_rg = ["pss3"] if ei == 0 else ["pab1"]
                                K.wait(t_eng, K.rw_["slot%d" % slots[ei]])
                                K.rr(t_eng, ["r1t%d" % ti, "slot%d" % slots[ei]])
                                K.rw(t_eng, a_rg)
                                ins = None
                                for c in range(NC):
                                    lhs = r1[:, ti * 1024 + c * 128:ti * 1024 + (c + 1) * 128]
                                    ins = t_eng.matmul(
                                        a_ps, lhs,
                                        big[:, off + c * 512:off + (c + 1) * 512],
                                        start=(c == 0), stop=(c == NC - 1))
                                ev = K.tick("pe", ins)
                                K.cw(a_rg, ev)
                                K.rw(t_eng, b_rg)
                                ins = None
                                for c in range(NC):
                                    lhs = r1[:, ti * 1024 + c * 128:ti * 1024 + (c + 1) * 128]
                                    ins = t_eng.matmul(
                                        b_ps, lhs,
                                        big[:, off + 4096 + c * 512:off + 4096 + (c + 1) * 512],
                                        start=(c == 0), stop=(c == NC - 1))
                                ev = K.tick("pe", ins)
                                K.cw(b_rg, ev)
                                K.cr(["r1t%d" % ti, "slot%d" % slots[ei]], ev)
                        K.body("tensor", t_ab)

                        def s_silu2(s):
                            for ei in range(2):
                                a_ps = ps_s[:, 1024:1536] if ei == 0 else ps_ab[:, 0:512]
                                a_rg = ["pss2"] if ei == 0 else ["pab0"]
                                sa = sc_f[:, XQN_O + ei * 512:XQN_O + (ei + 1) * 512]
                                K.rr(s, a_rg + ["xqn"])
                                K.rw(s, ["xqn"])
                                ev = K.tick("se", s.activation(sa[:], a_ps, AF.Silu,
                                                               bias=0.0, scale=1.0))
                                K.cw(["xqn"], ev)
                                K.cr(a_rg, ev)
                        K.body("scalar", s_silu2)

                        def v_stt2(v):
                            for ei in range(2):
                                kind, eidx = exps[ei]
                                b_ps = ps_s[:, 1536:2048] if ei == 0 else ps_ab[:, 512:1024]
                                b_rg = ["pss3"] if ei == 0 else ["pab1"]
                                sa = sc_f[:, XQN_O + ei * 512:XQN_O + (ei + 1) * 512]
                                ho = H1G_O[ei]
                                K.rr(v, ["xqn"] + b_rg)
                                if kind == "r":
                                    K.rr(v, ["gates"])
                                K.rw(v, ["h1g%d" % ei])
                                gsc = (sc_f[:, GATES_O + ti * 8 + eidx:
                                            GATES_O + ti * 8 + eidx + 1]
                                       if kind == "r" else 0.5)
                                ins = v.scalar_tensor_tensor(
                                    sc_bf[:, ho:ho + 512], sa[:], gsc, b_ps,
                                    op0=ALU.mult, op1=ALU.mult)
                                ev = K.tick("ve", ins)
                                K.cw(["h1g%d" % ei], ev)
                                K.cr(["xqn"] + b_rg, ev)
                                if kind == "r":
                                    K.cr(["gates"], ev)
                        K.body("vector", v_stt2)
                    return u1

                def mk_u2(p, ti, slots):
                    def u2():
                        def t_h1(t_eng):
                            K.rr(t_eng, ["h1g0", "h1g1"])
                            K.rw(t_eng, ["ptr"])
                            ins = None
                            for ei in range(2):
                                ho = H1G_O[ei]
                                for c in range(4):
                                    ins = t_eng.transpose(
                                        ps_tr[:, ei * 512 + c * 128:
                                              ei * 512 + (c + 1) * 128],
                                        sc_bf[:, ho + c * 128:ho + (c + 1) * 128],
                                        id_bf[:])
                            ev = K.tick("pe", ins)
                            K.cw(["ptr"], ev)
                            K.cr(["h1g0", "h1g1"], ev)
                        K.body("tensor", t_h1)

                        def v_h1T(v):
                            K.rr(v, ["ptr"])
                            K.rw(v, ["h1gT0", "h1gT1"])
                            v.tensor_copy(
                                sc_bf[:, H1GT_O[0]:H1GT_O[0] + 512], ps_tr[:, 0:512])
                            ev = K.tick("ve", v.tensor_copy(
                                sc_bf[:, H1GT_O[1]:H1GT_O[1] + 512], ps_tr[:, 512:1024]))
                            K.cw(["h1gT0", "h1gT1"], ev)
                            K.cr(["ptr"], ev)
                        K.body("vector", v_h1T)
                    return u2

                def mk_u3(p, ti, slots, t):
                    def u3():
                        def t_w2both(t_eng):
                            for ei in range(2):
                                off = slots[ei] * SLOT
                                hto = H1GT_O[ei]
                                K.rr(t_eng, ["h1gT%d" % ei, "slot%d" % slots[ei]])
                                if ei == 0:
                                    K.rw(t_eng, ["pss0", "pss1"])
                                ins = None
                                for hc in range(4):
                                    lhs = sc_bf[:, hto + hc * 128:hto + (hc + 1) * 128]
                                    for dh in range(2):
                                        ins = t_eng.matmul(
                                            ps_s[:, dh * 512:(dh + 1) * 512], lhs,
                                            big[:, off + 8192 + hc * 1024 + dh * 512:
                                                off + 8192 + hc * 1024 + (dh + 1) * 512],
                                            start=(hc == 0 and ei == 0),
                                            stop=(hc == 3 and ei == 1))
                                ev = K.tick("pe", ins)
                                if ei == 1:
                                    K.cw(["pss0", "pss1"], ev)
                                K.cr(["h1gT%d" % ei, "slot%d" % slots[ei]], ev)
                        K.body("tensor", t_w2both)

                        def v_yadd(v):
                            K.rr(v, ["pss0", "pss1"])
                            K.rw(v, ["res%d" % t])
                            ins = v.tensor_tensor(
                                resid[:, t * D:(t + 1) * D],
                                resid[:, t * D:(t + 1) * D],
                                ps_s[:, 0:1024], op=ALU.add)
                            ev = K.tick("ve", ins)
                            K.cw(["res%d" % t], ev)
                            K.cr(["pss0", "pss1"], ev)
                        K.body("vector", v_yadd)
                    return u3

                def mk_gw(i_str):
                    def gw():
                        def g_w(g):
                            j = i_str + 2
                            if j < len(pair_stream) and j not in attn_barrier:
                                sj0, sj1 = pair_slots(j)
                                (k0, e0), (k1, e1) = PAIRS[pair_stream[j][3]]
                                slot_dma(g, sj0, k0, e0)
                                slot_dma(g, sj1, k1, e1)
                        K.body("gpsimd", g_w)
                    return gw

                U1, U2, U3, GW = [], [], [], {}
                for p in range(len(PAIRS)):
                    i_str = base_i + p
                    slots = pair_slots(i_str)
                    for ti in range(8):
                        U1.append(mk_u1(p, ti, slots))
                        U2.append(mk_u2(p, ti, slots))
                        U3.append(mk_u3(p, ti, slots, t0 + ti))
                    GW[8 * p + 7] = mk_gw(i_str)

                n = len(U1)
                U1[0]()
                U2[0]()
                for k in range(n):
                    if k + 1 < n:
                        U1[k + 1]()
                    U3[k]()
                    if k + 1 < n:
                        U2[k + 1]()
                    if k in GW:
                        GW[k]()
                    if grp == 0 and k == 33:
                        ln_group([8 + j for j in range(8)], EPS_LN,
                                 alt_scratch=True)
                    elif grp == 1 and call == 1 and k == 33:
                        ln_group([j for j in range(8)], EPS_LN,
                                 alt_scratch=True)

        def qkv_half(seq, half):
            t0 = 8 * half
            tiles = [t0 + i for i in range(8)]

            def g_w(g, seq=seq, half=half):
                K.rw(g, ["wwa"])
                ev = K.tick("wa", g.dma_start(
                    big[:, WWA_O:WWA_O + 8192], wqkv_d[0]), 16)
                K.cw(["wwa"], ev)
                K.rw(g, ["wwb"])
                ev = K.tick("wb", g.dma_start(
                    big[:, WWB_O:WWB_O + 8192], wqkv_d[1]), 16)
                K.cw(["wwb"], ev)
                if half == 0:
                    K.rw(g, ["woutsb"])
                    ev = K.tick("wm", g.dma_start(
                        sc_bf[:, WOUT_O:WOUT_O + 8192], wout_d[:]), 16)
                    K.cw(["woutsb"], ev)
            K.body("gpsimd", g_w)

            if half == 1:
                ln_group(tiles, EPS_LN)

            for i, t in enumerate(tiles):
                xnT_tile(seq, 0, half, i, t, "vsb%d" % t, V_O + t * 1024)

            QSL = [(QSTAT_O, "qstat"), (QSTAT_O + 8, "qstatB"),
                   (1520, "qstatC"), (1528, "qstatD")]

            def qk_units(wname, j, i, t):
                g3 = j % 3
                if g3 == 0:
                    dst = ps_s[:, 0:1024]
                    dst_rg = ["pss0", "pss1"]
                elif g3 == 1:
                    dst = ps_s[:, 1024:2048]
                    dst_rg = ["pss2", "pss3"]
                else:
                    dst = ps_ab
                    dst_rg = ["pab0", "pab1"]
                if wname == "v":
                    par = i % 2
                    dst = ps_ab if par == 0 else ps_s[:, 0:1024]
                    dst_rg = ["pab0", "pab1"] if par == 0 else ["pss0", "pss1"]
                qo, qrg = QSL[{"q": 0, "k": 2, "v": 0}[wname] + i % 2]
                wrg, wcol = (("wwb", WWB_O) if wname == "k"
                             else ("wwa", WWA_O))

                def u1():
                    def t_mm(t_eng):
                        K.rr(t_eng, ["vsb%d" % t, wrg])
                        K.rw(t_eng, dst_rg)
                        xt = V_O + t * 1024
                        ins = None
                        for c in range(NC):
                            lhs = big[:, xt + c * 128:xt + (c + 1) * 128]
                            for dh in range(2):
                                ins = t_eng.matmul(
                                    dst[:, dh * 512:(dh + 1) * 512], lhs,
                                    big[:, wcol + c * 1024 + dh * 512:
                                        wcol + c * 1024 + (dh + 1) * 512],
                                    start=(c == 0), stop=(c == NC - 1))
                        ev = K.tick("pe", ins)
                        K.cw(dst_rg, ev)
                        K.cr(["vsb%d" % t, wrg], ev)
                    K.body("tensor", t_mm)

                def u2a():
                    if wname in ("q", "k"):
                        def v_st(v):
                            qsum = sc_f[:, qo:qo + 1]
                            K.rr(v, dst_rg)
                            K.rw(v, [qrg])
                            ev = K.tick("ve", v.reduce_sum(qsum[:], dst[:],
                                                           axis=AX.X))
                            K.cw([qrg], ev)
                            K.cr(dst_rg, ev)
                        K.body("vector", v_st)
                    else:
                        def v_vev(v):
                            K.rr(v, dst_rg)
                            K.rw(v, ["vsb%d" % t])
                            ev = K.tick("ve", v.tensor_copy(
                                big[:, V_O + t * 1024:V_O + (t + 1) * 1024],
                                dst[:]))
                            K.cw(["vsb%d" % t], ev)
                            K.cr(dst_rg, ev)
                        K.body("vector", v_vev)

                def u2b():
                    if wname not in ("q", "k"):
                        return
                    if True:
                        def s_sq(s):
                            qsum = sc_f[:, qo:qo + 1]
                            qss = sc_f[:, qo + 1:qo + 2]
                            mn = sc_f[:, qo + 2:qo + 3]
                            nmn = sc_f[:, qo + 3:qo + 4]
                            vr = sc_f[:, qo + 4:qo + 5]
                            sg = sc_f[:, qo + 5:qo + 6]
                            K.rr(s, dst_rg + [qrg, "cst"])
                            K.rw(s, ["atT2", "atT3", qrg])
                            ins = s.activation(sc_bf[:, 1024:1024 + D], dst[:],
                                               AF.Square, accum_out=qss[:])
                            s.mul(mn[:], qsum[:], cst[:, 2:3])
                            ins = s.mul(nmn[:], qsum[:], cst[:, 5:6])
                            K.cut(s, ins, "se")
                            ins = s.mul(vr[:], mn[:], nmn[:])
                            K.cut(s, ins, "se")
                            ins = s.add(vr[:], vr[:], cst[:, 1:2])
                            K.cut(s, ins, "se")
                            ins = s.activation(sg[:], qss[:], AF.Ln,
                                               bias=vr[:], scale=cst[:, 2:3])
                            K.cut(s, ins, "se")
                            rsd = sc_f[:, qo + 6:qo + 7]
                            ev = K.tick("se", s.activation(rsd[:], sg[:], AF.Exp,
                                                           bias=0.0,
                                                           scale=cst[:, 4:5]))
                            K.cw(["atT2", "atT3", qrg], ev)
                            K.cr(dst_rg, ev)
                        K.body("scalar", s_sq)

                        def v_nm(v):
                            rs = sc_f[:, qo + 6:qo + 7]
                            mn = sc_f[:, qo + 2:qo + 3]
                            K.rr(v, [qrg] + dst_rg + ["xn2"])
                            K.rw(v, ["xn2"])
                            ins = v.tensor_scalar(sc_bf[:, XN2_O:XN2_O + D],
                                                  dst[:], mn[:], rs[:],
                                                  op0=ALU.subtract,
                                                  op1=ALU.mult)
                            ev = K.tick("ve", ins)
                            K.cw(["xn2"], ev)
                            K.cr(dst_rg + [qrg], ev)
                        K.body("vector", v_nm)

                def u3():
                    if wname == "v":
                        return
                    scv = qs8 if wname == "q" else ks8

                    def t_tr(t_eng):
                        K.rr(t_eng, ["xn2", "gt"])
                        K.rw(t_eng, ["ptr"])
                        ins = None
                        for cc in range(8):
                            ins = t_eng.transpose(
                                ps_tr[:, cc * 128:(cc + 1) * 128],
                                sc_bf[:, XN2_O + cc * 128:XN2_O + (cc + 1) * 128],
                                id_bf[:])
                        ev = K.tick("pe", ins)
                        K.cw(["ptr"], ev)
                        K.cr(["xn2"], ev)
                    K.body("tensor", t_tr)

                    def v_qe(v):
                        K.rr(v, ["ptr"])
                        drg = (["r1t%d" % j2 for j2 in range(8)]
                               if wname == "q" else ["knt"])
                        K.rw(v, drg)
                        ins = None
                        for cc in range(8):
                            if wname == "q":
                                d_ap = r1[:, cc * 1024 + i * 128:
                                          cc * 1024 + (i + 1) * 128]
                            else:
                                d_ap = big[:, KNT_O + cc * 2048 + t * 128:
                                           KNT_O + cc * 2048 + (t + 1) * 128]
                            ins = v.tensor_scalar_mul(
                                d_ap, ps_tr[:, cc * 128:(cc + 1) * 128],
                                scv[:, cc:cc + 1])
                        ev = K.tick("ve", ins)
                        K.cw(drg, ev)
                        K.cr(["ptr"], ev)
                    K.body("vector", v_qe)
                return u1, u2a, u2b, u3

            us = [qk_units("q" if j % 2 == 0 else "k", j, j // 2,
                           tiles[j // 2]) for j in range(16)]
            us[0][0]()
            us[0][1]()
            us[0][2]()
            for i in range(len(us)):
                if i + 1 < len(us):
                    us[i + 1][0]()
                    us[i + 1][1]()
                us[i][3]()
                if i + 1 < len(us):
                    us[i + 1][2]()

            def g_wv(g):
                K.rw(g, ["wwa"])
                ev = K.tick("wa", g.dma_start(
                    big[:, WWA_O:WWA_O + 8192], wqkv_d[2]), 16)
                K.cw(["wwa"], ev)
            K.body("gpsimd", g_wv)

            vs = [qk_units("v", 0, i, t) for i, t in enumerate(tiles)]
            vs[0][0]()
            for i in range(len(vs)):
                if i + 1 < len(vs):
                    vs[i + 1][0]()
                vs[i][1]()

        def attn_qgroup(seq, qg, mid_wout=None):
            nkc = 4 * (qg + 1)
            qts = [4 * qg + j for j in range(4)]
            if qg == 0:
                waves = [[0, 1, 2, 3]]
                ro_of = {0: 0, 1: 512, 2: 1024, 3: 1536}
            elif qg == 1:
                waves = [[0, 1], [2, 3]]
                ro_of = {0: 0, 1: 1024, 2: 0, 3: 1024}
            else:
                waves = [[0], [1], [2], [3]]
                ro_of = {0: 0, 1: 0, 2: 0, 3: 0}

            def att_ap(pk, slot, lo, hi):
                if pk == 0:
                    return big[:, ATT_O + slot * 512 + lo:
                               ATT_O + slot * 512 + hi]
                return sc_bf[:, 1024 + slot * 512 + lo:1024 + slot * 512 + hi]

            def make_tct(h, c, po):
                def mk_pair(kc0):
                    kcs = [kc for kc in (kc0, kc0 + 1) if kc < nkc]
                    vqls = {kc: [ql for ql, qt in enumerate(qts) if qt >= kc]
                            for kc in kcs}
                    pk = (kc0 // 2) % 2
                    args = [("atT%d" % (pk * 2 + kc % 2)) for kc in kcs]

                    def t_at(t_eng, kcs=kcs, vqls=vqls):
                        regs = sorted({"asb%d" % ql for kc in kcs
                                       for ql in vqls[kc]})
                        K.rr(t_eng, regs + ["gt"])
                        K.rw(t_eng, ["ptr"])
                        ins = None
                        for j2, kc in enumerate(kcs):
                            for j, ql in enumerate(vqls[kc]):
                                ins = t_eng.transpose(
                                    ps_tr[:, j2 * 512 + j * 128:
                                          j2 * 512 + (j + 1) * 128],
                                    big[:, ASB_O + ql * 2048 + kc * 128:
                                        ASB_O + ql * 2048 + (kc + 1) * 128],
                                    id_bf[:])
                        ev = K.tick("pe", ins)
                        K.cw(["ptr"], ev)
                        K.cr(regs, ev)

                    def v_ev(v, kcs=kcs, vqls=vqls, pk=pk, args=args):
                        K.rr(v, ["ptr"])
                        K.rw(v, args)
                        ins = None
                        for j2, kc in enumerate(kcs):
                            q0 = vqls[kc][0] * 128
                            n = 512 - q0
                            ins = v.tensor_copy(
                                att_ap(pk, kc % 2, q0, 512),
                                ps_tr[:, j2 * 512:j2 * 512 + n])
                        ev = K.tick("ve", ins)
                        K.cw(args, ev)
                        K.cr(["ptr"], ev)

                    def t_cx(t_eng, kcs=kcs, vqls=vqls, pk=pk, c=c, po=po,
                             h=h):
                        for kc in kcs:
                            q0 = vqls[kc][0] * 128
                            n = 512 - q0
                            rg = "atT%d" % (pk * 2 + kc % 2)
                            K.rr(t_eng, [rg, "vsb%d" % kc])
                            if kc == 0:
                                K.rw(t_eng, ["pctx"])
                            ins = t_eng.matmul(
                                ps_ctx[po:po + 64, q0:q0 + n],
                                big[:, V_O + kc * 1024 + h * 64:
                                    V_O + kc * 1024 + h * 64 + 64],
                                att_ap(pk, kc % 2, q0, 512),
                                start=(kc == 0), stop=(kc == nkc - 1),
                                tile_position=(0, po))
                            ev = K.tick("pe", ins)
                            if kc == nkc - 1:
                                K.cw(["pctx"], ev)
                            K.cr([rg, "vsb%d" % kc], ev)
                    return t_at, v_ev, t_cx

                def emit_tct():
                    pairs = [mk_pair(kc0) for kc0 in range(0, nkc, 2)]
                    K.body("tensor", pairs[0][0])
                    K.body("vector", pairs[0][1])
                    for k in range(1, len(pairs)):
                        K.body("tensor", pairs[k][0])
                        K.body("vector", pairs[k][1])
                        K.body("tensor", pairs[k - 1][2])
                    K.body("tensor", pairs[-1][2])

                    def v_cx(v, c=c, po=po):
                        K.rr(v, ["pctx"])
                        K.rw(v, ["ctxT"])
                        ev = K.tick("ve", v.tensor_copy(
                            big[po:po + 64,
                                CTXT_O + c * 512:CTXT_O + (c + 1) * 512],
                            ps_ctx[po:po + 64, :]))
                        K.cw(["ctxT"], ev)
                        K.cr(["pctx"], ev)
                    K.body("vector", v_cx)
                return emit_tct

            BANKS = ["pss0", "pss1", "pss2", "pss3", "pab0", "pab1"]

            nkg0 = (128 * (qts[0] + 1) + 511) // 512
            step = {1: 1, 2: 2, 3: 3, 4: 2}[nkg0]
            mod = {1: 6, 2: 6, 3: 6, 4: 4}[nkg0]
            stream = []
            cur = 0
            prevb = set()
            for h in range(NH):
                for wi, wave in enumerate(waves):
                    alloc = {}
                    plan = {}
                    for ql in wave:
                        keys = 128 * (qts[ql] + 1)
                        nkg = (keys + 511) // 512
                        assert nkg == nkg0
                        bks = list(range(cur, cur + nkg))
                        alloc[ql] = bks
                        ejs = [j for j in range(nkg) if bks[j] not in prevb]
                        ljs = [j for j in range(nkg) if bks[j] in prevb]
                        assert ejs == list(range(min(ejs), max(ejs) + 1)) \
                            if ejs else True
                        plan[ql] = (ejs, ljs)
                        cur = (cur + step) % mod
                    prevb = {b for ql in wave for b in alloc[ql]}
                    stream.append((h, wi, wave, alloc, plan))

            def region_of(alloc, ql, js, keys):
                lo = alloc[ql][0] * 512 + js[0] * 512
                hi = alloc[ql][0] * 512 + min(keys, (js[-1] + 1) * 512)
                return lo, hi

            def emit_sc(item, which):
                h, wi, wave, alloc, plan = item
                c = h // 2
                po = 64 * (h % 2)
                part = [(ql, j) for ql in wave for j in plan[ql][which]]
                if not part:
                    return

                def t_sc(t_eng, part=part, alloc=alloc, c=c, po=po):
                    by_ql = {}
                    for ql, j in part:
                        by_ql.setdefault(ql, []).append(j)
                    for ql, js in by_ql.items():
                        qt = qts[ql]
                        keys = 128 * (qt + 1)
                        nkg = len(alloc[ql])
                        base = alloc[ql][0]
                        rgs = [BANKS[alloc[ql][j]] for j in js]
                        K.rr(t_eng, ["r1t%d" % c, "knt"])
                        K.rw(t_eng, rgs)
                        lhs = r1[po:po + 64, c * 1024 + (qt % 8) * 128:
                                 c * 1024 + (qt % 8 + 1) * 128]
                        ins = None
                        for j in js:
                            n = min(512, keys - j * 512)
                            o = base * 512 + j * 512
                            last = (j == nkg - 1)
                            ins = t_eng.matmul(
                                ps_all[:, o:o + n], lhs,
                                big[po:po + 64,
                                    KNT_O + c * 2048 + j * 512:
                                    KNT_O + c * 2048 + j * 512 + n],
                                start=True, stop=not last)
                            if last:
                                ins = t_eng.matmul(
                                    ps_all[:, o + n - 128:o + n],
                                    id_bf[:], mask_bf[:],
                                    start=False, stop=True)
                        ev = K.tick("pe", ins)
                        K.cw(rgs, ev)
                        K.cr(["r1t%d" % c, "knt"], ev)
                K.body("tensor", t_sc)

            def emit_mxa(item):
                h, wi, wave, alloc, plan = item
                if not any(plan[ql][0] for ql in wave):
                    return

                def v_mxa(v, wave=wave, alloc=alloc, plan=plan):
                    for ql in wave:
                        ejs, ljs = plan[ql]
                        if not ejs:
                            continue
                        qt = qts[ql]
                        keys = 128 * (qt + 1)
                        lo, hi = region_of(alloc, ql, ejs, keys)
                        rgs = [BANKS[alloc[ql][j]] for j in ejs]
                        K.rr(v, rgs)
                        if ljs:
                            mx = sc_f[:, MXS_O + ql * 4:MXS_O + ql * 4 + 1]
                            K.rw(v, ["mxs%d" % ql])
                            ins = v.reduce_max(mx[:], ps_all[:, lo:hi],
                                               axis=AX.X, negate=False)
                            ev = K.tick("ve", ins)
                            K.cw(["mxs%d" % ql], ev)
                        else:
                            nm = sc_f[:, SMX_O + ql:SMX_O + ql + 1]
                            K.rw(v, ["negm%d" % ql])
                            ins = v.reduce_max(nm[:], ps_all[:, lo:hi],
                                               axis=AX.X, negate=True)
                            ev = K.tick("ve", ins)
                            K.cw(["negm%d" % ql], ev)
                        K.cr(rgs, ev)
                K.body("vector", v_mxa)

            def emit_mxb(item):
                h, wi, wave, alloc, plan = item
                if not any(plan[ql][1] for ql in wave):
                    return

                def v_mxb(v, wave=wave, alloc=alloc, plan=plan):
                    for ql in wave:
                        ejs, ljs = plan[ql]
                        if not ljs:
                            continue
                        qt = qts[ql]
                        keys = 128 * (qt + 1)
                        lo, hi = region_of(alloc, ql, ljs, keys)
                        rgs = [BANKS[alloc[ql][j]] for j in ljs]
                        nm = sc_f[:, SMX_O + ql:SMX_O + ql + 1]
                        K.rr(v, rgs)
                        K.rw(v, ["negm%d" % ql, "mxs%d" % ql])
                        if not ejs:
                            ins = v.reduce_max(nm[:], ps_all[:, lo:hi],
                                               axis=AX.X, negate=True)
                        else:
                            mx1 = sc_f[:, MXS_O + ql * 4 + 1:
                                       MXS_O + ql * 4 + 2]
                            ins = v.reduce_max(mx1[:], ps_all[:, lo:hi],
                                               axis=AX.X, negate=False)
                            K.cut(v, ins, "ve")
                            ins = v.reduce_max(
                                nm[:], sc_f[:, MXS_O + ql * 4:
                                            MXS_O + ql * 4 + 2],
                                axis=AX.X, negate=True)
                        ev = K.tick("ve", ins)
                        K.cw(["negm%d" % ql, "mxs%d" % ql], ev)
                        K.cr(rgs, ev)
                K.body("vector", v_mxb)

            prev_tct = None
            for i, item in enumerate(stream):
                h, wi, wave, alloc, plan = item
                if i == 0:
                    emit_sc(item, 0)
                    emit_mxa(item)
                if i + 1 < len(stream):
                    emit_sc(stream[i + 1], 0)
                emit_mxb(item)

                if wi == 0 and prev_tct is not None:
                    prev_tct()
                    prev_tct = None

                def s_ex(s, wave=wave, alloc=alloc, plan=plan):
                    for ql in wave:
                        qt = qts[ql]
                        keys = 128 * (qt + 1)
                        ro = alloc[ql][0] * 512
                        bks = alloc[ql]
                        nm = sc_f[:, SMX_O + ql:SMX_O + ql + 1]
                        ls = sc_f[:, SMX_O + 4 + ql:SMX_O + 5 + ql]
                        ejs, ljs = plan[ql]
                        if len(bks) == 4 and ljs and ejs:
                            l2 = sc_f[:, LSS_O + ql * 4:LSS_O + ql * 4 + 1]
                            parts = [(ljs, ls), (ejs, l2)]
                            K.rr(s, [BANKS[b] for b in bks] + ["negm%d" % ql])
                            K.rw(s, ["asb%d" % ql, "lsum%d" % ql,
                                     "lss%d" % ql])
                            for js, acc in parts:
                                lo = js[0] * 512
                                hi = min(keys, (js[-1] + 1) * 512)
                                ins = s.activation(
                                    big[:, ASB_O + ql * 2048 + lo:
                                        ASB_O + ql * 2048 + hi],
                                    ps_all[:, ro + lo:ro + hi], AF.Exp,
                                    bias=nm[:], scale=1.0, accum_out=acc[:])
                                ev = K.tick("se", ins)
                                K.cr([BANKS[bks[j]] for j in js], ev)
                            K.cw(["asb%d" % ql, "lsum%d" % ql, "lss%d" % ql],
                                 ev)
                            K.cr(["negm%d" % ql], ev)
                        else:
                            rgs = [BANKS[b] for b in bks]
                            K.rr(s, rgs + ["negm%d" % ql])
                            K.rw(s, ["asb%d" % ql, "lsum%d" % ql])
                            ins = s.activation(
                                big[:, ASB_O + ql * 2048:
                                    ASB_O + ql * 2048 + keys],
                                ps_all[:, ro:ro + keys], AF.Exp,
                                bias=nm[:], scale=1.0, accum_out=ls[:])
                            ev = K.tick("se", ins)
                            K.cw(["asb%d" % ql, "lsum%d" % ql], ev)
                            K.cr(rgs + ["negm%d" % ql], ev)
                K.body("scalar", s_ex)

                if i + 1 < len(stream):
                    emit_sc(stream[i + 1], 1)
                    emit_mxa(stream[i + 1])

                def v_sc2(v, wave=wave, alloc=alloc, plan=plan):
                    ins = None
                    for ql in wave:
                        ls = sc_f[:, SMX_O + 4 + ql:SMX_O + 5 + ql]
                        rl = sc_f[:, SMX_O + 8 + ql:SMX_O + 9 + ql]
                        ejs, ljs = plan[ql]
                        split = len(alloc[ql]) == 4 and ljs and ejs
                        K.rr(v, ["lsum%d" % ql] +
                             (["lss%d" % ql] if split else []))
                        K.rw(v, ["rl%d" % ql, "lsum%d" % ql])
                        if split:
                            l2 = sc_f[:, LSS_O + ql * 4:LSS_O + ql * 4 + 1]
                            ins = v.tensor_tensor(ls[:], ls[:], l2[:],
                                                  op=ALU.add)
                            K.cut(v, ins, "ve")
                        ins = v.reciprocal(rl[:], ls[:])
                    K.cut(v, ins, "ve")
                    for ql in wave:
                        qt = qts[ql]
                        keys = 128 * (qt + 1)
                        rl = sc_f[:, SMX_O + 8 + ql:SMX_O + 9 + ql]
                        K.rr(v, ["rl%d" % ql, "asb%d" % ql])
                        K.rw(v, ["asb%d" % ql])
                        ins = v.tensor_scalar_mul(
                            big[:, ASB_O + ql * 2048:ASB_O + ql * 2048 + keys],
                            big[:, ASB_O + ql * 2048:ASB_O + ql * 2048 + keys],
                            rl[:])
                        ev = K.tick("ve", ins)
                        K.cw(["asb%d" % ql, "rl%d" % ql], ev)
                        K.cr(["lsum%d" % ql], ev)
                K.body("vector", v_sc2)

                if wi == len(waves) - 1:
                    assert prev_tct is None
                    prev_tct = make_tct(h, h // 2, 64 * (h % 2))
            prev_tct()

            for ql, qt in enumerate(qts):
                def t_wo(t_eng, ql=ql):
                    K.rr(t_eng, ["ctxT", "woutsb"])
                    K.rw(t_eng, ["pab0", "pab1"])
                    ins = None
                    for c2 in range(NC):
                        lhs = big[:, CTXT_O + c2 * 512 + ql * 128:
                                  CTXT_O + c2 * 512 + (ql + 1) * 128]
                        for dh in range(2):
                            ins = t_eng.matmul(
                                ps_ab[:, dh * 512:(dh + 1) * 512], lhs,
                                sc_bf[:, WOUT_O + c2 * 1024 + dh * 512:
                                      WOUT_O + c2 * 1024 + (dh + 1) * 512],
                                start=(c2 == 0), stop=(c2 == NC - 1))
                    ev = K.tick("pe", ins)
                    K.cw(["pab0", "pab1"], ev)
                    K.cr(["ctxT", "woutsb"], ev)
                K.body("tensor", t_wo)

                def v_wa(v, qt=qt):
                    K.rr(v, ["pab0", "pab1"])
                    K.rw(v, ["res%d" % qt])
                    ins = v.tensor_tensor(resid[:, qt * D:(qt + 1) * D],
                                          resid[:, qt * D:(qt + 1) * D],
                                          ps_ab, op=ALU.add)
                    ev = K.tick("ve", ins)
                    K.cw(["res%d" % qt], ev)
                    K.cr(["pab0", "pab1"], ev)
                K.body("vector", v_wa)

                if ql == 1 and mid_wout is not None:
                    mid_wout()

        K.block = None

        def emit(block):
            K.block = block
            prologue()
            for seq in range(NSEQ):
                def sx(g, seq=seq):
                    K.rw(g, ["res%d" % t for t in range(NT)])
                    ev = K.tick("sq_x", g.dma_start(
                        resid[:].rearrange("p (t f) -> p t f", t=NT),
                        x_d[seq].rearrange("(t p) f -> p t f", p=128)), 16)
                    K.cw(["res%d" % t for t in range(NT)], ev)
                K.body("sync", sx)

                ffn2(seq, 1)
                if phases >= 2:
                    for half in range(2):
                        qkv_half(seq, half)
                        for qgl in range(2):
                            qg = 2 * half + qgl
                            attn_qgroup(seq, qg)
                    def g_post(g, seq=seq):
                        for j in sorted(attn_barrier):
                            if attn_barrier[j] == seq:
                                sj0, sj1 = pair_slots(j)
                                (k0, e0), (k1, e1) = PAIRS[pair_stream[j][3]]
                                slot_dma(g, sj0, k0, e0)
                                slot_dma(g, sj1, k1, e1)
                    K.body("gpsimd", g_post)
                if phases >= 3:
                    ffn2(seq, 2)

                def sy(g, seq=seq):
                    for t in range(NT):
                        K.rr(g, ["res%d" % t])
                    ev = K.tick("sq_x", g.dma_start(
                        y_d[seq].rearrange("(t p) f -> p t f", p=128),
                        resid[:].rearrange("p (t f) -> p t f", t=NT)), 16)
                    K.cr(["res%d" % t for t in range(NT)], ev)
                    if seq == NSEQ - 1:
                        K.final_store = ev
                K.body("sync", sy)

            def g_end(g):
                K.wait(g, K.final_store)
            K.body("gpsimd", g_end)

        with nc.Block() as blk:
            emit(blk)

    return nc


def kernel(**inputs):
    xs = np.ascontiguousarray(np.asarray(inputs["L0_attn_x"], dtype=np.float32))
    wqkv = np.ascontiguousarray(np.asarray(inputs["L0_attn_W_qkv"], dtype=np.float32))
    qs = np.ascontiguousarray(np.asarray(inputs["L0_attn_q_scale"], dtype=np.float32).reshape(NC, 128).T)
    ks = np.ascontiguousarray(np.asarray(inputs["L0_attn_k_scale"], dtype=np.float32).reshape(NC, 128).T)
    wout = np.ascontiguousarray(np.asarray(inputs["L0_attn_W_out"], dtype=np.float32))
    rw = np.ascontiguousarray(np.asarray(inputs["L0_ffn_router_w"], dtype=np.float32))
    rb = np.ascontiguousarray(np.asarray(inputs["L0_ffn_router_b"], dtype=np.float32).reshape(1, E_R))
    rw1 = np.ascontiguousarray(np.asarray(inputs["L0_ffn_routed_w1"], dtype=np.float32))
    rw3 = np.ascontiguousarray(np.asarray(inputs["L0_ffn_routed_w3"], dtype=np.float32))
    rw2 = np.ascontiguousarray(np.asarray(inputs["L0_ffn_routed_w2"], dtype=np.float32))
    sw1 = np.ascontiguousarray(np.asarray(inputs["L0_ffn_shared_w1"], dtype=np.float32))
    sw3 = np.ascontiguousarray(np.asarray(inputs["L0_ffn_shared_w3"], dtype=np.float32))
    sw2 = np.ascontiguousarray(np.asarray(inputs["L0_ffn_shared_w2"], dtype=np.float32))

    def pack_dh(w):
        X = w.shape[1]
        return w.reshape(NC, 128, X).transpose(1, 0, 2).reshape(128, NC * X)

    def pack_h(w):
        X = w.shape[1]
        return w.reshape(4, 128, X).transpose(1, 0, 2).reshape(128, 4 * X)

    ffnw = np.stack([np.concatenate(
        [pack_dh(w1[e]), pack_dh(w3[e]), pack_h(w2[e])], axis=1)
        for (w1, w3, w2, es) in ((rw1, rw3, rw2, range(E_R)), (sw1, sw3, sw2, range(2)))
        for e in es])
    ffnw = np.ascontiguousarray(ffnw)
    wqkvl = np.ascontiguousarray(np.stack(
        [pack_dh(wqkv[:, m * D:(m + 1) * D]) for m in range(3)]))
    woutl = np.ascontiguousarray(pack_dh(wout))
    rwl = np.ascontiguousarray(pack_dh(rw))

    nc = build_program(phases=3)
    in_maps = []
    for core in range(8):
        in_maps.append({
            "x": np.ascontiguousarray(xs[2 * core:2 * core + 2]),
            "wqkvl": wqkvl, "q_scale": qs, "k_scale": ks, "woutl": woutl,
            "routerwl": rwl, "router_b": rb, "ffnw": ffnw,
        })
    res = run_bass_kernel_spmd(nc, in_maps, core_ids=list(range(8)))
    out = np.concatenate([r["y"] for r in res.results], axis=0)
    return out.reshape(16, S, D)
